# version 12
# baseline (speedup 1.0000x reference)
import numpy as np
import ml_dtypes
import concourse.bass as bass
import concourse.mybir as mybir
from concourse.bass_utils import run_bass_kernel_spmd
from contextlib import ExitStack

F32 = mybir.dt.float32
BF16 = mybir.dt.bfloat16
I32 = mybir.dt.int32
AF = mybir.ActivationFunctionType
ALU = mybir.AluOpType
AX = mybir.AxisListType.X
BF = ml_dtypes.bfloat16

NCORES = 8
D = 1024
C = 8
NT = 4096
TT = 512
NTILES = NT // TT
HEADS = 16
DK = 64
EPS = 1e-5
EPS_ATTN = 1e-6
NEG = -1.0e30
MAGIC = 123456789

ENGINES = ("tensor", "vector", "scalar", "gpsimd", "sync")


class Ctx:

    def __init__(self, nc, block, sems, bufs, dram):
        self.nc = nc
        self.block = block
        self.sems = sems
        self.counts = {k: 0 for k in sems}
        self.bufs = bufs
        self.dram = dram
        self.cur = None
        self.eng = None
        self.streams = {}
        self.nq = sum(1 for k in sems if k.startswith("q"))

    def _stream(self, name):
        if name not in self.streams:
            cls = getattr(self, "_sqc", {}).get(name, "hw")
            if cls == "sw":
                slots = [f"q{i}" for i in range(16, 24)]
            else:
                slots = [f"q{i}" for i in range(16)]
            n = sum(1 for k, v in self.streams.items()
                    if getattr(self, "_scls", {}).get(k) == cls)
            if not hasattr(self, "_scls"):
                self._scls = {}
            self._scls[name] = cls
            self.streams[name] = slots[n % len(slots)]
        return self.streams[name]

    def run(self, phase_fn, *args, **kwargs):
        base = dict(self.counts)
        final = None
        for en in ENGINES:
            self.counts = dict(base)
            self.cur = en
            def body(e):
                self.eng = e
                phase_fn(self, *args, **kwargs)
            getattr(self.block, en)(body)
            if final is None:
                final = dict(self.counts)
            else:
                assert final == self.counts, f"phase {phase_fn.__name__} nondeterministic across engine passes"
        self.counts = final

    def op(self, engine, fn, inc=None, n=1):
        if self.cur == engine:
            ins = fn(self.eng)
            if inc is not None:
                ins.then_inc(self.sems[inc], n)
        if inc is not None:
            self.counts[inc] += n
            return self.counts[inc]
        return None

    def dma(self, queue, out, in_, stream="misc", **kw):
        qclass = "sw" if queue == "gpsimd" else "hw"
        if not hasattr(self, "_sqc"):
            self._sqc = {}
        prev = self._sqc.setdefault(stream, qclass)
        assert prev == qclass, f"stream {stream} mixes HWDGE and SWDGE queues"
        sem = self._stream(stream)
        if self.cur == queue:
            self.eng.dma_start(out=out, in_=in_, **kw).then_inc(self.sems[sem], 16)
        self.counts[sem] += 16
        return self.counts[sem]

    def wait_dma(self, engine, stream):
        sem = self._stream(stream)
        if self.cur == engine:
            self.eng.wait_ge(self.sems[sem], self.counts[sem])

    def wait(self, engine, sem, val):
        assert val <= self.counts[sem], (
            f"wait_ge({sem},{val}) ahead of count {self.counts[sem]} on {engine}")
        if self.cur == engine:
            self.eng.wait_ge(self.sems[sem], val)

    def now(self, sem):
        return self.counts[sem]

    def raw(self, engine, fn):
        if self.cur == engine:
            fn(self.eng)


def _fm(x):
    return np.ascontiguousarray(np.asarray(x).T)


def _cv(v):
    v = np.asarray(v).reshape(-1)
    return np.ascontiguousarray(v.reshape(-1, 128).T)


def stage_inputs(inputs):
    x = np.asarray(inputs["L0_attn_x_in"])
    per_core = []
    for c in range(NCORES):
        b, h = c // 2, c % 2
        d = {}
        d["x_fm"] = _fm(x[b, h * NT:(h + 1) * NT, :])
        d["side"] = np.array([[h]], dtype=np.int32)
        per_core.append(d)

    shared = {}
    for l in range(4):
        for nm in ("Wq", "Wk", "Wv", "Wo"):
            shared[f"l0_{nm}{l}"] = np.ascontiguousarray(inputs[f"L0_attn_{nm}"][l])
        shared[f"l0_W1{l}"] = np.ascontiguousarray(inputs["L0_attn_W1"][l])
        shared[f"l0_W2{l}"] = np.ascontiguousarray(inputs["L0_attn_W2"][l])
    names = dict(
        l0_bq="L0_attn_bq", l0_bk="L0_attn_bk", l0_bv="L0_attn_bv", l0_bo="L0_attn_bo",
        l0_b1="L0_attn_b1", l0_b2="L0_attn_b2", l0_ln1s="L0_attn_ln1_s", l0_ln1b="L0_attn_ln1_b",
        l0_ln2s="L0_attn_ln2_s", l0_ln2b="L0_attn_ln2_b",
    )
    for k, v in names.items():
        shared[k] = np.ascontiguousarray(inputs[v])
    shared["l0_lnfs"] = inputs["L0_attn_lnf_s"].reshape(D, 1)
    shared["l0_lnfb"] = inputs["L0_attn_lnf_b"].reshape(D, 1)

    shared["f0_lng"] = inputs["L0_ffn_ln_g"].reshape(D, 1)
    shared["f0_lnb"] = inputs["L0_ffn_ln_b"].reshape(D, 1)
    shared["f0_aw1"] = np.ascontiguousarray(inputs["L0_ffn_aw1"])
    shared["f0_ab1"] = inputs["L0_ffn_ab1"].reshape(4 * D, 1)
    shared["f0_aw2"] = np.ascontiguousarray(inputs["L0_ffn_aw2"])
    shared["f0_ab2"] = inputs["L0_ffn_ab2"].reshape(D, 1)
    shared["f0_gw"] = np.ascontiguousarray(inputs["L0_ffn_gate_w"])
    shared["f0_gb"] = inputs["L0_ffn_gate_b"].reshape(6, 1)
    shared["f0_ew1"] = np.ascontiguousarray(inputs["L0_ffn_ew1"])
    shared["f0_eb1"] = np.ascontiguousarray(inputs["L0_ffn_eb1"])
    shared["f0_ew2"] = np.ascontiguousarray(inputs["L0_ffn_ew2"])
    shared["f0_eb2"] = np.ascontiguousarray(inputs["L0_ffn_eb2"])
    shared["f0_gens"] = np.ascontiguousarray(inputs["L0_ffn_gens"])
    shared["f0_gcw"] = np.ascontiguousarray(inputs["L0_ffn_gc_w"])
    shared["f0_gcb"] = inputs["L0_ffn_gc_b"].reshape(1, 1)

    for nm in ("Wq", "Wk", "Wv", "Wo", "W1", "W2"):
        shared[f"l1_{nm}"] = np.ascontiguousarray(inputs[f"L1_attn_{nm}"])
    for nm, src in (("bq", "L1_attn_bq"), ("bk", "L1_attn_bk"), ("bv", "L1_attn_bv"),
                    ("bo", "L1_attn_bo"), ("g1", "L1_attn_g1"), ("b1", "L1_attn_b1"),
                    ("c2", "L1_attn_c2"), ("g2", "L1_attn_g2"), ("b2", "L1_attn_b2")):
        shared[f"l1_{nm}"] = inputs[src].reshape(-1, 1)
    shared["l1_c1"] = inputs["L1_attn_c1"].reshape(4 * D, 1)

    shared["f1_cT"] = _fm(inputs["L1_ffn_centroids"])
    shared["f1_lng"] = np.ascontiguousarray(inputs["L1_ffn_ln_g"])
    shared["f1_lnb"] = np.ascontiguousarray(inputs["L1_ffn_ln_b"])
    shared["f1_w1"] = np.ascontiguousarray(np.transpose(inputs["L1_ffn_w1"], (0, 2, 1)))
    shared["f1_b1"] = np.ascontiguousarray(inputs["L1_ffn_b1"])
    shared["f1_w2"] = np.ascontiguousarray(np.transpose(inputs["L1_ffn_w2"], (0, 2, 1)))
    shared["f1_b2"] = np.ascontiguousarray(inputs["L1_ffn_b2"])

    shared["f0_lngc"] = _cv(inputs["L0_ffn_ln_g"])
    shared["f0_lnbc"] = _cv(inputs["L0_ffn_ln_b"])
    shared["f0_ab1c"] = _cv(inputs["L0_ffn_ab1"])
    shared["f0_ab2c"] = _cv(inputs["L0_ffn_ab2"])
    shared["f0_eb1c"] = np.stack([_cv(inputs["L0_ffn_eb1"][e]) for e in range(6)])
    eb2p = np.zeros((8, 1024), np.float32); eb2p[:6] = inputs["L0_ffn_eb2"]
    shared["f0_eb2p"] = eb2p
    shared["f0_gcwa"] = _cv(inputs["L0_ffn_gc_w"][:1024, 0])
    shared["f0_gcwb"] = _cv(inputs["L0_ffn_gc_w"][1024:, 0])
    for nm in ("bq", "bk", "bv", "bo", "b2"):
        shared[f"l0_{nm}c"] = np.stack([_cv(inputs[f"L0_attn_{nm}"][l]) for l in range(4)])
    shared["l0_b1c"] = np.stack([_cv(inputs["L0_attn_b1"][l]) for l in range(4)])
    for nm in ("ln1_s", "ln1_b", "ln2_s", "ln2_b"):
        shared[f"l0_{nm.replace('_','')}c"] = np.stack([_cv(inputs[f"L0_attn_{nm}"][l]) for l in range(4)])
    shared["l0_lnfsc"] = _cv(inputs["L0_attn_lnf_s"])
    shared["l0_lnfbc"] = _cv(inputs["L0_attn_lnf_b"])
    for nm in ("bq", "bk", "bv", "bo", "g1", "b1", "c2", "g2", "b2"):
        shared[f"l1_{nm}c"] = _cv(inputs[f"L1_attn_{nm}"])
    shared["l1_c1c"] = _cv(inputs["L1_attn_c1"])
    shared["l1_bvr"] = np.ascontiguousarray(inputs["L1_attn_bv"].reshape(1, -1))
    shared["f1_b1c"] = np.stack([_cv(inputs["L1_ffn_b1"][e]) for e in range(8)])
    shared["f1_lngcc"] = np.stack([_cv(inputs["L1_ffn_ln_g"][e]) for e in range(8)])
    shared["f1_lnbcc"] = np.stack([_cv(inputs["L1_ffn_ln_b"][e]) for e in range(8)])
    shared["f1_lngs"] = np.ascontiguousarray(shared["f1_lngcc"].transpose(1, 0, 2).reshape(128, 64))
    shared["f1_lnbs"] = np.ascontiguousarray(shared["f1_lnbcc"].transpose(1, 0, 2).reshape(128, 64))
    shared["f1_b1s"] = np.ascontiguousarray(np.stack([_cv(inputs["L1_ffn_b1"][e]) for e in range(8)]).transpose(1, 0, 2).reshape(128, 256))
    shared["ones128"] = np.ones((1, 128), dtype=BF)
    shared["ones_col"] = np.ones((128, 1), dtype=BF)
    shared["ones128f"] = np.ones((1, 128), dtype=np.float32)
    shared["idt"] = np.eye(128, dtype=BF)
    shared["idtf"] = np.eye(128, dtype=np.float32)
    shared["ind8"] = _make_ind8()
    i16c = np.zeros((128, C * 16), dtype=BF)
    i16T = np.zeros((16, C * 128), dtype=BF)
    for c in range(C):
        for p in range(128):
            h = 2 * c + p // 64
            i16c[p, c * 16 + h] = 1.0
            i16T[h, c * 128 + p] = 1.0
    shared["i16c"] = i16c
    shared["i16Tc"] = i16T
    consts_per_core = [_core_consts(c) for c in range(NCORES)]
    for c in range(NCORES):
        per_core[c].update(shared)
        per_core[c].update(consts_per_core[c])
    return per_core


def _make_ind8():
    out = np.zeros((8, 8 * 128), dtype=BF)
    for e in range(8):
        out[e, e * 128:(e + 1) * 128] = 1.0
    return out


def _posemb_slice(h):
    import math
    half = D // 2
    freq = np.exp(np.arange(half, dtype=np.float32) * (-math.log(10000.0) / (half - 1)))
    pos = (np.arange(h * NT, (h + 1) * NT, dtype=np.float32) + 1.0)[:, None] * freq[None, :]
    pe = np.concatenate([np.sin(pos), np.cos(pos)], axis=-1).astype(np.float32)
    return _fm(pe)


_BAND = None


def _win_masks(side):
    global _BAND
    if _BAND is None:
        i = np.arange(128)[:, None]
        j = np.arange(640)[None, :]
        _BAND = np.where((j - i >= 0) & (j - i <= 512), 0.0, -1e9).astype(np.float32)
    m = np.zeros((NTILES, 128, 4 * 640), np.float32)
    for t in range(NTILES):
        for qs in range(4):
            F0 = t * 512 + qs * 128
            pos = F0 + np.arange(640)
            edge = np.zeros(640, np.float32)
            if side == 0:
                edge[pos < 256] = -1e9
            else:
                edge[pos >= NTE - 256] = -1e9
            m[t, :, qs * 640:(qs + 1) * 640] = _BAND + edge[None, :]
    return m


def _core_consts(c):
    h = c % 2
    return {"posemb": _posemb_slice(h), "maskt_d": _win_masks(h)}


def _ceil(a, b):
    return (a + b - 1) // b


def fm_dram_tile(d, t0, L, c0=0, nch=C):
    return d[c0 * 128:(c0 + nch) * 128, t0:t0 + L].rearrange("(c p) t -> p c t", c=nch)


def tile_view(buf, L, nch=C):
    return buf[:, 0:nch * L].rearrange("p (c t) -> p c t", c=nch)


def ch(buf, c, L=TT):
    return buf[:, c * L:(c + 1) * L]


XROUNDS = 12


def exchange_drams():
    d = {}
    for r in range(XROUNDS):
        d[f"xch{r}"] = ([2 * 128 * 4096], F32, "shared")
    d["xfl"] = ([2 * XROUNDS], I32, "shared")
    d["side"] = ([1, 1], I32, "in")
    return d


def emit_xch_init(ctx):
    B = ctx.bufs
    ld = ctx.dma("sync", B["sside"][:], ctx.dram["side"][:], stream="xch")
    for r in range(XROUNDS):
        ctx.op("gpsimd", lambda e, r=r: e.memset(B["magics"][0:1, r:r + 1], MAGIC + r),
               inc="g" if r == XROUNDS - 1 else None)
    mg = ctx.now("g")
    ctx.wait("sync", "g", mg)
    ctx.wait_dma("sync", "xch")

    def setup(e):
        R = ctx.xregs
        e.reg_mov(R["pw"], 0)
        ctx.side_sv = e.partition_id() % 2
    ctx.raw("sync", setup)


def emit_exchange(ctx, rnd, src_ap, dst_ap, nelems, combine=None):
    B = ctx.bufs
    F = nelems // 128
    assert nelems % 128 == 0 and nelems <= 128 * 4096
    xch = ctx.dram[f"xch{rnd}"].tensor
    xfl = ctx.dram["xfl"].tensor
    sname = ctx._stream("xch")
    sem = ctx.sems[sname]
    psem = ctx.psems["poll"]
    after_payload = ctx.counts[sname] + 16

    def ex(e):
        R = ctx.xregs
        for br in e.Switch(ctx.side_sv, 2):
            e.dma_start(out=bass.AP(xch, br * nelems, [[F, 128], [1, F]]),
                        in_=src_ap).then_inc(sem, 16)
            e.wait_ge(sem, after_payload)
            e.dma_start(out=bass.AP(xfl, br * XROUNDS + rnd, [[1, 1], [1, 1]]),
                        in_=B["magics"][0:1, rnd:rnd + 1]).then_inc(sem, 16)
            e.reg_mov(R["cnd"], 1)
            with e.While(R["cnd"]):
                e.reg_add(R["pw"], R["pw"], 16)
                e.dma_start(out=B["flagsb"][0:1, 0:1],
                            in_=bass.AP(xfl, (1 - br) * XROUNDS + rnd, [[1, 1], [1, 1]])
                            ).then_inc(psem, 16)
                e.wait_ge(psem, R["pw"])
                e.reg_load(R["fl"], B["flagsb"][0:1, 0:1])
                e.reg_sub(R["cnd"], R["fl"], MAGIC + rnd)
            e.dma_start(out=dst_ap,
                        in_=bass.AP(xch, (1 - br) * nelems, [[F, 128], [1, F]])
                        ).then_inc(sem, 16)
    ctx.raw("sync", ex)
    ctx.counts[sname] += 48
    return ctx.counts[sname]


XCH_BUFS = dict(
    sside=([1, 1], I32), magics=([1, XROUNDS], I32), flagsb=([1, 1], I32),
)


KONST = {EPS: 0, EPS_ATTN: 1, 1.0: 2, -1.0e9: 3}


def emit_consts(ctx):
    B = ctx.bufs
    for val, col in KONST.items():
        ctx.op("gpsimd", lambda e, v=val, j=col: e.memset(B["konst"][:, j:j + 1], v),
               inc="g" if col == len(KONST) - 1 else None)
    kdone = ctx.now("g")
    for en in ("tensor", "vector", "scalar", "sync"):
        ctx.wait(en, "g", kdone)


def kc(ctx, val):
    return ctx.bufs["konst"][0:1, KONST[val]:KONST[val] + 1]


def emit_p0(ctx):
    B, Dm = ctx.bufs, ctx.dram
    ctx.op("gpsimd", lambda e: e.memset(B["onesr1"][0:1, :], 1.0), inc="g")
    ones_mk = ctx.now("g")
    ctx.wait("vector", "g", ones_mk)
    ctx.dma("sync", tile_view(B["gwf"], 6), Dm["f0_gw"][:, :].rearrange("(c p) e -> p c e", c=C), stream="cst")
    ctx.dma("sync", B["gb6"][0:6, :], Dm["f0_gb"][:], stream="cst")
    ctx.dma("sync", B["ones_col"][:], Dm["ones_col"][:], stream="cst")
    ctx.dma("sync", B["ones_rowf"][:], Dm["ones128f"][:], stream="cst")
    ctx.dma("sync", B["lngc"][:], Dm["f0_lngc"][:], stream="cst")
    ctx.dma("sync", B["lnbc"][:], Dm["f0_lnbc"][:], stream="cst")

    def ln_stats(xbf, sqbf, murow_ap, rstdrow_ap, wait_pre):
        for sem, val in wait_pre:
            ctx.wait("tensor", sem, val)
        ctx.wait_dma("tensor", "cst")
        for c in range(C):
            ctx.op("tensor", lambda e, c=c: e.matmul(
                B["pb0"][0:1, :], B["ones_col"][:], ch(xbf, c), start=(c == 0), stop=(c == C - 1)))
            ctx.op("tensor", lambda e, c=c: e.matmul(
                B["pb1"][0:1, :], B["ones_col"][:], ch(sqbf, c), start=(c == 0), stop=(c == C - 1)),
                inc="t" if c == C - 1 else None)
        sums_done = ctx.now("t")
        ctx.wait("vector", "t", sums_done)
        ctx.op("vector", lambda e: e.tensor_scalar_mul(murow_ap, B["pb0"][0:1, :], 1.0 / D))
        ctx.op("vector", lambda e: e.tensor_scalar_mul(B["r1"][0:1, :], B["pb1"][0:1, :], 1.0 / D))
        ctx.op("vector", lambda e: e.tensor_mul(B["r2"][0:1, :], murow_ap, murow_ap))
        ctx.op("vector", lambda e: e.tensor_sub(B["r1"][0:1, :], B["r1"][0:1, :], B["r2"][0:1, :]), inc="v")
        var_done = ctx.now("v")
        ctx.wait("scalar", "v", var_done)
        ctx.op("scalar", lambda e: e.activation(B["r2"][0:1, :], B["r1"][0:1, :], AF.Ln, bias=kc(ctx, EPS)))
        ctx.op("scalar", lambda e: e.activation(rstdrow_ap, B["r2"][0:1, :], AF.Exp, scale=-0.5), inc="s")
        return ctx.now("s"), sums_done

    for t in range(NTILES):
        t0 = t * TT
        ctx.dma("sync", tile_view(B["xt"], TT), fm_dram_tile(Dm["x_fm"], t0, TT), stream="xt")
        ctx.wait_dma("scalar", "xt")
        ctx.wait("scalar", "v", ctx.now("v"))
        ctx.wait("scalar", "t", ctx.now("t"))
        ctx.op("scalar", lambda e: e.copy(B["xb"][:], B["xt"][:]))
        ctx.op("scalar", lambda e: e.activation(B["sq"][:], B["xt"][:], AF.Square), inc="s")
        sq_done = ctx.now("s")
        rstd_done, _ = ln_stats(B["xb"], B["sq"], B["murow"][0:1, t0:t0 + TT],
                                B["rstdrow"][0:1, t0:t0 + TT], [("s", sq_done)])
        ctx.wait("tensor", "s", rstd_done)
        ctx.wait("tensor", "v", ctx.now("v"))
        ctx.op("tensor", lambda e: e.matmul(B["pb2"][:, :], B["ones_rowf"][:], B["murow"][0:1, t0:t0 + TT], start=True, stop=True))
        ctx.op("tensor", lambda e: e.matmul(B["pb3"][:, :], B["ones_rowf"][:], B["rstdrow"][0:1, t0:t0 + TT], start=True, stop=True), inc="t")
        bc_done = ctx.now("t")
        ctx.wait("vector", "t", bc_done)
        for c in range(C):
            ctx.op("vector", lambda e, c=c: e.tensor_sub(ch(B["t1"], c), ch(B["xt"], c), B["pb2"][:, :]))
            ctx.op("vector", lambda e, c=c: e.tensor_mul(ch(B["t1"], c), ch(B["t1"], c), B["pb3"][:, :]),
                   inc="v" if c == C - 1 else None)
        xn_done = ctx.now("v")
        ctx.wait("scalar", "v", xn_done)
        ctx.op("scalar", lambda e: e.copy(B["xb"][:], B["t1"][:]))
        ctx.op("scalar", lambda e: e.activation(B["sq"][:], B["t1"][:], AF.Square), inc="s")
        sq2_done = ctx.now("s")
        rstd2_done, _ = ln_stats(B["xb"], B["sq"], B["mu2r"][0:1, :], B["rstd2r"][0:1, :],
                                 [("s", sq2_done), ("v", xn_done)])
        ctx.wait("tensor", "s", rstd2_done)
        ctx.wait("tensor", "v", xn_done)
        ctx.op("tensor", lambda e: e.matmul(B["pb2"][:, :], B["ones_rowf"][:], B["mu2r"][0:1, :], start=True, stop=True))
        ctx.op("tensor", lambda e: e.matmul(B["pb3"][:, :], B["ones_rowf"][:], B["rstd2r"][0:1, :], start=True, stop=True), inc="t")
        bc2 = ctx.now("t")
        ctx.wait("vector", "t", bc2)
        for c in range(C):
            ctx.op("vector", lambda e, c=c: e.tensor_sub(ch(B["t2"], c), ch(B["t1"], c), B["pb2"][:, :]))
            ctx.op("vector", lambda e, c=c: e.tensor_mul(ch(B["t2"], c), ch(B["t2"], c), B["pb3"][:, :]))
            ctx.op("vector", lambda e, c=c: e.tensor_scalar(
                ch(B["t2"], c), ch(B["t2"], c), B["lngc"][:, c:c + 1], B["lnbc"][:, c:c + 1],
                ALU.mult, ALU.add), inc="v" if c == C - 1 else None)
        xn2_done = ctx.now("v")
        ctx.wait("scalar", "v", xn2_done)
        ctx.wait_dma("scalar", "sxnb")
        ctx.op("scalar", lambda e: e.copy(B["xnb"][:], B["t2"][:]), inc="s")
        xnb_done = ctx.now("s")
        ctx.wait("sync", "s", xnb_done)
        ctx.dma("sync", fm_dram_tile(Dm["xn2b_d"], t0, TT), tile_view(B["xnb"], TT), stream="sxnb")
        ctx.wait("tensor", "v", xn2_done)
        for c in range(C):
            ctx.op("tensor", lambda e, c=c: e.matmul(
                B["pb4"][0:6, :], B["gwf"][:, c * 6:(c + 1) * 6], ch(B["t2"], c),
                start=(c == 0), stop=(c == C - 1)), inc="t" if c == C - 1 else None)
        lg_done = ctx.now("t")
        ctx.wait("vector", "t", lg_done)
        ctx.wait_dma("vector", "cst")
        L32, M32, K32, I1, I2, R32 = B["L32"], B["M32"], B["K32"], B["I1"], B["I2"], B["R32"]
        ctx.op("vector", lambda e: e.memset(L32[0:8, :], NEG))
        ctx.op("vector", lambda e: e.tensor_scalar_add(L32[0:6, :], B["pb4"][0:6, :], B["gb6"][0:6, :]))
        ctx.op("vector", lambda e: e.tensor_copy(M32[:, :], L32[:, :]))
        for r in (4, 2, 1):
            ctx.op("vector", lambda e, r=r: e.stream_shuffle(R32[:, :], M32[:, :], mask=[(i + r) % 32 for i in range(32)]))
            ctx.op("vector", lambda e: e.tensor_max(M32[:, :], M32[:, :], R32[:, :]))
        ctx.op("vector", lambda e: e.stream_shuffle(R32[:, :], M32[:, :], mask=[0] * 32))
        ctx.op("vector", lambda e: e.tensor_tensor(I1[0:8, :], L32[0:8, :], R32[0:8, :], ALU.is_equal))
        ctx.op("vector", lambda e: e.scalar_tensor_tensor(K32[0:8, :], I1[0:8, :], NEG, L32[0:8, :], ALU.mult, ALU.add))
        for r in (4, 2, 1):
            ctx.op("vector", lambda e, r=r: e.stream_shuffle(R32[:, :], K32[:, :], mask=[(i + r) % 32 for i in range(32)]))
            ctx.op("vector", lambda e: e.tensor_max(K32[:, :], K32[:, :], R32[:, :]))
        ctx.op("vector", lambda e: e.tensor_sub(B["r1"][0:1, :], K32[0:1, :], M32[0:1, :]), inc="v")
        d01 = ctx.now("v")
        ctx.wait("scalar", "v", d01)
        ctx.op("scalar", lambda e: e.activation(B["r4"][0:1, :], B["r1"][0:1, :], AF.Sigmoid, scale=-1.0))
        ctx.op("scalar", lambda e: e.activation(B["r3"][0:1, :], B["r1"][0:1, :], AF.Sigmoid), inc="s")
        eb = ctx.now("s")
        ctx.wait("vector", "s", eb)
        ctx.op("vector", lambda e: e.scalar_tensor_tensor(R32[0:8, :], I1[0:8, :], NEG, L32[0:8, :], ALU.mult, ALU.add))
        ctx.op("vector", lambda e: e.stream_shuffle(B["S32"][:, :], K32[:, :], mask=[0] * 32))
        ctx.op("vector", lambda e: e.tensor_tensor(I2[0:8, :], R32[0:8, :], B["S32"][0:8, :], ALU.is_equal))
        ctx.op("vector", lambda e: e.tensor_copy(B["S32"][0:1, :], B["r4"][0:1, :]))
        ctx.op("vector", lambda e: e.stream_shuffle(B["S32"][:, :], B["S32"][:, :], mask=[0] * 32))
        ctx.op("vector", lambda e: e.tensor_mul(I1[0:8, :], I1[0:8, :], B["S32"][0:8, :]))
        ctx.op("vector", lambda e: e.tensor_copy(B["S32"][0:1, :], B["r3"][0:1, :]))
        ctx.op("vector", lambda e: e.stream_shuffle(B["S32"][:, :], B["S32"][:, :], mask=[0] * 32))
        ctx.op("vector", lambda e: e.tensor_mul(I2[0:8, :], I2[0:8, :], B["S32"][0:8, :]))
        ctx.op("vector", lambda e: e.tensor_add(B["wfull"][0:8, t0:t0 + TT], I1[0:8, :], I2[0:8, :]), inc="v")
        ctx.wait("sync", "v", ctx.now("v"))

    ctx.dma("sync", Dm["murow_d"][:], B["murow"][0:1, :], stream="sout")
    ctx.dma("sync", Dm["rstdrow_d"][:], B["rstdrow"][0:1, :], stream="sout")
    ctx.dma("sync", Dm["wfull_d"][:], B["wfull"][0:8, :], stream="sout")
    ctx.wait_dma("sync", "sout")


P0_BUFS = dict(
    xt=([128, C * TT], F32), t1=([128, C * TT], F32), t2=([128, C * TT], F32),
    xb=([128, C * TT], BF16), sq=([128, C * TT], BF16), xnb=([128, C * TT], BF16),
    gwf=([128, C * 6], F32), gb6=([8, 1], F32),
    lngc=([128, 8], F32), lnbc=([128, 8], F32),
    ones_col=([128, 1], BF16), ones_rowf=([1, 128], F32),
    murow=([1, NT], F32), rstdrow=([1, NT], F32), wfull=([8, NT], F32),
    mu2r=([1, TT], F32), rstd2r=([1, TT], F32),
    r1=([1, TT], F32), r2=([1, TT], F32), r3=([1, TT], F32), r4=([1, TT], F32),
    onesr1=([1, TT], F32),
    L32=([32, TT], F32), M32=([32, TT], F32), K32=([32, TT], F32),
    I1=([8, TT], F32), I2=([8, TT], F32), R32=([32, TT], F32), S32=([32, TT], F32),
)

PSUM_BANKS = ["pb0", "pb1", "pb2", "pb3", "pb4", "pb5", "pb6", "pb7"]


def emit_barrier(ctx):
    for en in ENGINES:
        for s in ("t", "v", "s", "g"):
            ctx.wait(en, s, ctx.now(s))
        for name in list(ctx.streams):
            ctx.wait_dma(en, name)


def build_program(phase_specs, drams, shared_bufs=None):
    nc = bass.Bass("TRN2", target_bir_lowering=False, detect_race_conditions=False)
    dram = {}
    for name, (shape, dt, kind) in drams.items():
        if kind == "in":
            dram[name] = nc.dram_tensor(name, shape, dt, kind="ExternalInput").ap()
        elif kind == "out":
            dram[name] = nc.dram_tensor(name, shape, dt, kind="ExternalOutput").ap()
        elif kind == "shared":
            dram[name] = nc.dram_tensor(name, shape, dt, addr_space="Shared").ap()
        else:
            dram[name] = nc.dram_tensor(name, shape, dt).ap()
    es = ExitStack()
    B = {}
    for name, (shape, dt) in (shared_bufs or {}).items():
        B[name] = es.enter_context(nc.sbuf_tensor("sb_" + name, shape, dt)).ap()
    B["konst"] = es.enter_context(nc.sbuf_tensor("sb_konst", [128, 8], F32)).ap()
    for name, (shape, dt) in XCH_BUFS.items():
        B[name] = es.enter_context(nc.sbuf_tensor("sb_" + name, shape, dt)).ap()
    for i in range(7):
        B[f"pb{i}"] = es.enter_context(nc.psum_tensor(f"pb{i}", [128, 512], F32)).ap()
    B["pt"] = es.enter_context(nc.psum_tensor("pt", [128, 1024], BF16)).ap()
    sems = {}
    for s in ["t", "v", "s", "g"] + [f"q{i}" for i in range(24)]:
        sems[s] = es.enter_context(nc.semaphore(s))
    psems = {"poll": es.enter_context(nc.semaphore("poll"))}
    xregs = {}
    for rn in ("side", "pw", "sflag", "pflag", "tmp", "tmp2", "cnd", "fl"):
        xregs[rn] = es.enter_context(nc.sync.register("xr_" + rn))
    block = es.enter_context(nc.Block())
    ctx = Ctx(nc, block, sems, B, dram)
    ctx.psems = psems
    ctx.xregs = xregs
    ctx.run(emit_consts)
    counter = [0]

    def run_specs(specs, base_bufs):
        for spec in specs:
            i = counter[0]
            counter[0] += 1
            if spec[0] is None:
                gshared, subspecs = spec[1], spec[2]
                with ExitStack() as gs:
                    gextra = {}
                    for name, (shape, dt) in (gshared or {}).items():
                        gextra[name] = gs.enter_context(
                            nc.sbuf_tensor(f"sg{i}_{name}", shape, dt)).ap()
                    run_specs(subspecs, dict(base_bufs, **gextra))
                continue
            fn, bufs = spec[0], spec[1]
            kwargs = spec[2] if len(spec) > 2 else {}
            with ExitStack() as ps:
                extra = {}
                for name, (shape, dt) in (bufs or {}).items():
                    if name in base_bufs:
                        continue
                    extra[name] = ps.enter_context(
                        nc.sbuf_tensor(f"sb{i}_{name}", shape, dt)).ap()
                ctx.bufs = dict(base_bufs, **extra)
                ctx.run(fn, **kwargs)
                ctx.run(emit_barrier)
    run_specs(phase_specs, B)
    ctx.bufs = B
    es.close()
    return nc


LAUNCH_A_DRAMS = dict(
    x_fm=([D, NT], F32, "in"),
    f0_gw=([D, 6], F32, "in"),
    f0_gb=([6, 1], F32, "in"),
    f0_lngc=([128, 8], F32, "in"),
    f0_lnbc=([128, 8], F32, "in"),
    ones_col=([128, 1], BF16, "in"),
    ones128f=([1, 128], F32, "in"),
    xn2b_d=([D, NT], BF16, "out"),
    murow_d=([1, NT], F32, "out"),
    rstdrow_d=([1, NT], F32, "out"),
    wfull_d=([8, NT], F32, "out"),
)


def kernel(**inputs):
    per_core = stage_inputs(inputs)
    raise NotImplementedError("phases under construction")


def sq(buf, c, j0=0, w=1024):
    return buf[:, c * 1024 + j0: c * 1024 + j0 + w]


def emit_mm_sq(ctx, lhsTb, rhsb, evac, wait_t=None, tag=""):
    B = ctx.bufs
    prev_evac = [None, None]
    for i in range(C):
        for j2 in range(2):
            g = i * 2 + j2
            bank = B[f"pb{g % 2}"]
            if prev_evac[g % 2] is not None:
                ctx.wait("tensor", "v", prev_evac[g % 2])
            if wait_t is not None and g == 0:
                for sem, val in wait_t:
                    ctx.wait("tensor", sem, val)
            for k in range(C):
                ctx.op("tensor", lambda e, i=i, j2=j2, k=k, bank=bank: e.matmul(
                    bank[:, 0:512], sq(lhsTb, k, i * 128, 128), sq(rhsb, k, j2 * 512, 512),
                    start=(k == 0), stop=(k == C - 1)), inc="t" if k == C - 1 else None)
            tmark = ctx.now("t")
            ctx.wait("vector", "t", tmark)
            evac(ctx, bank[:, 0:512], i, j2, last=(g == 15))
            prev_evac[g % 2] = ctx.now("v")
    return ctx.now("t"), ctx.now("v")


def emit_transpose_sq(ctx, srcb, dstb, f32=False, wait_t=None):
    B = ctx.bufs
    idt = B["idt_f"] if f32 else B["idt_b"]
    per = 4 if f32 else 8
    bank = B["pb2"] if f32 else B["pt"]
    prev_copy = None
    for j in range(C):
        for grp in range(C // per):
            if prev_copy is not None:
                ctx.wait("tensor", "v", prev_copy)
            if wait_t is not None and j == 0 and grp == 0:
                for sem, val in wait_t:
                    ctx.wait("tensor", sem, val)
            for ii in range(per):
                i = grp * per + ii
                ctx.op("tensor", lambda e, i=i, j=j, ii=ii: e.transpose(
                    bank[:, ii * 128:(ii + 1) * 128], sq(srcb, i, j * 128, 128), idt[:]),
                    inc="t" if ii == per - 1 else None)
            tm = ctx.now("t")
            ctx.wait("vector", "t", tm)
            ctx.op("vector", lambda e, j=j, grp=grp: e.tensor_copy(
                sq(dstb, j, grp * per * 128, per * 128), bank[:, 0:per * 128]), inc="v")
            prev_copy = ctx.now("v")
    return ctx.now("v")


def emit_p1(ctx):
    B, Dm = ctx.bufs, ctx.dram
    ctx.dma("sync", B["idt_b"][:], Dm["idt"][:], stream="cst")
    ctx.dma("sync", B["idt_f"][:], Dm["idtf"][:], stream="cst")
    S = 6
    NTAY = 4
    inv = 1.0 / (1 << S)
    for g in range(4):
        ctx.wait("sync", "v", ctx.now("v"))
        ctx.wait("sync", "s", ctx.now("s"))
        ctx.dma("sync", B["gensf"][:].rearrange("p (c n) -> p c n", c=C),
                Dm["f0_gens"][g].rearrange("(c p) n -> p c n", c=C), stream="gens")
        ctx.wait_dma("vector", "gens")
        ctx.wait_dma("vector", "cst")
        ctx.op("vector", lambda e: e.tensor_scalar_mul(B["Df"][:], B["gensf"][:], inv / NTAY), inc="v")
        dready = ctx.now("v")
        for k in range(NTAY - 1, 0, -1):
            ctx.wait("scalar", "v", dready)
            ctx.op("scalar", lambda e: e.copy(B["Db"][:], B["Df"][:]))
            ctx.op("scalar", lambda e, k=k: e.activation(B["Ab"][:], B["gensf"][:], AF.Copy, scale=-inv / k), inc="s")
            casts = ctx.now("s")

            def evac(ctx, ps, i, j2, last, k=k):
                ctx.op("vector", lambda e: e.scalar_tensor_tensor(
                    sq(B["Df"], i, j2 * 512, 512), sq(B["gensf"], i, j2 * 512, 512),
                    inv / k, ps, ALU.mult, ALU.add), inc="v")
            emit_mm_sq(ctx, B["Ab"], B["Db"], evac, wait_t=[("s", casts)])
            dready = ctx.now("v")
        for sqi in range(S):
            ctx.wait("scalar", "v", dready)
            ctx.op("scalar", lambda e: e.copy(B["Db"][:], B["Df"][:]), inc="s")
            casts = ctx.now("s")
            tdone = emit_transpose_sq(ctx, B["Db"], B["DTb"], wait_t=[("s", casts)])

            def evac(ctx, ps, i, j2, last):
                ctx.op("vector", lambda e: e.scalar_tensor_tensor(
                    sq(B["Df"], i, j2 * 512, 512), sq(B["Df"], i, j2 * 512, 512),
                    2.0, ps, ALU.mult, ALU.add), inc="v")
            emit_mm_sq(ctx, B["DTb"], B["Db"], evac, wait_t=[("v", tdone)])
            dready = ctx.now("v")
        if g == 0:
            ctx.wait("vector", "v", dready)
            ctx.op("vector", lambda e: e.tensor_copy(B["Rf"][:], B["Df"][:]), inc="v")
        else:
            ctx.wait("scalar", "v", dready)
            ctx.op("scalar", lambda e: e.copy(B["Db"][:], B["Df"][:]))
            ctx.op("scalar", lambda e: e.copy(B["RTb"][:], B["RTf"][:]), inc="s")
            casts = ctx.now("s")

            def evac(ctx, ps, i, j2, last):
                ctx.op("vector", lambda e: e.scalar_tensor_tensor(
                    sq(B["Rf"], i, j2 * 512, 512), sq(B["Rf"], i, j2 * 512, 512),
                    1.0, ps, ALU.mult, ALU.add), inc="v")
            emit_mm_sq(ctx, B["RTb"], B["Db"], evac, wait_t=[("s", casts)])
            ctx.wait("vector", "v", ctx.now("v"))
            ctx.op("vector", lambda e: e.tensor_add(B["Rf"][:], B["Rf"][:], B["Df"][:]), inc="v")
        rdone = ctx.now("v")
        if g < 3:
            emit_transpose_sq(ctx, B["Rf"], B["RTf"], f32=True, wait_t=[("v", rdone)])
    ctx.wait("sync", "v", ctx.now("v"))
    ctx.dma("sync", Dm["rotD_d"][:].rearrange("(c p) n -> p c n", c=C),
            B["Rf"][:].rearrange("p (c n) -> p c n", c=C), stream="sout")
    ctx.wait_dma("sync", "sout")


P1_BUFS = dict(
    gensf=([128, C * 1024], F32), Df=([128, C * 1024], F32),
    Rf=([128, C * 1024], F32), RTf=([128, C * 1024], F32),
    Ab=([128, C * 1024], BF16), Db=([128, C * 1024], BF16),
    DTb=([128, C * 1024], BF16), RTb=([128, C * 1024], BF16),
    idt_b=([128, 128], BF16), idt_f=([128, 128], F32),
)

LAUNCH_EXPM_DRAMS = dict(
    f0_gens=([4, D, D], F32, "in"),
    idt=([128, 128], BF16, "in"),
    idtf=([128, 128], F32, "in"),
    rotD_d=([D, D], F32, "out"),
)


def emit_proj(ctx, nj, nk, lhsT_fn, rhs_fn, evac, banks=("pb0", "pb1"), wait_first=None, nfree=TT):
    prev = {b: None for b in banks}
    last = None
    for j in range(nj):
        bank = ctx.bufs[banks[j % len(banks)]]
        if prev[banks[j % len(banks)]] is not None:
            sem, cnt = prev[banks[j % len(banks)]]
            ctx.wait("tensor", sem, cnt)
        if wait_first is not None and j == 0:
            for sem, val in wait_first:
                if sem.startswith("q:"):
                    ctx.wait_dma("tensor", sem[2:])
                else:
                    ctx.wait("tensor", sem, val)
        for k in range(nk):
            ctx.op("tensor", lambda e, k=k, j=j, bank=bank: e.matmul(
                bank[:, 0:nfree], lhsT_fn(k, j), rhs_fn(k),
                start=(k == 0), stop=(k == nk - 1)), inc="t" if k == nk - 1 else None)
        tm = ctx.now("t")
        last = evac(ctx, bank[:, 0:nfree], j, tm)
        prev[banks[j % len(banks)]] = last
    return last


def _ew_evac(ctx, engine, fn, tmark):
    ctx.wait(engine, "t", tmark)
    sem = {"vector": "v", "scalar": "s", "gpsimd": "g"}[engine]
    ctx.op(engine, fn, inc=sem)
    return (sem, ctx.now(sem))


def emit_p2(ctx):
    B, Dm = ctx.bufs, ctx.dram
    ctx.dma("sync", B["wfull"][0:8, :], Dm["wfull_d"][:], stream="cst")
    ctx.dma("sync", B["ind8"][:], Dm["ind8"][:], stream="cst")
    ctx.wait_dma("scalar", "cst")
    ctx.op("scalar", lambda e: e.copy(B["wfullb"][0:8, :], B["wfull"][0:8, :]), inc="s")
    wfb = ctx.now("s")
    for e_i in range(6):
        ctx.wait("gpsimd", "t", ctx.now("t"))
        ctx.dma("gpsimd", B["ew1b"][:].rearrange("p (c n) -> p c n", c=C),
                Dm["f0_ew1"][e_i].rearrange("(c p) n -> p c n", c=C), stream="wup")
        for half in range(2):
            ctx.dma("gpsimd", B["ew2b"][:, half * 8 * 1024:(half + 1) * 8 * 1024].rearrange("p (c n) -> p c n", c=8),
                    Dm["f0_ew2"][e_i, half * 1024:(half + 1) * 1024].rearrange("(c p) n -> p c n", c=8), stream="wup")
        ctx.dma("sync", B["eb1t"][:], Dm["f0_eb1c"][e_i], stream="wups")
        for t in range(NTILES):
            t0 = t * TT
            ctx.wait("sync", "t", ctx.now("t"))
            ctx.dma("sync", tile_view(B["xnb"], TT), fm_dram_tile(Dm["xn2b_d"], t0, TT), stream="xnb")
            ctx.wait_dma("tensor", "xnb")
            if t == 0:
                ctx.wait_dma("tensor", "wup")
                ctx.wait_dma("tensor", "wups")
                ctx.wait("tensor", "s", wfb)
            ctx.wait("tensor", "v", ctx.now("v"))
            ctx.op("tensor", lambda e: e.matmul(
                B["pb4"][:, 0:TT], B["ind8"][:, e_i * 128:(e_i + 1) * 128],
                B["wfullb"][0:8, t0:t0 + TT], start=True, stop=True), inc="t")
            wbm = ctx.now("t")
            ctx.wait("vector", "t", wbm)
            ctx.op("vector", lambda e: e.tensor_copy(B["wbs"][:, 0:TT], B["pb4"][:, 0:TT]), inc="v")
            wbs_done = ctx.now("v")
            def lh(k, j):
                return B["ew1b"][:, k * 2048 + j * 128: k * 2048 + (j + 1) * 128]
            def rh(k):
                return ch(B["xnb"], k)
            def evh(ctx, ps, j, tm):
                ctx.wait_dma("scalar", "wups")
                return _ew_evac(ctx, "scalar", lambda e, ps=ps, j=j: e.activation(
                    B["he"][:, j * TT:(j + 1) * TT], ps, AF.Gelu, bias=B["eb1t"][:, j:j + 1]), tm)
            hsem, hcnt = emit_proj(ctx, 16, C, lh, rh, evh)
            if e_i > 0:
                ctx.dma("sync", tile_view(B["ntile"], TT), fm_dram_tile(Dm["nacc_d"], t0, TT), stream="nacc")
                ctx.wait_dma("vector", "nacc")

            def lo(k, j):
                return B["ew2b"][:, k * 1024 + j * 128: k * 1024 + (j + 1) * 128]
            def ro(k):
                return B["he"][:, k * TT:(k + 1) * TT]
            def evo(ctx, ps, j, tm):
                ctx.wait("vector", "t", tm)
                if e_i == 0:
                    ctx.op("vector", lambda e, ps=ps, j=j: e.tensor_mul(
                        ch(B["ntile"], j), ps, B["wbs"][:, 0:TT]), inc="v")
                else:
                    ctx.op("vector", lambda e, ps=ps: e.tensor_mul(B["tmp5"][:, 0:TT], ps, B["wbs"][:, 0:TT]))
                    ctx.op("vector", lambda e, j=j: e.tensor_add(
                        ch(B["ntile"], j), ch(B["ntile"], j), B["tmp5"][:, 0:TT]), inc="v")
                return ("v", ctx.now("v"))
            emit_proj(ctx, C, 16, lo, ro, evo, banks=("pb2", "pb3"),
                      wait_first=[(hsem, hcnt), ("v", wbs_done)])
            ctx.wait("sync", "v", ctx.now("v"))
            ctx.dma("sync", fm_dram_tile(Dm["nacc_d"], t0, TT), tile_view(B["ntile"], TT), stream="nacc")


def emit_p2rot(ctx):
    B, Dm = ctx.bufs, ctx.dram
    ctx.dma("sync", B["wfull"][0:8, :], Dm["wfull_d"][:], stream="cst2")
    ctx.wait_dma("scalar", "cst2")
    ctx.op("scalar", lambda e: e.copy(B["wfullb"][0:8, :], B["wfull"][0:8, :]), inc="s")
    ctx.dma("gpsimd", B["eb2b"][:], Dm["f0_eb2p"][:], stream="cst2g")
    ctx.dma("gpsimd", B["rotDb"][:].rearrange("p (c n) -> p c n", c=C),
            Dm["rotD_d"][:].rearrange("(c p) n -> p c n", c=C), stream="cst2g")
    for t in range(NTILES):
        t0 = t * TT
        ctx.wait("sync", "s", ctx.now("s"))
        ctx.dma("sync", tile_view(B["ntile"], TT), fm_dram_tile(Dm["nacc_d"], t0, TT), stream="nacc")
        ctx.wait_dma("tensor", "nacc")
        if t == 0:
            ctx.wait_dma("tensor", "cst2")
            ctx.wait_dma("tensor", "cst2g")
            ctx.wait("tensor", "s", ctx.now("s"))

        def l1(k, j):
            return B["eb2b"][0:8, j * 128:(j + 1) * 128]
        def r1(k):
            return B["wfullb"][0:8, t0:t0 + TT]
        def ev1(ctx, ps, j, tm):
            return _ew_evac(ctx, "vector", lambda e, ps=ps, j=j: e.tensor_add(
                ch(B["ntile"], j), ch(B["ntile"], j), ps), tm)
        emit_proj(ctx, C, 1, l1, r1, ev1)
        nadd = ctx.now("v")
        ctx.wait("scalar", "v", nadd)
        ctx.op("scalar", lambda e: e.copy(B["nb"][:], B["ntile"][:]), inc="s")
        nbc = ctx.now("s")

        def l2(k, j):
            return B["rotDb"][:, k * 1024 + j * 128: k * 1024 + (j + 1) * 128]
        def r2(k):
            return ch(B["nb"], k)
        def ev2(ctx, ps, j, tm):
            return _ew_evac(ctx, "vector", lambda e, ps=ps, j=j: e.tensor_add(
                ch(B["ntile"], j), ch(B["ntile"], j), ps), tm)
        emit_proj(ctx, C, C, l2, r2, ev2, banks=("pb2", "pb3"), wait_first=[("s", nbc)])
        ctx.wait("sync", "v", ctx.now("v"))
        ctx.dma("sync", fm_dram_tile(Dm["nacc_d"], t0, TT), tile_view(B["ntile"], TT), stream="nacc")
    ctx.wait_dma("sync", "nacc")


P2_BUFS = dict(
    wfull=([8, NT], F32), wfullb=([8, NT], BF16), ind8=([8, 8 * 128], BF16),
    ew1b=([128, C * 2048], BF16), ew2b=([128, 16 * 1024], BF16),
    eb1t=([128, 16], F32), eb2b=([8, 1024], BF16),
    xnb=([128, C * TT], BF16), he=([128, 16 * TT], BF16),
    ntile=([128, C * TT], F32), nb=([128, C * TT], BF16),
    tmp5=([128, TT], F32), wbs=([128, TT], F32),
    rotDb=([128, C * 1024], BF16),
)


P2ROT_BUFS = dict(
    wfull=([8, NT], F32), wfullb=([8, NT], BF16), eb2b=([8, 1024], BF16),
    rotDb=([128, C * 1024], BF16), ntile=([128, C * TT], F32), nb=([128, C * TT], BF16),
)


def emit_p3(ctx):
    B, Dm = ctx.bufs, ctx.dram
    ctx.dma("sync", B["ones_rowf"][:], Dm["ones128f"][:], stream="cst3")
    ctx.dma("sync", B["gcwa"][:], Dm["f0_gcwa"][:], stream="cst3")
    ctx.dma("sync", B["gcwb"][:], Dm["f0_gcwb"][:], stream="cst3")
    ctx.dma("sync", B["gcb"][:], Dm["f0_gcb"][:], stream="cst3")
    ctx.dma("sync", B["ab1c"][:], Dm["f0_ab1c"][:], stream="cst3")
    ctx.dma("sync", B["ab2c"][:], Dm["f0_ab2c"][:], stream="cst3")
    for t in range(NTILES):
        t0 = t * TT
        ctx.wait("sync", "s", ctx.now("s"))
        ctx.wait("sync", "v", ctx.now("v"))
        ctx.wait("sync", "t", ctx.now("t"))
        ctx.dma("sync", tile_view(B["xt"], TT), fm_dram_tile(Dm["x_fm"], t0, TT), stream="xt")
        ctx.dma("sync", tile_view(B["xnb"], TT), fm_dram_tile(Dm["xn2b_d"], t0, TT), stream="xt")
        ctx.dma("sync", tile_view(B["ntile"], TT), fm_dram_tile(Dm["nacc_d"], t0, TT), stream="xt")
        ctx.dma("sync", B["murt"][:], Dm["murow_d"][0:1, t0:t0 + TT], stream="xt")
        ctx.dma("sync", B["rstdt"][:], Dm["rstdrow_d"][0:1, t0:t0 + TT], stream="xt")
        for j2 in range(4):
            ctx.wait("gpsimd", "t", ctx.now("t"))
            ctx.dma("gpsimd", B["aw1s"][:].rearrange("p (c n) -> p c n", c=C),
                    Dm["f0_aw1"][:, j2 * 1024:(j2 + 1) * 1024].rearrange("(c p) n -> p c n", c=C),
                    stream="aw")
            ctx.dma("gpsimd", B["aw2s"][:].rearrange("p (c n) -> p c n", c=C),
                    Dm["f0_aw2"][j2 * 1024:(j2 + 1) * 1024, :].rearrange("(c p) n -> p c n", c=C),
                    stream="aw")

            def lh(k, j):
                return B["aw1s"][:, k * 1024 + j * 128:k * 1024 + (j + 1) * 128]
            def rh(k):
                return ch(B["xnb"], k)
            def evh(ctx, ps, j, tm, j2=j2):
                return _ew_evac(ctx, "scalar", lambda e, ps=ps, j=j: e.activation(
                    B["h1"][:, j * TT:(j + 1) * TT], ps, AF.Gelu,
                    bias=B["ab1c"][:, j2 * 8 + j:j2 * 8 + j + 1]), tm)
            wf = [("q:aw", 0), ("q:xt", 0)] if True else None
            hsem, hcnt = emit_proj(ctx, 8, C, lh, rh, evh, wait_first=wf)

            def lo(k, j):
                return B["aw2s"][:, k * 1024 + j * 128:k * 1024 + (j + 1) * 128]
            def ro(k):
                return B["h1"][:, k * TT:(k + 1) * TT]
            def evo(ctx, ps, j, tm, j2=j2):
                ctx.wait("vector", "t", tm)
                if j2 == 0:
                    ctx.op("vector", lambda e, ps=ps, j=j: e.tensor_scalar_add(
                        ch(B["abel"], j), ps, B["ab2c"][:, j:j + 1]), inc="v")
                else:
                    ctx.op("vector", lambda e, ps=ps, j=j: e.tensor_add(
                        ch(B["abel"], j), ch(B["abel"], j), ps), inc="v")
                return ("v", ctx.now("v"))
            emit_proj(ctx, C, 8, lo, ro, evo, banks=("pb2", "pb3"), wait_first=[(hsem, hcnt)])
        abel_done = ctx.now("v")
        ctx.wait("tensor", "v", abel_done)
        ctx.wait_dma("tensor", "cst3")
        ctx.wait("tensor", "s", ctx.now("s"))
        for c in range(C):
            ctx.op("tensor", lambda e, c=c: e.matmul(
                B["pb4"][0:1, 0:TT], B["gcwa"][:, c:c + 1], ch(B["abel"], c),
                start=(c == 0), stop=False))
        for c in range(C):
            ctx.op("tensor", lambda e, c=c: e.matmul(
                B["pb4"][0:1, 0:TT], B["gcwb"][:, c:c + 1], ch(B["ntile"], c),
                start=False, stop=(c == C - 1)), inc="t" if c == C - 1 else None)
        gm = ctx.now("t")
        ctx.wait("scalar", "t", gm)
        ctx.op("scalar", lambda e: e.activation(B["r3"][0:1, :], B["pb4"][0:1, 0:TT],
                                                AF.Sigmoid, bias=B["gcb"][0:1, 0:1]), inc="s")
        gr = ctx.now("s")
        ctx.wait("tensor", "s", gr)
        ctx.wait("tensor", "v", ctx.now("v"))
        ctx.op("tensor", lambda e: e.matmul(B["pb5"][:, 0:TT], B["ones_rowf"][:], B["r3"][0:1, :], start=True, stop=True))
        ctx.op("tensor", lambda e: e.matmul(B["pb4"][:, 0:TT], B["ones_rowf"][:], B["murt"][0:1, :], start=True, stop=True))
        ctx.op("tensor", lambda e: e.matmul(B["pb6"][:, 0:TT], B["ones_rowf"][:], B["rstdt"][0:1, :], start=True, stop=True), inc="t")
        bc = ctx.now("t")
        ctx.wait("vector", "t", bc)
        for c in range(C):
            ctx.op("vector", lambda e, c=c: e.tensor_sub(ch(B["t2"], c), ch(B["xt"], c), B["pb4"][:, 0:TT]))
            ctx.op("vector", lambda e, c=c: e.tensor_mul(ch(B["t2"], c), ch(B["t2"], c), B["pb6"][:, 0:TT]))
            ctx.op("vector", lambda e, c=c: e.tensor_sub(B["tmp5"][:, 0:TT], ch(B["ntile"], c), ch(B["abel"], c)))
            ctx.op("vector", lambda e, c=c: e.tensor_mul(B["tmp5"][:, 0:TT], B["tmp5"][:, 0:TT], B["pb5"][:, 0:TT]))
            ctx.op("vector", lambda e, c=c: e.tensor_add(B["tmp5"][:, 0:TT], B["tmp5"][:, 0:TT], ch(B["abel"], c)))
            ctx.op("vector", lambda e, c=c: e.tensor_add(ch(B["t2"], c), ch(B["t2"], c), B["tmp5"][:, 0:TT]),
                   inc="v" if c == C - 1 else None)
        mix_done = ctx.now("v")
        ctx.wait("scalar", "v", mix_done)
        ctx.wait_dma("scalar", "gst")
        ctx.op("scalar", lambda e: e.activation(B["gt"][:], B["t2"][:], AF.Silu), inc="s")
        g_done = ctx.now("s")
        ctx.wait("sync", "s", g_done)
        ctx.dma("sync", fm_dram_tile(Dm["g_scr"], t0, TT), tile_view(B["gt"], TT), stream="gst")
    ctx.wait_dma("sync", "gst")


P3_BUFS = dict(
    xt=([128, C * TT], F32), t2=([128, C * TT], F32), xnb=([128, C * TT], BF16),
    ntile=([128, C * TT], F32), abel=([128, C * TT], F32), gt=([128, C * TT], F32),
    h1=([128, 8 * TT], BF16), aw1s=([128, C * 1024], BF16), aw2s=([128, C * 1024], BF16),
    murt=([1, TT], F32), rstdt=([1, TT], F32), r3=([1, TT], F32),
    gcwa=([128, 8], F32), gcwb=([128, 8], F32), gcb=([1, 1], F32),
    ab1c=([128, 32], F32), ab2c=([128, 8], F32), tmp5=([128, TT], F32),
    ones_rowf=([1, 128], F32),
)

LAUNCH_B_DRAMS = dict(
    x_fm=([D, NT], F32, "in"),
    xn2b_d=([D, NT], BF16, "in"),
    wfull_d=([8, NT], F32, "in"),
    rotD_d=([D, D], F32, "in"),
    murow_d=([1, NT], F32, "in"),
    rstdrow_d=([1, NT], F32, "in"),
    f0_ew1=([6, D, 2048], F32, "in"),
    f0_ew2=([6, 2048, D], F32, "in"),
    f0_eb1c=([6, 128, 16], F32, "in"),
    f0_eb2p=([8, 1024], F32, "in"),
    f0_aw1=([D, 4 * D], F32, "in"),
    f0_aw2=([4 * D, D], F32, "in"),
    f0_ab1c=([128, 32], F32, "in"),
    f0_ab2c=([128, 8], F32, "in"),
    f0_gcwa=([128, 8], F32, "in"),
    f0_gcwb=([128, 8], F32, "in"),
    f0_gcb=([1, 1], F32, "in"),
    ind8=([8, 8 * 128], BF16, "in"),
    ones128f=([1, 128], F32, "in"),
    nacc_d=([D, NT], F32, "internal"),
    g_scr=([D, NT], F32, "out"),
)


def _ln_stats_rows(ctx, xbf, sqbf, murow_ap, rstdrow_ap, wait_pre, banks=("pb0", "pb1")):
    B = ctx.bufs
    for sem, val in wait_pre:
        if sem.startswith("q:"):
            ctx.wait_dma("tensor", sem[2:])
        else:
            ctx.wait("tensor", sem, val)
    b0, b1 = ctx.bufs[banks[0]], ctx.bufs[banks[1]]
    for c in range(C):
        ctx.op("tensor", lambda e, c=c: e.matmul(
            b0[0:1, :], B["ones_col"][:], ch(xbf, c), start=(c == 0), stop=(c == C - 1)))
        ctx.op("tensor", lambda e, c=c: e.matmul(
            b1[0:1, :], B["ones_col"][:], ch(sqbf, c), start=(c == 0), stop=(c == C - 1)),
            inc="t" if c == C - 1 else None)
    sums_done = ctx.now("t")
    ctx.wait("vector", "t", sums_done)
    ctx.op("vector", lambda e: e.tensor_scalar_mul(murow_ap, b0[0:1, :], 1.0 / D))
    ctx.op("vector", lambda e: e.tensor_scalar_mul(B["r1"][0:1, :], b1[0:1, :], 1.0 / D))
    ctx.op("vector", lambda e: e.tensor_mul(B["r2"][0:1, :], murow_ap, murow_ap))
    ctx.op("vector", lambda e: e.tensor_sub(B["r1"][0:1, :], B["r1"][0:1, :], B["r2"][0:1, :]), inc="v")
    var_done = ctx.now("v")
    ctx.wait("scalar", "v", var_done)
    ctx.op("scalar", lambda e: e.activation(B["r2"][0:1, :], B["r1"][0:1, :], AF.Ln, bias=kc(ctx, EPS)))
    ctx.op("scalar", lambda e: e.activation(rstdrow_ap, B["r2"][0:1, :], AF.Exp, scale=-0.5), inc="s")
    return ctx.now("s"), sums_done


def _bcast2(ctx, murow_ap, rstdrow_ap, rstd_done, free_mark, banks=("pb2", "pb3")):
    B = ctx.bufs
    ctx.wait("tensor", "s", rstd_done)
    ctx.wait("tensor", "v", free_mark)
    ctx.op("tensor", lambda e: e.matmul(ctx.bufs[banks[0]][:, :], B["ones_rowf"][:], murow_ap, start=True, stop=True))
    ctx.op("tensor", lambda e: e.matmul(ctx.bufs[banks[1]][:, :], B["ones_rowf"][:], rstdrow_ap, start=True, stop=True), inc="t")
    bc = ctx.now("t")
    ctx.wait("vector", "t", bc)
    return bc


def emit_p4_init(ctx):
    B, Dm = ctx.bufs, ctx.dram
    ctx.dma("sync", B["ones_rowf"][:], Dm["ones128f"][:], stream="cst4")
    for t in range(NTILES):
        t0 = t * TT
        ctx.wait("sync", "v", ctx.now("v"))
        ctx.dma("sync", tile_view(B["xt"], TT), fm_dram_tile(Dm["x_fm"], t0, TT), stream="xt")
        ctx.dma("sync", tile_view(B["pe"], TT), fm_dram_tile(Dm["posemb"], t0, TT), stream="xt")
        ctx.dma("sync", B["murt"][:], Dm["murow_d"][0:1, t0:t0 + TT], stream="xt")
        ctx.dma("sync", B["rstdt"][:], Dm["rstdrow_d"][0:1, t0:t0 + TT], stream="xt")
        ctx.wait_dma("tensor", "xt")
        ctx.wait_dma("tensor", "cst4")
        ctx.wait("tensor", "v", ctx.now("v"))
        ctx.op("tensor", lambda e: e.matmul(B["pb2"][:, :], B["ones_rowf"][:], B["murt"][0:1, :], start=True, stop=True))
        ctx.op("tensor", lambda e: e.matmul(B["pb3"][:, :], B["ones_rowf"][:], B["rstdt"][0:1, :], start=True, stop=True), inc="t")
        bc = ctx.now("t")
        ctx.wait("vector", "t", bc)
        for c in range(C):
            yv = B["y"][:, c * NT + t0: c * NT + t0 + TT]
            ctx.op("vector", lambda e, c=c, yv=yv: e.tensor_sub(B["tmp5"][:, 0:TT], ch(B["xt"], c), B["pb2"][:, :]))
            ctx.op("vector", lambda e, c=c, yv=yv: e.tensor_mul(B["tmp5"][:, 0:TT], B["tmp5"][:, 0:TT], B["pb3"][:, :]))
            ctx.op("vector", lambda e, c=c, yv=yv: e.scalar_tensor_tensor(
                yv, B["tmp5"][:, 0:TT], 32.0, ch(B["pe"], c), ALU.mult, ALU.add),
                inc="v" if c == C - 1 else None)


P4I_BUFS = dict(
    xt=([128, C * TT], F32), pe=([128, C * TT], F32), tmp5=([128, TT], F32),
    murt=([1, TT], F32), rstdt=([1, TT], F32),
    ones_rowf=([1, 128], F32), r1=([1, TT], F32), r2=([1, TT], F32),
    ones_col=([128, 1], BF16),
)


def emit_p4_attA(ctx, l):
    B, Dm = ctx.bufs, ctx.dram
    ctx.dma("sync", B["ones_col"][:], Dm["ones_col"][:], stream="cstA")
    ctx.dma("sync", B["ones_rowf"][:], Dm["ones128f"][:], stream="cstA")
    ctx.dma("sync", B["ln1s"][:], Dm["l0_ln1sc"][l], stream="cstA")
    ctx.dma("sync", B["ln1b"][:], Dm["l0_ln1bc"][l], stream="cstA")
    ctx.dma("gpsimd", B["wkb"][:].rearrange("p (c n) -> p c n", c=C),
            Dm[f"l0_Wk{l}"][:].rearrange("(c p) n -> p c n", c=C), stream="wkv")
    ctx.dma("gpsimd", B["wvb"][:].rearrange("p (c n) -> p c n", c=C),
            Dm[f"l0_Wv{l}"][:].rearrange("(c p) n -> p c n", c=C), stream="wkv")
    ctx.dma("gpsimd", B["bkr"][:], Dm["l0_bk"][l:l + 1, :], stream="wkv")
    ctx.dma("gpsimd", B["bvr"][:], Dm["l0_bv"][l:l + 1, :], stream="wkv")
    ctx.dma("gpsimd", B["onesr"][:], Dm["ones128"][:], stream="wkv")
    nmm_kv = 0
    for t in range(NTILES):
        t0 = t * TT
        ctx.wait("scalar", "t", ctx.now("t"))
        ctx.wait("scalar", "s", ctx.now("s"))
        yv = lambda c: B["y"][:, c * NT + t0: c * NT + t0 + TT]
        for c in range(C):
            ctx.op("scalar", lambda e, c=c: e.copy(ch(B["xe"], c), yv(c)))
            ctx.op("scalar", lambda e, c=c: e.activation(ch(B["sq"], c), yv(c), AF.Square),
                   inc="s" if c == C - 1 else None)
        casts = ctx.now("s")
        rstd_done, _ = _ln_stats_rows(ctx, B["xe"], B["sq"], B["mur"][0:1, :], B["rstdr"][0:1, :],
                                      [("s", casts), ("q:cstA", 0)])
        bc = _bcast2(ctx, B["mur"][0:1, :], B["rstdr"][0:1, :], rstd_done, ctx.now("v"))
        ctx.wait_dma("vector", "xes")
        for c in range(C):
            ctx.op("vector", lambda e, c=c: e.tensor_sub(B["tmp5"][:, 0:TT], yv(c), B["pb2"][:, :]))
            ctx.op("vector", lambda e, c=c: e.tensor_mul(B["tmp5"][:, 0:TT], B["tmp5"][:, 0:TT], B["pb3"][:, :]))
            ctx.op("vector", lambda e, c=c: e.tensor_scalar(
                ch(B["xe"], c), B["tmp5"][:, 0:TT], B["ln1s"][:, c:c + 1], B["ln1b"][:, c:c + 1],
                ALU.mult, ALU.add), inc="v" if c == C - 1 else None)
        xe_done = ctx.now("v")
        ctx.wait("sync", "v", xe_done)
        ctx.dma("sync", fm_dram_tile(Dm["xe_scr"], t0, TT), tile_view(B["xe"], TT), stream="xes")
        ctx.wait("tensor", "v", xe_done)
        if t == 0:
            ctx.wait_dma("tensor", "wkv")
        prev_ev = {"pb0": None, "pb1": None}
        for s4 in range(4):
            for half in range(2):
                for W, dst, isk in ((B["wkb"], B["kp"], True), (B["wvb"], B["vt"], False)):
                    bname = "pb0" if isk else "pb1"
                    bank = B[bname]
                    br = B["bkr"] if isk else B["bvr"]
                    if prev_ev[bname] is not None:
                        ctx.wait("tensor", "v", prev_ev[bname])
                    ctx.op("tensor", lambda e, bank=bank, br=br, half=half: e.matmul(
                        bank[:, 0:512], B["onesr"][:], br[0:1, half * 512:(half + 1) * 512],
                        start=True, stop=False))
                    for k in range(C):
                        ctx.op("tensor", lambda e, k=k, s4=s4, half=half, W=W, bank=bank: e.matmul(
                            bank[:, 0:512], ch(B["xe"], k)[:, s4 * 128:(s4 + 1) * 128],
                            W[:, k * 1024 + half * 512: k * 1024 + (half + 1) * 512],
                            start=False, stop=(k == C - 1)), inc="t" if k == C - 1 else None)
                    tm = ctx.now("t")
                    dsl = dst[:, half * 512:(half + 1) * 512]
                    if isk:
                        ctx.wait("vector", "t", tm)
                        ctx.op("vector", lambda e, bank=bank: e.tensor_single_scalar(
                            B["tmp5"][:, 0:512], bank[:, 0:512], 0.0, ALU.min))
                        ctx.op("vector", lambda e, bank=bank: e.tensor_relu(
                            B["tmp6"][:, 0:512], bank[:, 0:512]), inc="v")
                        vm = ctx.now("v")
                        ctx.wait("scalar", "v", vm)
                        ctx.op("scalar", lambda e: e.activation(B["tmp5"][:, 0:512], B["tmp5"][:, 0:512], AF.Exp), inc="s")
                        sm = ctx.now("s")
                        ctx.wait("vector", "s", sm)
                        ctx.op("vector", lambda e, dsl=dsl: e.tensor_add(dsl, B["tmp5"][:, 0:512], B["tmp6"][:, 0:512]), inc="v")
                    else:
                        ctx.wait("vector", "t", tm)
                        ctx.op("vector", lambda e, dsl=dsl, bank=bank: e.tensor_copy(dsl, bank[:, 0:512]), inc="v")
                    prev_ev[bname] = ctx.now("v")
            kvready = ctx.now("v")
            ctx.wait("tensor", "v", kvready)
            for h in range(16):
                pbase, pcol = (h % 2) * 64, (h // 2) * 64
                ctx.op("tensor", lambda e, h=h, pbase=pbase, pcol=pcol: e.matmul(
                    B["pb6"][pbase:pbase + 64, pcol:pcol + 64],
                    B["kp"][:, h * 64:(h + 1) * 64],
                    B["vt"][:, h * 64:(h + 1) * 64],
                    start=True, stop=True), inc="t" if h == 15 else None)
            for h in range(16):
                pbase = 0 if h % 2 == 0 else 64
                pcol = h // 2
                ctx.op("tensor", lambda e, h=h, pbase=pbase, pcol=pcol: e.matmul(
                    B["pb4"][pbase:pbase + 64, pcol:pcol + 1],
                    B["kp"][:, h * 64:(h + 1) * 64],
                    B["ones_col"][:, 0:1],
                    start=True, stop=True), inc="t" if h == 15 else None)
            km = ctx.now("t")
            ctx.wait("vector", "t", km)
            first = (t == 0 and s4 == 0)
            nmm_kv += 1
            if first:
                ctx.op("vector", lambda e: e.tensor_copy(B["kvsb"][:, 0:512], B["pb6"][:, :]))
                ctx.op("vector", lambda e: e.tensor_copy(B["ksum"][:, 0:8], B["pb4"][:, 0:8]), inc="v")
            else:
                ctx.op("vector", lambda e: e.tensor_add(B["kvsb"][:, 0:512], B["kvsb"][:, 0:512], B["pb6"][:, :]))
                ctx.op("vector", lambda e: e.tensor_add(B["ksum"][:, 0:8], B["ksum"][:, 0:8], B["pb4"][:, 0:8]), inc="v")
            kvacc = ctx.now("v")
    ctx.op("vector", lambda e: e.tensor_copy(B["kvpay"][:, 0:512], B["kvsb"][:, 0:512]))
    ctx.op("vector", lambda e: e.tensor_copy(B["kvpay"][:, 512:520], B["ksum"][:, 0:8]), inc="v")
    pay = ctx.now("v")
    ctx.wait("sync", "v", pay)
    emit_exchange(ctx, l, B["kvpay"][:, 0:520], B["kvgot"][:, 0:520], 128 * 520)
    ctx.wait_dma("vector", "xch")
    ctx.op("vector", lambda e: e.tensor_add(B["kvpay"][:, 0:520], B["kvpay"][:, 0:520], B["kvgot"][:, 0:520]), inc="v")
    ksm = ctx.now("v")
    ctx.wait("scalar", "v", ksm)
    ctx.op("scalar", lambda e: e.copy(B["kvb"][:, :], B["kvpay"][:, 0:512]), inc="s")
    ctx.wait("vector", "v", ksm)
    ctx.op("vector", lambda e: e.tensor_copy(B["ksumf"][:, 0:8], B["kvpay"][:, 512:520]), inc="v")
    ctx.wait("sync", "s", ctx.now("s"))
    ctx.wait("sync", "v", ctx.now("v"))


P4A_BUFS = dict(
    sq=([128, C * TT], BF16), xe=([128, C * TT], BF16),
    kp=([128, 1024], BF16), vt=([128, 1024], BF16),
    wkb=([128, C * 1024], BF16), wvb=([128, C * 1024], BF16),
    bkr=([1, 1024], BF16), bvr=([1, 1024], BF16), onesr=([1, 128], BF16),
    ones_col=([128, 1], BF16), ones_rowf=([1, 128], F32),
    ln1s=([128, 8], F32), ln1b=([128, 8], F32),
    mur=([1, TT], F32), rstdr=([1, TT], F32), r1=([1, TT], F32), r2=([1, TT], F32),
    tmp5=([128, TT], F32), tmp6=([128, TT], F32),
    kvpay=([128, 520], F32), kvgot=([128, 520], F32), ksum=([128, 8], F32),
    kvsb=([128, 512], F32),
)


def emit_p4_attB(ctx, l):
    B, Dm = ctx.bufs, ctx.dram
    ctx.dma("sync", B["i16"][:], Dm["i16c"][:], stream="cstB")
    ctx.dma("sync", B["i16T"][:], Dm["i16Tc"][:], stream="cstB")
    ctx.dma("sync", B["boc"][:], Dm["l0_boc"][l], stream="cstB")
    ctx.dma("sync", B["bqc"][:], Dm["l0_bqc"][l], stream="cstB")
    ctx.dma("gpsimd", B["wqb"][:].rearrange("p (c n) -> p c n", c=C),
            Dm[f"l0_Wq{l}"][:].rearrange("(c p) n -> p c n", c=C), stream="wqo")
    ctx.dma("gpsimd", B["wob"][:].rearrange("p (c n) -> p c n", c=C),
            Dm[f"l0_Wo{l}"][:].rearrange("(c p) n -> p c n", c=C), stream="wqo")
    ctx.wait_dma("vector", "cstB")
    for c in range(C):
        ctx.op("vector", lambda e, c=c: e.tensor_scalar_mul(
            B["i16k"][:, c * 16:(c + 1) * 16], B["i16"][:, c * 16:(c + 1) * 16],
            B["ksumf"][:, c:c + 1]), inc="v" if c == C - 1 else None)
    i16k_done = ctx.now("v")
    for t in range(NTILES):
        t0 = t * TT
        ctx.wait("sync", "t", ctx.now("t"))
        ctx.dma("sync", tile_view(B["xe"], TT), fm_dram_tile(Dm["xe_scr"], t0, TT), stream="xeld")
        ctx.wait_dma("tensor", "xeld")
        if t == 0:
            ctx.wait_dma("tensor", "wqo")
            ctx.wait_dma("tensor", "cstB")
            ctx.wait("tensor", "v", i16k_done)
            ctx.wait("tensor", "s", ctx.now("s"))

        def lq(k, j):
            return B["wqb"][:, k * 1024 + j * 128: k * 1024 + (j + 1) * 128]
        def rq(k):
            return ch(B["xe"], k)
        def evq(ctx, ps, j, tm):
            ctx.wait("vector", "t", tm)
            ctx.op("vector", lambda e, ps=ps, j=j: e.tensor_scalar(
                B["tmp5"][:, 0:TT], ps, B["bqc"][:, j:j + 1], 0.0, ALU.add, ALU.min))
            ctx.op("vector", lambda e, ps=ps, j=j: e.tensor_scalar(
                B["tmp6"][:, 0:TT], ps, B["bqc"][:, j:j + 1], 0.0, ALU.add, ALU.max), inc="v")
            vm = ctx.now("v")
            ctx.wait("scalar", "v", vm)
            ctx.op("scalar", lambda e: e.activation(B["tmp5"][:, 0:TT], B["tmp5"][:, 0:TT], AF.Exp), inc="s")
            sm = ctx.now("s")
            ctx.wait("vector", "s", sm)
            ctx.op("vector", lambda e, j=j: e.tensor_add(
                B["qp"][:, j * TT:(j + 1) * TT], B["tmp5"][:, 0:TT], B["tmp6"][:, 0:TT]), inc="v")
            return ("v", ctx.now("v"))
        qsem, qcnt = emit_proj(ctx, C, C, lq, rq, evq, banks=("pb0", "pb1"))
        ctx.wait("tensor", qsem, qcnt)
        ctx.wait("tensor", "s", ctx.now("s"))
        for c in range(C):
            ctx.op("tensor", lambda e, c=c: e.matmul(
                B["pb2"][0:16, 0:TT], B["i16k"][:, c * 16:(c + 1) * 16],
                B["qp"][:, c * TT:(c + 1) * TT], start=(c == 0), stop=(c == C - 1)),
                inc="t" if c == C - 1 else None)
        dm_ = ctx.now("t")
        ctx.wait("vector", "t", dm_)
        ctx.op("vector", lambda e: e.tensor_scalar_add(B["zr"][0:16, :], B["pb2"][0:16, 0:TT], EPS_ATTN))
        ctx.op("vector", lambda e: e.reciprocal(B["zr"][0:16, :], B["zr"][0:16, :]), inc="v")
        zv = ctx.now("v")
        ctx.wait("scalar", "v", zv)
        ctx.op("scalar", lambda e: e.copy(B["zb"][0:16, :], B["zr"][0:16, :]), inc="s")
        zb_done = ctx.now("s")
        ctx.wait("tensor", "s", zb_done)
        for c in range(C):
            ctx.op("tensor", lambda e, c=c: e.matmul(
                B["pb3"][:, 0:TT], B["i16T"][0:16, c * 128:(c + 1) * 128],
                B["zb"][0:16, :], start=True, stop=True), inc="t")
            zm = ctx.now("t")
            ctx.wait("vector", "t", zm)
            ctx.op("vector", lambda e, c=c: e.tensor_mul(
                B["qps"][:, 0:TT], B["qp"][:, c * TT:(c + 1) * TT], B["pb3"][:, 0:TT]), inc="v")
            qs = ctx.now("v")
            ctx.wait("tensor", "v", qs)
            for hh in range(2):
                h = 2 * c + hh
                kb, kcol = (h % 2) * 64, (h // 2) * 64
                ctx.op("tensor", lambda e, hh=hh, h=h, kb=kb, kcol=kcol: e.matmul(
                    B["pb4"][hh * 64:(hh + 1) * 64, 0:TT],
                    B["kvb"][kb:kb + 64, kcol:kcol + 64],
                    B["qps"][hh * 64:(hh + 1) * 64, 0:TT], start=True, stop=True),
                    inc="t" if hh == 1 else None)
            am = ctx.now("t")
            ctx.wait("vector", "t", am)
            ctx.op("vector", lambda e, c=c: e.tensor_copy(
                B["attnb"][:, c * TT:(c + 1) * TT], B["pb4"][:, 0:TT]), inc="v")
        attn_done = ctx.now("v")

        def loo(k, j):
            return B["wob"][:, k * 1024 + j * 128: k * 1024 + (j + 1) * 128]
        def roo(k):
            return B["attnb"][:, k * TT:(k + 1) * TT]
        def evo(ctx, ps, j, tm):
            ctx.wait("vector", "t", tm)
            yv = B["y"][:, j * NT + t0: j * NT + t0 + TT]
            ctx.op("vector", lambda e, ps=ps, yv=yv, j=j: e.scalar_tensor_tensor(
                yv, ps, B["boc"][:, j:j + 1], yv, ALU.add, ALU.add), inc="v")
            return ("v", ctx.now("v"))
        emit_proj(ctx, C, C, loo, roo, evo, banks=("pb5", "pb6"), wait_first=[("v", attn_done)])
    ctx.wait("sync", "v", ctx.now("v"))
    ctx.wait("sync", "s", ctx.now("s"))


P4B_BUFS = dict(
    xe=([128, C * TT], BF16), qp=([128, C * TT], BF16), qps=([128, TT], BF16),
    attnb=([128, C * TT], BF16),
    wqb=([128, C * 1024], BF16), wob=([128, C * 1024], BF16),
    bqc=([128, 8], F32), boc=([128, 8], F32),
    i16=([128, C * 16], BF16), i16k=([128, C * 16], BF16), i16T=([16, C * 128], BF16),
    zr=([16, TT], F32), zr2=([16, TT], F32), zb=([16, TT], BF16), ones16=([16, TT], F32),
    tmp5=([128, TT], F32), tmp6=([128, TT], F32),
)


def emit_p4_ffn(ctx, l):
    B, Dm = ctx.bufs, ctx.dram
    ctx.dma("sync", B["ones_col"][:], Dm["ones_col"][:], stream="cstF")
    ctx.dma("sync", B["ones_rowf"][:], Dm["ones128f"][:], stream="cstF")
    ctx.dma("sync", B["ln2s"][:], Dm["l0_ln2sc"][l], stream="cstF")
    ctx.dma("sync", B["ln2b"][:], Dm["l0_ln2bc"][l], stream="cstF")
    ctx.dma("sync", B["b1c"][:], Dm["l0_b1c"][l], stream="cstF")
    ctx.dma("sync", B["b2c"][:], Dm["l0_b2c"][l], stream="cstF")
    for t in range(NTILES):
        t0 = t * TT
        yv = lambda c: B["y"][:, c * NT + t0: c * NT + t0 + TT]
        ctx.wait("scalar", "t", ctx.now("t"))
        ctx.wait("scalar", "s", ctx.now("s"))
        for c in range(C):
            ctx.op("scalar", lambda e, c=c: e.copy(ch(B["bfX"], c), yv(c)))
        for c in range(C):
            ctx.op("scalar", lambda e, c=c: e.activation(ch(B["sq"], c), yv(c), AF.Square),
                   inc="s" if c == C - 1 else None)
        casts = ctx.now("s")
        rstd_done, _ = _ln_stats_rows(ctx, B["bfX"], B["sq"], B["mur"][0:1, :], B["rstdr"][0:1, :],
                                      [("s", casts), ("q:cstF", 0)])
        bc = _bcast2(ctx, B["mur"][0:1, :], B["rstdr"][0:1, :], rstd_done, ctx.now("v"))
        for c in range(C):
            ctx.op("vector", lambda e, c=c: e.tensor_sub(B["tmp5"][:, 0:TT], yv(c), B["pb2"][:, :]))
            ctx.op("vector", lambda e, c=c: e.tensor_mul(B["tmp5"][:, 0:TT], B["tmp5"][:, 0:TT], B["pb3"][:, :]))
            ctx.op("vector", lambda e, c=c: e.tensor_scalar(
                ch(B["bfX"], c), B["tmp5"][:, 0:TT], B["ln2s"][:, c:c + 1], B["ln2b"][:, c:c + 1],
                ALU.mult, ALU.add), inc="v" if c == C - 1 else None)
        xf_done = ctx.now("v")
        for j2 in range(C):
            ctx.wait("gpsimd", "t", ctx.now("t"))
            ctx.dma("gpsimd", B["w1s"][:].rearrange("p (c n) -> p c n", c=C),
                    Dm[f"l0_W1{l}"][:, j2 * 512:(j2 + 1) * 512].rearrange("(c p) n -> p c n", c=C),
                    stream="w1")

            def lh(k, j, j2=j2):
                return B["w1s"][:, k * 512 + j * 128: k * 512 + (j + 1) * 128]
            def rh(k):
                return ch(B["bfX"], k)
            def evh(ctx, ps, j, tm, j2=j2):
                return _ew_evac(ctx, "scalar", lambda e, ps=ps, j=j: e.activation(
                    B["h1"][:, (j2 * 4 + j) * TT:(j2 * 4 + j + 1) * TT], ps, AF.Relu,
                    bias=B["b1c"][:, j2 * 4 + j:j2 * 4 + j + 1]), tm)
            emit_proj(ctx, 4, C, lh, rh, evh, banks=("pb0", "pb1"),
                      wait_first=[("q:w1", 0), ("v", xf_done)])
        for cc in range(C):
            ctx.wait("gpsimd", "t", ctx.now("t"))
            ctx.dma("gpsimd", B["w2s"][:].rearrange("p (c n) -> p c n", c=32),
                    Dm[f"l0_W2{l}"][:, cc * 128:(cc + 1) * 128].rearrange("(c p) n -> p c n", c=32),
                    stream="w2")

            def lo(k, j, cc=cc):
                return B["w2s"][:, k * 128:(k + 1) * 128]
            def ro(k):
                return B["h1"][:, k * TT:(k + 1) * TT]
            def evo(ctx, ps, j, tm, cc=cc):
                ctx.wait("vector", "t", tm)
                yvv = B["y"][:, cc * NT + t0: cc * NT + t0 + TT]
                ctx.op("vector", lambda e, ps=ps, yvv=yvv, cc=cc: e.scalar_tensor_tensor(
                    yvv, ps, B["b2c"][:, cc:cc + 1], yvv, ALU.add, ALU.add), inc="v")
                return ("v", ctx.now("v"))
            emit_proj(ctx, 1, 32, lo, ro, evo, banks=("pb4", "pb5") if cc % 2 == 0 else ("pb6", "pb3"),
                      wait_first=[("q:w2", 0)])
    ctx.wait("sync", "v", ctx.now("v"))


P4F_BUFS = dict(
    bfX=([128, C * TT], BF16), sq=([128, C * TT], BF16), h1=([128, 32 * TT], BF16),
    w1s=([128, C * 512], BF16), w2s=([128, 32 * 128], BF16),
    ln2s=([128, 8], F32), ln2b=([128, 8], F32), b1c=([128, 32], F32), b2c=([128, 8], F32),
    mur=([1, TT], F32), rstdr=([1, TT], F32), r1=([1, TT], F32), r2=([1, TT], F32),
    tmp5=([128, TT], F32),
    ones_col=([128, 1], BF16), ones_rowf=([1, 128], F32),
)


def emit_p4_fin(ctx):
    B, Dm = ctx.bufs, ctx.dram
    ctx.dma("sync", B["ones_col"][:], Dm["ones_col"][:], stream="cstZ")
    ctx.dma("sync", B["ones_rowf"][:], Dm["ones128f"][:], stream="cstZ")
    ctx.dma("sync", B["lnfs"][:], Dm["l0_lnfsc"][:], stream="cstZ")
    ctx.dma("sync", B["lnfb"][:], Dm["l0_lnfbc"][:], stream="cstZ")
    for t in range(NTILES):
        t0 = t * TT
        yv = lambda c: B["y"][:, c * NT + t0: c * NT + t0 + TT]
        ctx.wait("scalar", "t", ctx.now("t"))
        ctx.wait("scalar", "s", ctx.now("s"))
        for c in range(C):
            ctx.op("scalar", lambda e, c=c: e.copy(ch(B["bfX"], c), yv(c)))
        for c in range(C):
            ctx.op("scalar", lambda e, c=c: e.activation(ch(B["sq"], c), yv(c), AF.Square),
                   inc="s" if c == C - 1 else None)
        casts = ctx.now("s")
        rstd_done, _ = _ln_stats_rows(ctx, B["bfX"], B["sq"], B["mur"][0:1, :], B["rstdr"][0:1, :],
                                      [("s", casts), ("q:cstZ", 0)])
        bc = _bcast2(ctx, B["mur"][0:1, :], B["rstdr"][0:1, :], rstd_done, ctx.now("v"))
        ctx.wait_dma("vector", "sst")
        for c in range(C):
            ctx.op("vector", lambda e, c=c: e.tensor_sub(B["tmp5"][:, 0:TT], yv(c), B["pb2"][:, :]))
            ctx.op("vector", lambda e, c=c: e.tensor_mul(B["tmp5"][:, 0:TT], B["tmp5"][:, 0:TT], B["pb3"][:, :]))
            ctx.op("vector", lambda e, c=c: e.tensor_scalar(
                ch(B["st"], c), B["tmp5"][:, 0:TT], B["lnfs"][:, c:c + 1], B["lnfb"][:, c:c + 1],
                ALU.mult, ALU.add), inc="v" if c == C - 1 else None)
        sdone = ctx.now("v")
        ctx.wait("sync", "v", sdone)
        ctx.dma("sync", fm_dram_tile(Dm["s_scr"], t0, TT), tile_view(B["st"], TT), stream="sst")
    ctx.wait_dma("sync", "sst")


P4Z_BUFS = dict(
    bfX=([128, C * TT], BF16), sq=([128, C * TT], BF16), st=([128, C * TT], F32),
    lnfs=([128, 8], F32), lnfb=([128, 8], F32),
    mur=([1, TT], F32), rstdr=([1, TT], F32), r1=([1, TT], F32), r2=([1, TT], F32),
    tmp5=([128, TT], F32),
    ones_col=([128, 1], BF16), ones_rowf=([1, 128], F32),
)

P4_SHARED = dict(
    y=([128, C * NT], F32), kvb=([128, 512], BF16), ksumf=([128, 8], F32),
)


def launch_c_phases():
    ph = [(emit_xch_init, None), (emit_p4_init, P4I_BUFS)]
    for l in range(4):
        ph.append((emit_p4_attA, P4A_BUFS, dict(l=l)))
        ph.append((emit_p4_attB, P4B_BUFS, dict(l=l)))
        ph.append((emit_p4_ffn, P4F_BUFS, dict(l=l)))
    ph.append((emit_p4_fin, P4Z_BUFS))
    return ph


def launch_c_drams():
    d = dict(
        x_fm=([D, NT], F32, "in"),
        posemb=([D, NT], F32, "in"),
        murow_d=([1, NT], F32, "in"),
        rstdrow_d=([1, NT], F32, "in"),
        ones_col=([128, 1], BF16, "in"),
        ones128=([1, 128], BF16, "in"),
        ones128f=([1, 128], F32, "in"),
        i16c=([128, C * 16], BF16, "in"),
        i16Tc=([16, C * 128], BF16, "in"),
        l0_bk=([4, D], F32, "in"),
        l0_bv=([4, D], F32, "in"),
        l0_bqc=([4, 128, 8], F32, "in"),
        l0_boc=([4, 128, 8], F32, "in"),
        l0_b1c=([4, 128, 32], F32, "in"),
        l0_b2c=([4, 128, 8], F32, "in"),
        l0_ln1sc=([4, 128, 8], F32, "in"),
        l0_ln1bc=([4, 128, 8], F32, "in"),
        l0_ln2sc=([4, 128, 8], F32, "in"),
        l0_ln2bc=([4, 128, 8], F32, "in"),
        l0_lnfsc=([128, 8], F32, "in"),
        l0_lnfbc=([128, 8], F32, "in"),
        xe_scr=([D, NT], BF16, "internal"),
        s_scr=([D, NT], F32, "out"),
    )
    for l in range(4):
        for nm in ("Wq", "Wk", "Wv", "Wo"):
            d[f"l0_{nm}{l}"] = ([D, D], F32, "in")
        d[f"l0_W1{l}"] = ([D, 4 * D], F32, "in")
        d[f"l0_W2{l}"] = ([4 * D, D], F32, "in")
    d.update(exchange_drams())
    return d


NTE = NT + 512


def emit_p5(ctx):
    B, Dm = ctx.bufs, ctx.dram
    for t in range(NTILES):
        t0 = t * TT
        ctx.wait("sync", "v", ctx.now("v"))
        ctx.dma("sync", tile_view(B["xt"], TT), fm_dram_tile(Dm["x_fm"], t0, TT), stream="xt5")
        ctx.dma("sync", tile_view(B["st"], TT), fm_dram_tile(Dm["s_scr"], t0, TT), stream="xt5")
        ctx.dma("sync", tile_view(B["gt"], TT), fm_dram_tile(Dm["g_scr"], t0, TT), stream="xt5")
        ctx.wait_dma("vector", "xt5")
        for c in range(C):
            ctx.op("vector", lambda e, c=c: e.tensor_mul(ch(B["st"], c), ch(B["st"], c), ch(B["gt"], c)))
            ctx.op("vector", lambda e, c=c: e.tensor_add(ch(B["st"], c), ch(B["st"], c), ch(B["xt"], c)),
                   inc="v" if c == C - 1 else None)
        x1done = ctx.now("v")
        ctx.wait("sync", "v", x1done)
        ctx.dma("sync", fm_dram_tile(Dm["x1e"], 256 + t0, TT), tile_view(B["st"], TT), stream="x1st")
    ctx.wait_dma("sync", "x1st")
    ctx.dma("sync", B["hx"][:, 0:2048].rearrange("p (c t) -> p c t", c=C),
            fm_dram_tile(Dm["x1e"], 256, 256), stream="hld")
    ctx.dma("sync", B["hx"][:, 2048:4096].rearrange("p (c t) -> p c t", c=C),
            fm_dram_tile(Dm["x1e"], 256 + NT - 256, 256), stream="hld")
    ctx.wait_dma("sync", "hld")
    emit_exchange(ctx, 4, B["hx"][:, 0:4096], B["hg"][:, 0:4096], 128 * 4096)
    ctx.wait_dma("gpsimd", "xch")
    ctx.op("gpsimd", lambda e: e.memset(B["hz"][:], 0.0), inc="g")
    zs = ctx.now("g")
    ctx.wait("vector", "g", zs)
    ctx.wait_dma("vector", "xch")
    ctx.op("vector", lambda e: e.tensor_copy(B["hm"][:, 0:2048], B["hz"][:]), inc="v")
    hm = ctx.now("v")
    ctx.wait("sync", "v", hm)
    ctx.dma("sync", Dm["hscr"][0].rearrange("(c p) t -> p c t", c=C), B["hm"][:, 0:2048].rearrange("p (c t) -> p c t", c=C), stream="hst")
    ctx.dma("sync", Dm["hscr"][1].rearrange("(c p) t -> p c t", c=C), B["hg"][:, 0:2048].rearrange("p (c t) -> p c t", c=C), stream="hst")
    ctx.dma("sync", Dm["hscr"][2].rearrange("(c p) t -> p c t", c=C), B["hg"][:, 2048:4096].rearrange("p (c t) -> p c t", c=C), stream="hst")
    ctx.wait_dma("sync", "hst")

    S_ = D * 256
    sname = ctx._stream("x1st")
    sem = ctx.sems[sname]

    def margins(e):
        R = ctx.xregs
        hscr_t = ctx.dram["hscr"].tensor
        for br in e.Switch(ctx.side_sv, 2):
            e.dma_start(out=fm_dram_tile(ctx.dram["x1e"], 0, 256),
                        in_=bass.AP(hscr_t, 2 * br * S_, [[256, 128], [256 * 128, 8], [1, 256]])
                        ).then_inc(sem, 16)
            e.dma_start(out=fm_dram_tile(ctx.dram["x1e"], NTE - 256, 256),
                        in_=bass.AP(hscr_t, (1 - br) * S_, [[256, 128], [256 * 128, 8], [1, 256]])
                        ).then_inc(sem, 16)
    ctx.raw("sync", margins)
    ctx.counts[sname] += 32
    ctx.wait_dma("sync", "x1st")


P5_BUFS = dict(
    xt=([128, C * TT], F32), st=([128, C * TT], F32), gt=([128, C * TT], F32),
    hx=([128, 4096], F32), hg=([128, 4096], F32), hz=([128, 2048], F32),
    hm=([128, 2048], F32),
)


def fm_dram_tile_e(d, t0, L, nch=C):
    return d[:, t0:t0 + L].rearrange("(c p) t -> p c t", c=nch)


def emit_p6(ctx):
    B, Dm = ctx.bufs, ctx.dram
    ctx.dma("sync", tile_view(B["ctf"], 8), Dm["f1_cT"][:, :].rearrange("(c p) e -> p c e", c=C), stream="cst6")
    ctx.dma("sync", B["ones_col"][:], Dm["ones_col"][:], stream="cst6")
    ctx.dma("sync", B["ones_rowf"][:], Dm["ones128f"][:], stream="cst6")
    NTILE_E = NTE // TT
    for t in range(NTILE_E):
        t0 = t * TT
        ctx.wait("sync", "t", ctx.now("t"))
        ctx.wait("sync", "v", ctx.now("v"))
        ctx.dma("sync", tile_view(B["xt"], TT), Dm["x1e"][:, t0:t0 + TT].rearrange("(c p) t -> p c t", c=C), stream="xt6")
        ctx.wait_dma("scalar", "xt6")
        ctx.wait("scalar", "v", ctx.now("v"))
        ctx.wait("scalar", "t", ctx.now("t"))
        ctx.op("scalar", lambda e: e.copy(B["xb"][:], B["xt"][:]))
        ctx.op("scalar", lambda e: e.activation(B["sq"][:], B["xt"][:], AF.Square), inc="s")
        sq_done = ctx.now("s")
        rstd_done, _ = _ln_stats_rows(ctx, B["xb"], B["sq"], B["murowB"][0:1, t0:t0 + TT],
                                      B["rstdrowB"][0:1, t0:t0 + TT], [("s", sq_done), ("q:cst6", 0)])
        bc = _bcast2(ctx, B["murowB"][0:1, t0:t0 + TT], B["rstdrowB"][0:1, t0:t0 + TT], rstd_done, ctx.now("v"))
        for c in range(C):
            ctx.op("vector", lambda e, c=c: e.tensor_sub(ch(B["t1"], c), ch(B["xt"], c), B["pb2"][:, :]))
            ctx.op("vector", lambda e, c=c: e.tensor_mul(ch(B["t1"], c), ch(B["t1"], c), B["pb3"][:, :]),
                   inc="v" if c == C - 1 else None)
        xn_done = ctx.now("v")
        ctx.wait("scalar", "v", xn_done)
        ctx.wait_dma("scalar", "sx6")
        ctx.op("scalar", lambda e: e.copy(B["xnb"][:], B["t1"][:]), inc="s")
        xnb_done = ctx.now("s")
        ctx.wait("sync", "s", xnb_done)
        ctx.dma("sync", fm_dram_tile_e(Dm["xn1b_e"], t0, TT), tile_view(B["xnb"], TT), stream="sx6")
    for t in range(NTILES):
        t0 = t * TT
        e0 = 256 + t0
        ctx.wait("sync", "t", ctx.now("t"))
        ctx.wait("sync", "v", ctx.now("v"))
        ctx.wait("sync", "s", ctx.now("s"))
        ctx.dma("sync", tile_view(B["xt"], TT), Dm["x1e"][:, e0:e0 + TT].rearrange("(c p) t -> p c t", c=C), stream="xt6")
        ctx.wait_dma("tensor", "xt6")
        ctx.wait("tensor", "v", ctx.now("v"))
        ctx.wait("tensor", "s", ctx.now("s"))
        ctx.op("tensor", lambda e, e0=e0: e.matmul(B["pb2"][:, :], B["ones_rowf"][:], B["murowB"][0:1, e0:e0 + TT], start=True, stop=True))
        ctx.op("tensor", lambda e, e0=e0: e.matmul(B["pb3"][:, :], B["ones_rowf"][:], B["rstdrowB"][0:1, e0:e0 + TT], start=True, stop=True), inc="t")
        bc = ctx.now("t")
        ctx.wait("vector", "t", bc)
        for c in range(C):
            ctx.op("vector", lambda e, c=c: e.tensor_sub(ch(B["t1"], c), ch(B["xt"], c), B["pb2"][:, :]))
            ctx.op("vector", lambda e, c=c: e.tensor_mul(ch(B["t1"], c), ch(B["t1"], c), B["pb3"][:, :]),
                   inc="v" if c == C - 1 else None)
        xn_done = ctx.now("v")
        ctx.wait("scalar", "v", xn_done)
        ctx.op("scalar", lambda e: e.copy(B["xb"][:], B["t1"][:]))
        ctx.op("scalar", lambda e: e.activation(B["sq"][:], B["t1"][:], AF.Square), inc="s")
        sq2 = ctx.now("s")
        rstd2, _ = _ln_stats_rows(ctx, B["xb"], B["sq"], B["mu2r"][0:1, :], B["rstd2r"][0:1, :],
                                  [("s", sq2), ("v", xn_done)])
        bc2 = _bcast2(ctx, B["mu2r"][0:1, :], B["rstd2r"][0:1, :], rstd2, xn_done)
        for c in range(C):
            ctx.op("vector", lambda e, c=c: e.tensor_sub(ch(B["t2"], c), ch(B["t1"], c), B["pb2"][:, :]))
            ctx.op("vector", lambda e, c=c: e.tensor_mul(ch(B["t2"], c), ch(B["t2"], c), B["pb3"][:, :]),
                   inc="v" if c == C - 1 else None)
        x2done = ctx.now("v")
        ctx.wait("scalar", "v", x2done)
        ctx.wait_dma("scalar", "sx7")
        ctx.op("scalar", lambda e: e.copy(B["xh2b"][:], B["t2"][:]), inc="s")
        xh2c = ctx.now("s")
        ctx.wait("sync", "s", xh2c)
        ctx.dma("sync", fm_dram_tile(Dm["xhat2b_d"], t0, TT), tile_view(B["xh2b"], TT), stream="sx7")
        ctx.wait("tensor", "v", xn_done)
        for c in range(C):
            ctx.op("tensor", lambda e, c=c: e.matmul(
                B["pb4"][0:8, :], B["ctf"][:, c * 8:(c + 1) * 8], ch(B["t1"], c),
                start=(c == 0), stop=(c == C - 1)), inc="t" if c == C - 1 else None)
        am = ctx.now("t")
        ctx.wait("vector", "t", am)
        L32, M32, R32 = B["L32"], B["M32"], B["R32"]
        ctx.op("vector", lambda e: e.tensor_copy(L32[0:8, :], B["pb4"][0:8, :]))
        ctx.op("vector", lambda e: e.tensor_copy(M32[0:8, :], L32[0:8, :]))
        for r in (4, 2, 1):
            ctx.op("vector", lambda e, r=r: e.stream_shuffle(R32[:, :], M32[:, :], mask=[(i + r) % 32 for i in range(32)]))
            ctx.op("vector", lambda e: e.tensor_max(M32[:, :], M32[:, :], R32[:, :]))
        ctx.op("vector", lambda e: e.stream_shuffle(R32[:, :], M32[:, :], mask=[0] * 32))
        ctx.op("vector", lambda e, t0=t0: e.tensor_tensor(
            B["masks"][0:8, t0:t0 + TT], L32[0:8, :], R32[0:8, :], ALU.is_equal), inc="v")
        mk = ctx.now("v")
        ctx.wait("scalar", "v", mk)
        ctx.op("scalar", lambda e, t0=t0: e.activation(
            B["alphar"][0:1, t0:t0 + TT], M32[0:1, :], AF.Sigmoid), inc="s")
    fin_s = ctx.now("s")
    ctx.wait("scalar", "s", fin_s)
    ctx.op("scalar", lambda e: e.copy(B["masksb"][0:8, :], B["masks"][0:8, :]), inc="s")
    ctx.wait("sync", "s", ctx.now("s"))
    ctx.wait("sync", "v", ctx.now("v"))
    ctx.dma("sync", Dm["murowB_d"][:], B["murowB"][0:1, :], stream="so6")
    ctx.dma("sync", Dm["rstdrowB_d"][:], B["rstdrowB"][0:1, :], stream="so6")
    ctx.dma("sync", Dm["masks_d"][:], B["masks"][0:8, :], stream="so6")
    ctx.dma("sync", Dm["masksb_d"][:], B["masksb"][0:8, :], stream="so6")
    ctx.dma("sync", Dm["alphar_d"][:], B["alphar"][0:1, :], stream="so6")
    ctx.wait_dma("sync", "so6")


P6_BUFS = dict(
    xt=([128, C * TT], F32), t1=([128, C * TT], F32), t2=([128, C * TT], F32),
    xb=([128, C * TT], BF16), sq=([128, C * TT], BF16),
    xnb=([128, C * TT], BF16), xh2b=([128, C * TT], BF16),
    ctf=([128, C * 8], F32),
    murowB=([1, NTE], F32), rstdrowB=([1, NTE], F32),
    mu2r=([1, TT], F32), rstd2r=([1, TT], F32),
    masks=([8, NT], F32), masksb=([8, NT], BF16), alphar=([1, NT], F32),
    r1=([1, TT], F32), r2=([1, TT], F32),
    L32=([32, TT], F32), M32=([32, TT], F32), R32=([32, TT], F32),
    ones_col=([128, 1], BF16), ones_rowf=([1, 128], F32),
)


TT7 = 1024
NTILES7 = NT // TT7


def emit_p7(ctx):
    B, Dm = ctx.bufs, ctx.dram
    ctx.dma("sync", B["ind8"][:], Dm["ind8"][:], stream="cst7")
    ctx.dma("sync", B["masksb"][0:8, :], Dm["masksb_d"][:], stream="cst7")
    ctx.dma("sync", B["ones_rowf"][:], Dm["ones128f"][:], stream="cst7")
    ctx.dma("gpsimd", B["b2b"][:], Dm["f1_b2"][:], stream="cst7g")
    ctx.dma("sync", B["lngs"][:], Dm["f1_lngs"][:], stream="cst7")
    ctx.dma("sync", B["lnbs"][:], Dm["f1_lnbs"][:], stream="cst7")
    ctx.dma("sync", B["b1s"][:], Dm["f1_b1s"][:], stream="cst7")
    for t in range(NTILES7):
        t0 = t * TT7
        ctx.wait("sync", "t", ctx.now("t"))
        ctx.wait("sync", "v", ctx.now("v"))
        ctx.wait("sync", "s", ctx.now("s"))
        ctx.dma("sync", tile_view(B["xh2"], TT7), fm_dram_tile(Dm["xhat2b_d"], t0, TT7), stream="x7")
        ctx.dma("sync", B["alr"][0:1, 0:TT7], Dm["alphar_d"][0:1, t0:t0 + TT7], stream="x7")
        for e_i in range(8):
            if e_i > 0:
                ctx.wait("vector", "t", ctx.now("t"))
            else:
                ctx.wait("vector", "s", ctx.now("s"))
            ctx.wait_dma("vector", "x7")
            ctx.wait_dma("vector", "cst7")
            for c in range(C):
                ctx.op("vector", lambda e, c=c, e_i=e_i: e.tensor_scalar(
                    ch(B["xhe"], c, TT7), ch(B["xh2"], c, TT7),
                    B["lngs"][:, e_i * 8 + c:e_i * 8 + c + 1], B["lnbs"][:, e_i * 8 + c:e_i * 8 + c + 1],
                    ALU.mult, ALU.add), inc="v" if c == C - 1 else None)
            xe_done = ctx.now("v")
            ctx.wait("tensor", "v", xe_done)
            for hf in range(2):
                ctx.op("tensor", lambda e, hf=hf, e_i=e_i, t0=t0: e.matmul(
                    B["pb5" if hf == 0 else "pb6"][:, 0:512],
                    B["ind8"][:, e_i * 128:(e_i + 1) * 128],
                    B["masksb"][0:8, t0 + hf * 512: t0 + (hf + 1) * 512],
                    start=True, stop=True), inc="t" if hf == 1 else None)
            mb = ctx.now("t")
            ctx.wait("vector", "t", mb)
            ctx.op("vector", lambda e: e.tensor_copy(B["mbs"][:, 0:512], B["pb5"][:, 0:512]))
            ctx.op("vector", lambda e: e.tensor_copy(B["mbs"][:, 512:1024], B["pb6"][:, 0:512]), inc="v")
            mbv = ctx.now("v")
            for j2 in range(8):
                ctx.wait("gpsimd", "t", ctx.now("t"))
                ctx.dma("gpsimd", B["ws7"][:].rearrange("p (c n) -> p c n", c=C),
                        Dm["f1_w1"][e_i][:, j2 * 512:(j2 + 1) * 512].rearrange("(c p) n -> p c n", c=C),
                        stream="w17")
                def lh(k, j, j2=j2):
                    return B["ws7"][:, k * 512 + j * 128: k * 512 + (j + 1) * 128]
                for half in range(2):
                    def rh(k, half=half):
                        return ch(B["xhe"], k, TT7)[:, half * 512:(half + 1) * 512]
                    def evh(ctx, ps, j, tm, j2=j2, half=half, e_i=e_i):
                        return _ew_evac(ctx, "scalar", lambda e, ps=ps, j=j: e.activation(
                            B["h1"][:, (j2 * 4 + j) * TT7 + half * 512:(j2 * 4 + j) * TT7 + (half + 1) * 512],
                            ps, AF.Relu, bias=B["b1s"][:, e_i * 32 + j2 * 4 + j:e_i * 32 + j2 * 4 + j + 1]), tm)
                    emit_proj(ctx, 4, C, lh, rh, evh, banks=("pb0", "pb1"), wait_first=[("q:w17", 0)])
            for cc in range(C):
                ctx.wait("gpsimd", "t", ctx.now("t"))
                ctx.dma("gpsimd", B["ws7"][:].rearrange("p (c n) -> p c n", c=32),
                        Dm["f1_w2"][e_i][:, cc * 128:(cc + 1) * 128].rearrange("(c p) n -> p c n", c=32),
                        stream="w27")
                for half in range(2):
                    def lo(k, j, cc=cc):
                        return B["ws7"][:, k * 128:(k + 1) * 128]
                    def ro(k, half=half):
                        return B["h1"][:, k * TT7 + half * 512:k * TT7 + (half + 1) * 512]
                    def evo(ctx, ps, j, tm, cc=cc, half=half, e_i=e_i):
                        ctx.wait("vector", "t", tm)
                        msl = B["mbs"][:, half * 512:(half + 1) * 512]
                        dst = ch(B["ft"], cc, TT7)[:, half * 512:(half + 1) * 512]
                        if e_i == 0:
                            ctx.op("vector", lambda e, ps=ps, dst=dst, msl=msl: e.tensor_mul(
                                dst, ps, msl), inc="v")
                        else:
                            ctx.op("vector", lambda e, ps=ps, msl=msl: e.tensor_mul(
                                B["tmp5"][:, 0:512], ps, msl))
                            ctx.op("vector", lambda e, dst=dst: e.tensor_add(
                                dst, dst, B["tmp5"][:, 0:512]), inc="v")
                        return ("v", ctx.now("v"))
                    emit_proj(ctx, 1, 32, lo, ro, evo, banks=("pb2", "pb3") if half == 0 else ("pb4", "pb3"),
                              wait_first=[("q:w27", 0), ("v", mbv)])
        ctx.dma("sync", tile_view(B["x1t"], TT7), Dm["x1e"][:, 256 + t0:256 + t0 + TT7].rearrange("(c p) t -> p c t", c=C), stream="x7b")
        ctx.dma("sync", B["murt"][0:1, 0:TT7], Dm["murowB_d"][0:1, 256 + t0:256 + t0 + TT7], stream="x7b")
        ctx.dma("sync", B["rstdt"][0:1, 0:TT7], Dm["rstdrowB_d"][0:1, 256 + t0:256 + t0 + TT7], stream="x7b")
        ctx.wait("tensor", "v", ctx.now("v"))
        ctx.wait("tensor", "s", ctx.now("s"))
        ctx.wait_dma("tensor", "cst7g")
        for cc in range(C):
            for half in range(2):
                ctx.wait("tensor", "v", ctx.now("v"))
                ctx.op("tensor", lambda e, cc=cc, half=half, t0=t0: e.matmul(
                    B["pb2"][:, 0:512], B["b2b"][0:8, cc * 128:(cc + 1) * 128],
                    B["masksb"][0:8, t0 + half * 512:t0 + (half + 1) * 512],
                    start=True, stop=True), inc="t")
                b2m = ctx.now("t")
                ctx.wait("vector", "t", b2m)
                dst = ch(B["ft"], cc, TT7)[:, half * 512:(half + 1) * 512]
                ctx.op("vector", lambda e, dst=dst: e.tensor_add(dst, dst, B["pb2"][:, 0:512]), inc="v")
        fb2 = ctx.now("v")
        ctx.wait_dma("tensor", "x7b")
        ctx.wait("tensor", "v", fb2)
        ctx.wait("tensor", "s", ctx.now("s"))
        for half in range(2):
            ctx.op("tensor", lambda e, half=half: e.matmul(
                B["pb5" if half == 0 else "pb6"][:, 0:512], B["ones_rowf"][:],
                B["alr"][0:1, half * 512:(half + 1) * 512], start=True, stop=True))
        ctx.op("tensor", lambda e: e.matmul(B["pb3"][:, 0:512], B["ones_rowf"][:], B["murt"][0:1, 0:512], start=True, stop=True))
        ctx.op("tensor", lambda e: e.matmul(B["pb4"][:, 0:512], B["ones_rowf"][:], B["rstdt"][0:1, 0:512], start=True, stop=True))
        ctx.op("tensor", lambda e: e.matmul(B["pb0"][:, 0:512], B["ones_rowf"][:], B["murt"][0:1, 512:1024], start=True, stop=True))
        ctx.op("tensor", lambda e: e.matmul(B["pb1"][:, 0:512], B["ones_rowf"][:], B["rstdt"][0:1, 512:1024], start=True, stop=True), inc="t")
        bcx = ctx.now("t")
        ctx.wait("vector", "t", bcx)
        for c in range(C):
            for half in range(2):
                mu_b = B["pb3"] if half == 0 else B["pb0"]
                rs_b = B["pb4"] if half == 0 else B["pb1"]
                al_b = B["pb5"] if half == 0 else B["pb6"]
                x1s = ch(B["x1t"], c, TT7)[:, half * 512:(half + 1) * 512]
                fs = ch(B["ft"], c, TT7)[:, half * 512:(half + 1) * 512]
                ctx.op("vector", lambda e, x1s=x1s, mu_b=mu_b: e.tensor_sub(B["tmp5"][:, 0:512], x1s, mu_b[:, 0:512]))
                ctx.op("vector", lambda e, rs_b=rs_b: e.tensor_mul(B["tmp5"][:, 0:512], B["tmp5"][:, 0:512], rs_b[:, 0:512]))
                ctx.op("vector", lambda e, fs=fs, al_b=al_b: e.tensor_mul(fs, fs, al_b[:, 0:512]))
                ctx.op("vector", lambda e, fs=fs: e.tensor_add(fs, fs, B["tmp5"][:, 0:512]),
                       inc="v" if (c == C - 1 and half == 1) else None)
        pre = ctx.now("v")
        ctx.wait("scalar", "v", pre)
        ctx.wait_dma("scalar", "g2st")
        ctx.op("scalar", lambda e: e.activation(B["ft"][:, 0:C * TT7], B["ft"][:, 0:C * TT7], AF.Silu), inc="s")
        g2d = ctx.now("s")
        ctx.wait("sync", "s", g2d)
        ctx.dma("sync", fm_dram_tile(Dm["g2_scr"], t0, TT7), tile_view(B["ft"], TT7), stream="g2st")
    ctx.wait_dma("sync", "g2st")


def ch(buf, c, L=TT):
    return buf[:, c * L:(c + 1) * L]


P7_BUFS = dict(
    xh2=([128, C * TT7], BF16), xhe=([128, C * TT7], BF16),
    h1=([128, 32 * TT7], BF16), ws7=([128, 4096], BF16),
    ft=([128, C * TT7], F32), x1t=([128, C * TT7], F32),
    masksb=([8, NT], BF16), ind8=([8, 8 * 128], BF16), b2b=([8, 1024], BF16),
    lngs=([128, 64], F32), lnbs=([128, 64], F32), b1s=([128, 8 * 32], F32),
    alr=([1, TT7], F32), murt=([1, TT7], F32), rstdt=([1, TT7], F32),
    tmp5=([128, 512], F32), mbs=([128, 1024], F32), ones_rowf=([1, 128], F32),
)


DBG_P8 = False


def emit_p8a(ctx):
    B, Dm = ctx.bufs, ctx.dram
    for nm, dn in (("wqb", "l1_Wq"), ("wkb", "l1_Wk"), ("wvb", "l1_Wv"), ("wob", "l1_Wo")):
        ctx.dma("gpsimd", B[nm][:].rearrange("p (c n) -> p c n", c=C),
                Dm[dn][:].rearrange("(c p) n -> p c n", c=C), stream="w8")
    ctx.dma("gpsimd", B["bvr"][:], Dm["l1_bvr"][:], stream="w8")
    ctx.dma("gpsimd", B["onesr"][:], Dm["ones128"][:], stream="w8")
    ctx.dma("sync", B["bqc"][:], Dm["l1_bqc"][:], stream="w8s")
    ctx.dma("sync", B["bkc"][:], Dm["l1_bkc"][:], stream="w8s")
    ctx.dma("sync", B["boc"][:], Dm["l1_boc"][:], stream="w8s")
    ctx.dma("sync", B["idt_b"][:], Dm["idt"][:], stream="w8s")
    for t in range(NTILES):
        t0 = t * TT
        ctx.wait("sync", "t", ctx.now("t"))
        ctx.wait("sync", "v", ctx.now("v"))
        ctx.dma("sync", tile_view(B["xkv"], 1024), fm_dram_tile_e(Dm["xn1b_e"], t0, 1024), stream="x8")
        ctx.dma("sync", B["mask"][:], Dm["maskt_d"][t], stream="x8")
        ctx.wait_dma("tensor", "x8")
        if t == 0:
            ctx.wait_dma("tensor", "w8")

        def lk(k, j):
            cc, half = j // 2, j % 2
            return B["wkb"][:, k * 1024 + cc * 128: k * 1024 + (cc + 1) * 128]
        def rk(k, _t=t):
            return None
        prevv = None
        for j in range(16):
            cc, half = j // 2, j % 2
            bank = B["pb0"] if j % 2 == 0 else B["pb1"]
            if j >= 2:
                ctx.wait("tensor", "v", ctx.now("v"))
            for k in range(C):
                ctx.op("tensor", lambda e, k=k, cc=cc, half=half, bank=bank: e.matmul(
                    bank[:, 0:512], B["wkb"][:, k * 1024 + cc * 128: k * 1024 + (cc + 1) * 128],
                    ch(B["xkv"], k, 1024)[:, half * 512:(half + 1) * 512],
                    start=(k == 0), stop=(k == C - 1)), inc="t" if k == C - 1 else None)
            tm = ctx.now("t")
            ctx.wait("vector", "t", tm)
            ctx.op("vector", lambda e, cc=cc, half=half, bank=bank: e.tensor_scalar_add(
                B["kf"][:, cc * 1024 + half * 512: cc * 1024 + (half + 1) * 512],
                bank[:, 0:512], B["bkc"][:, cc:cc + 1]), inc="v")
        for j in range(16):
            sb, half = j // 2, j % 2
            bank = B["pb0"] if j % 2 == 0 else B["pb1"]
            ctx.wait("tensor", "v", ctx.now("v"))
            ctx.op("tensor", lambda e, half=half, bank=bank: e.matmul(
                bank[:, 0:512], B["onesr"][:], B["bvr"][0:1, half * 512:(half + 1) * 512],
                start=True, stop=False))
            for k in range(C):
                ctx.op("tensor", lambda e, k=k, sb=sb, half=half, bank=bank: e.matmul(
                    bank[:, 0:512], ch(B["xkv"], k, 1024)[:, sb * 128:(sb + 1) * 128],
                    B["wvb"][:, k * 1024 + half * 512: k * 1024 + (half + 1) * 512],
                    start=False, stop=(k == C - 1)), inc="t" if k == C - 1 else None)
            tm = ctx.now("t")
            ctx.wait("vector", "t", tm)
            ctx.op("vector", lambda e, sb=sb, half=half, bank=bank: e.tensor_copy(
                B["vt"][:, sb * 1024 + half * 512: sb * 1024 + (half + 1) * 512], bank[:, 0:512]), inc="v")
        for cc in range(C):
            bank = B["pb0"] if cc % 2 == 0 else B["pb1"]
            ctx.wait("tensor", "v", ctx.now("v"))
            for k in range(C):
                ctx.op("tensor", lambda e, k=k, cc=cc, bank=bank: e.matmul(
                    bank[:, 0:512], B["wqb"][:, k * 1024 + cc * 128: k * 1024 + (cc + 1) * 128],
                    ch(B["xkv"], k, 1024)[:, 256:768],
                    start=(k == 0), stop=(k == C - 1)), inc="t" if k == C - 1 else None)
            tm = ctx.now("t")
            ctx.wait("vector", "t", tm)
            ctx.op("vector", lambda e, cc=cc, bank=bank: e.tensor_scalar(
                B["qf"][:, cc * 512:(cc + 1) * 512], bank[:, 0:512],
                B["bqc"][:, cc:cc + 1], 0.125, ALU.add, ALU.mult), inc="v")
        qv = ctx.now("v")
        for qs in range(4):
            for cc in range(C):
                for hh in range(2):
                    h = 2 * cc + hh
                    hb = hh * 64
                    ctx.wait("tensor", "v", ctx.now("v"))
                    ctx.wait("tensor", "s", ctx.now("s"))
                    ctx.op("tensor", lambda e, hb=hb, cc=cc, qs=qs: e.matmul(
                        B["pb0"][:, 0:512],
                        B["qf"][hb:hb + 64, cc * 512 + qs * 128: cc * 512 + (qs + 1) * 128],
                        B["kf"][hb:hb + 64, cc * 1024 + qs * 128: cc * 1024 + qs * 128 + 512],
                        start=True, stop=True))
                    ctx.op("tensor", lambda e, hb=hb, cc=cc, qs=qs: e.matmul(
                        B["pb1"][:, 0:128],
                        B["qf"][hb:hb + 64, cc * 512 + qs * 128: cc * 512 + (qs + 1) * 128],
                        B["kf"][hb:hb + 64, cc * 1024 + qs * 128 + 512: cc * 1024 + qs * 128 + 640],
                        start=True, stop=True), inc="t")
                    sm_ = ctx.now("t")
                    ctx.wait("vector", "t", sm_)
                    ctx.op("vector", lambda e, qs=qs: e.tensor_add(
                        B["sc"][:, 0:512], B["pb0"][:, 0:512], B["mask"][:, qs * 640: qs * 640 + 512]))
                    ctx.op("vector", lambda e, qs=qs: e.tensor_add(
                        B["sc"][:, 512:640], B["pb1"][:, 0:128], B["mask"][:, qs * 640 + 512: (qs + 1) * 640]))
                    ctx.op("vector", lambda e: e.reduce_max(
                        B["rmax"][:, 0:1], B["sc"][:, 0:640], axis=AX, negate=True), inc="v")
                    vm = ctx.now("v")
                    ctx.wait("scalar", "v", vm)
                    ctx.op("scalar", lambda e: e.activation(
                        B["pnf"][:, 0:640], B["sc"][:, 0:640], AF.Exp, bias=B["rmax"][:, 0:1],
                        accum_out=B["rsum"][:, 0:1]), inc="s")
                    em = ctx.now("s")
                    ctx.wait("vector", "s", em)
                    ctx.op("vector", lambda e: e.reciprocal(B["rz"][:, 0:1], B["rsum"][:, 0:1]), inc="v")
                    rzv = ctx.now("v")
                    ctx.wait("gpsimd", "v", rzv)
                    ctx.op("gpsimd", lambda e: e.tensor_scalar_mul(
                        B["pn"][:, 0:640], B["pnf"][:, 0:640], B["rz"][:, 0:1]), inc="g")
                    pg = ctx.now("g")
                    ctx.wait("tensor", "g", pg)
                    ctx.wait("tensor", "v", ctx.now("v"))
                    for j in range(5):
                        ctx.op("tensor", lambda e, j=j: e.transpose(
                            B["pt"][:, j * 128:(j + 1) * 128], B["pn"][:, j * 128:(j + 1) * 128],
                            B["idt_b"][:]), inc="t" if j == 4 else None)
                    ttm = ctx.now("t")
                    ctx.wait("vector", "t", ttm)
                    ctx.op("vector", lambda e: e.tensor_copy(B["pTs"][:, 0:640], B["pt"][:, 0:640]), inc="v")
                    ptv = ctx.now("v")
                    ctx.wait("tensor", "v", ptv)
                    for j in range(5):
                        ctx.op("tensor", lambda e, j=j, qs=qs, h=h, hb=hb: e.matmul(
                            B["pb2"][hb:hb + 64, 0:128],
                            B["vt"][:, (qs + j) * 1024 + h * 64: (qs + j) * 1024 + (h + 1) * 64],
                            B["pTs"][:, j * 128:(j + 1) * 128],
                            start=(j == 0), stop=(j == 4)), inc="t" if j == 4 else None)
                    avm = ctx.now("t")
                    if hh == 1:
                        ctx.wait("vector", "t", avm)
                        ctx.op("vector", lambda e, cc=cc, qs=qs: e.tensor_copy(
                            B["afm"][:, cc * 512 + qs * 128: cc * 512 + (qs + 1) * 128],
                            B["pb2"][:, 0:128]), inc="v")
                    if DBG_P8 and t == NTILES - 1 and qs == 3 and cc == 7 and hh == 1:
                        ctx.wait("sync", "v", ctx.now("v"))
                        ctx.wait("sync", "s", ctx.now("s"))
                        ctx.dma("sync", Dm["sc_dbg"][:], B["sc"][:, 0:640], stream="ast")
                        ctx.dma("sync", Dm["pn_dbg"][:], B["pn"][:, 0:640], stream="ast")
                        ctx.dma("sync", Dm["pts_dbg"][:], B["pTs"][:, 0:640], stream="ast")
                        ctx.dma("sync", Dm["rs_dbg"][:], B["rsum"][:, 0:1], stream="ast")
                        ctx.dma("sync", Dm["rz_dbg"][:], B["rz"][:, 0:1], stream="ast")
        attn_done = ctx.now("v")

        def loo(k, j):
            return B["wob"][:, k * 1024 + j * 128: k * 1024 + (j + 1) * 128]
        def roo(k):
            return B["afm"][:, k * 512:(k + 1) * 512]
        def evo(ctx, ps, j, tm):
            ctx.wait("vector", "t", tm)
            ctx.op("vector", lambda e, ps=ps, j=j: e.tensor_scalar_add(
                ch(B["a_t"], j), ps, B["boc"][:, j:j + 1]), inc="v")
            return ("v", ctx.now("v"))
        emit_proj(ctx, C, C, loo, roo, evo, banks=("pb3", "pb4"), wait_first=[("v", attn_done)])
        ad = ctx.now("v")
        ctx.wait("sync", "v", ad)
        ctx.dma("sync", fm_dram_tile(Dm["a_scr"], t0, TT), tile_view(B["a_t"], TT), stream="ast")
    if DBG_P8:
        ctx.wait("sync", "v", ctx.now("v"))
        ctx.dma("sync", Dm["kf_dbg"][:], B["kf"][:], stream="ast")
        ctx.dma("sync", Dm["vt_dbg"][:], B["vt"][:], stream="ast")
        ctx.dma("sync", Dm["qf_dbg"][:], B["qf"][:], stream="ast")
        ctx.dma("sync", Dm["afm_dbg"][:], B["afm"][:], stream="ast")
    ctx.wait_dma("sync", "ast")


P8A_BUFS = dict(
    xkv=([128, C * 1024], BF16), kf=([128, C * 1024], BF16), vt=([128, 8 * 1024], BF16),
    qf=([128, C * 512], BF16), afm=([128, C * 512], BF16), a_t=([128, C * TT], F32),
    wqb=([128, C * 1024], BF16), wkb=([128, C * 1024], BF16),
    wvb=([128, C * 1024], BF16), wob=([128, C * 1024], BF16),
    bqc=([128, 8], F32), bkc=([128, 8], F32), boc=([128, 8], F32),
    bvr=([1, 1024], BF16), onesr=([1, 128], BF16), idt_b=([128, 128], BF16),
    mask=([128, 4 * 640], F32), sc=([128, 640], F32), pn=([128, 640], BF16), pnf=([128, 640], F32),
    pTs=([128, 640], BF16), rmax=([128, 1], F32), rsum=([128, 1], F32), rz=([128, 1], F32),
)


def emit_p8b(ctx):
    B, Dm = ctx.bufs, ctx.dram
    ctx.dma("sync", B["ones_col"][:], Dm["ones_col"][:], stream="c8b")
    ctx.dma("sync", B["ones_rowf"][:], Dm["ones128f"][:], stream="c8b")
    ctx.dma("sync", B["g1c"][:], Dm["l1_g1c"][:], stream="c8b")
    ctx.dma("sync", B["b1cc"][:], Dm["l1_b1c"][:], stream="c8b")
    ctx.dma("sync", B["g2c"][:], Dm["l1_g2c"][:], stream="c8b")
    ctx.dma("sync", B["b2cc"][:], Dm["l1_b2c"][:], stream="c8b")
    ctx.dma("sync", B["c1c"][:], Dm["l1_c1c"][:], stream="c8b")
    ctx.dma("sync", B["c2c"][:], Dm["l1_c2c"][:], stream="c8b")
    for t in range(NTILES):
        t0 = t * TT
        ctx.wait("sync", "t", ctx.now("t"))
        ctx.wait("sync", "v", ctx.now("v"))
        ctx.wait("sync", "s", ctx.now("s"))
        ctx.dma("sync", tile_view(B["a_t"], TT), fm_dram_tile(Dm["a_scr"], t0, TT), stream="x8b")
        ctx.dma("sync", tile_view(B["x1t"], TT), Dm["x1e"][:, 256 + t0:256 + t0 + TT].rearrange("(c p) t -> p c t", c=C), stream="x8b")
        ctx.dma("sync", B["murt"][:], Dm["murowB_d"][0:1, 256 + t0:256 + t0 + TT], stream="x8b")
        ctx.dma("sync", B["rstdt"][:], Dm["rstdrowB_d"][0:1, 256 + t0:256 + t0 + TT], stream="x8b")
        ctx.dma("sync", tile_view(B["g2t"], TT), fm_dram_tile(Dm["g2_scr"], t0, TT), stream="x8b")
        ctx.wait_dma("tensor", "x8b")
        ctx.wait_dma("tensor", "c8b")
        ctx.wait("tensor", "v", ctx.now("v"))
        ctx.op("tensor", lambda e: e.matmul(B["pb2"][:, :], B["ones_rowf"][:], B["murt"][0:1, :], start=True, stop=True))
        ctx.op("tensor", lambda e: e.matmul(B["pb3"][:, :], B["ones_rowf"][:], B["rstdt"][0:1, :], start=True, stop=True), inc="t")
        bc = ctx.now("t")
        ctx.wait("vector", "t", bc)
        ctx.wait_dma("vector", "x8b")
        for c in range(C):
            ctx.op("vector", lambda e, c=c: e.tensor_sub(B["tmp5"][:, 0:TT], ch(B["x1t"], c), B["pb2"][:, :]))
            ctx.op("vector", lambda e, c=c: e.tensor_mul(B["tmp5"][:, 0:TT], B["tmp5"][:, 0:TT], B["pb3"][:, :]))
            ctx.op("vector", lambda e, c=c: e.tensor_add(ch(B["ht"], c), B["tmp5"][:, 0:TT], ch(B["a_t"], c)),
                   inc="v" if c == C - 1 else None)
        pre_done = ctx.now("v")
        ctx.wait("scalar", "v", pre_done)
        ctx.op("scalar", lambda e: e.copy(B["hb"][:], B["ht"][:]))
        ctx.op("scalar", lambda e: e.activation(B["sq"][:], B["ht"][:], AF.Square), inc="s")
        casts = ctx.now("s")
        rstd_done, _ = _ln_stats_rows(ctx, B["hb"], B["sq"], B["mur"][0:1, :], B["rstdr"][0:1, :],
                                      [("s", casts)])
        bc2 = _bcast2(ctx, B["mur"][0:1, :], B["rstdr"][0:1, :], rstd_done, pre_done)
        for c in range(C):
            ctx.op("vector", lambda e, c=c: e.tensor_sub(B["tmp5"][:, 0:TT], ch(B["ht"], c), B["pb2"][:, :]))
            ctx.op("vector", lambda e, c=c: e.tensor_mul(B["tmp5"][:, 0:TT], B["tmp5"][:, 0:TT], B["pb3"][:, :]))
            ctx.op("vector", lambda e, c=c: e.tensor_scalar(
                ch(B["ht"], c), B["tmp5"][:, 0:TT], B["g1c"][:, c:c + 1], B["b1cc"][:, c:c + 1],
                ALU.mult, ALU.add), inc="v" if c == C - 1 else None)
        h_done = ctx.now("v")
        ctx.wait("scalar", "v", h_done)
        ctx.op("scalar", lambda e: e.copy(B["hb"][:], B["ht"][:]), inc="s")
        hb_done = ctx.now("s")
        for j2 in range(C):
            ctx.wait("gpsimd", "t", ctx.now("t"))
            ctx.dma("gpsimd", B["w1s"][:].rearrange("p (c n) -> p c n", c=C),
                    Dm["l1_W1"][:, j2 * 512:(j2 + 1) * 512].rearrange("(c p) n -> p c n", c=C),
                    stream="w1b")

            def lh(k, j, j2=j2):
                return B["w1s"][:, k * 512 + j * 128: k * 512 + (j + 1) * 128]
            def rh(k):
                return ch(B["hb"], k)
            def evh(ctx, ps, j, tm, j2=j2):
                ctx.wait("vector", "t", tm)
                jj = j2 * 4 + j
                ctx.op("vector", lambda e, ps=ps, jj=jj: e.tensor_scalar_add(
                    B["gx"][:, 0:TT], ps, B["c1c"][:, jj:jj + 1]), inc="v")
                v1 = ctx.now("v")
                ctx.wait("scalar", "v", v1)
                ctx.op("scalar", lambda e: e.activation(B["gx2"][:, 0:TT], B["gx"][:, 0:TT], AF.Square), inc="s")
                s1_ = ctx.now("s")
                ctx.wait("vector", "s", s1_)
                ctx.op("vector", lambda e: e.tensor_scalar(
                    B["gx2"][:, 0:TT], B["gx2"][:, 0:TT], 0.044715 * 0.7978845608028654, 0.7978845608028654,
                    ALU.mult, ALU.add))
                ctx.op("vector", lambda e: e.tensor_mul(B["gx2"][:, 0:TT], B["gx2"][:, 0:TT], B["gx"][:, 0:TT]), inc="v")
                v2 = ctx.now("v")
                ctx.wait("scalar", "v", v2)
                ctx.op("scalar", lambda e: e.activation(B["gx2"][:, 0:TT], B["gx2"][:, 0:TT], AF.Tanh), inc="s")
                s2_ = ctx.now("s")
                ctx.wait("vector", "s", s2_)
                ctx.op("vector", lambda e: e.tensor_scalar(
                    B["gx2"][:, 0:TT], B["gx2"][:, 0:TT], 1.0, 0.5, ALU.add, ALU.mult))
                ctx.op("vector", lambda e, jj=jj: e.tensor_mul(
                    B["h1"][:, jj * TT:(jj + 1) * TT], B["gx2"][:, 0:TT], B["gx"][:, 0:TT]), inc="v")
                return ("v", ctx.now("v"))
            emit_proj(ctx, 4, C, lh, rh, evh, banks=("pb0", "pb1"),
                      wait_first=[("q:w1b", 0), ("s", hb_done)])
        for cc in range(C):
            ctx.wait("gpsimd", "t", ctx.now("t"))
            ctx.dma("gpsimd", B["w2s"][:].rearrange("p (c n) -> p c n", c=32),
                    Dm["l1_W2"][:, cc * 128:(cc + 1) * 128].rearrange("(c p) n -> p c n", c=32),
                    stream="w2b")

            def lo(k, j, cc=cc):
                return B["w2s"][:, k * 128:(k + 1) * 128]
            def ro(k):
                return B["h1"][:, k * TT:(k + 1) * TT]
            def evo(ctx, ps, j, tm, cc=cc):
                ctx.wait("vector", "t", tm)
                ctx.op("vector", lambda e, ps=ps, cc=cc: e.tensor_scalar_add(
                    ch(B["ft"], cc), ps, B["c2c"][:, cc:cc + 1]))
                ctx.op("vector", lambda e, cc=cc: e.tensor_add(
                    ch(B["ft"], cc), ch(B["ft"], cc), ch(B["ht"], cc)), inc="v")
                return ("v", ctx.now("v"))
            emit_proj(ctx, 1, 32, lo, ro, evo, banks=("pb4", "pb5") if cc % 2 == 0 else ("pb6", "pb5"),
                      wait_first=[("q:w2b", 0)])
        f_done = ctx.now("v")
        ctx.wait("scalar", "v", f_done)
        ctx.op("scalar", lambda e: e.copy(B["hb"][:], B["ft"][:]))
        ctx.op("scalar", lambda e: e.activation(B["sq"][:], B["ft"][:], AF.Square), inc="s")
        casts2 = ctx.now("s")
        rstd2_done, _ = _ln_stats_rows(ctx, B["hb"], B["sq"], B["mur"][0:1, :], B["rstdr"][0:1, :],
                                       [("s", casts2)])
        bc3 = _bcast2(ctx, B["mur"][0:1, :], B["rstdr"][0:1, :], rstd2_done, f_done)
        ctx.wait_dma("vector", "ost")
        for c in range(C):
            ctx.op("vector", lambda e, c=c: e.tensor_sub(B["tmp5"][:, 0:TT], ch(B["ft"], c), B["pb2"][:, :]))
            ctx.op("vector", lambda e, c=c: e.tensor_mul(B["tmp5"][:, 0:TT], B["tmp5"][:, 0:TT], B["pb3"][:, :]))
            ctx.op("vector", lambda e, c=c: e.tensor_scalar(
                B["tmp5"][:, 0:TT], B["tmp5"][:, 0:TT], B["g2c"][:, c:c + 1], B["b2cc"][:, c:c + 1],
                ALU.mult, ALU.add))
            ctx.op("vector", lambda e, c=c: e.tensor_mul(B["tmp5"][:, 0:TT], B["tmp5"][:, 0:TT], ch(B["g2t"], c)))
            ctx.op("vector", lambda e, c=c: e.tensor_add(ch(B["ot"], c), B["tmp5"][:, 0:TT], ch(B["x1t"], c)),
                   inc="v" if c == C - 1 else None)
        odone = ctx.now("v")
        ctx.wait("sync", "v", odone)
        ctx.dma("sync", fm_dram_tile(Dm["out_fm"], t0, TT), tile_view(B["ot"], TT), stream="ost")
    ctx.wait_dma("sync", "ost")


P8B_BUFS = dict(
    a_t=([128, C * TT], F32), x1t=([128, C * TT], F32), ht=([128, C * TT], F32),
    ft=([128, C * TT], F32), g2t=([128, C * TT], F32), ot=([128, C * TT], F32),
    hb=([128, C * TT], BF16), sq=([128, C * TT], BF16), h1=([128, 32 * TT], BF16),
    w1s=([128, C * 512], BF16), w2s=([128, 32 * 128], BF16),
    g1c=([128, 8], F32), b1cc=([128, 8], F32), g2c=([128, 8], F32), b2cc=([128, 8], F32),
    c1c=([128, 32], F32), c2c=([128, 8], F32),
    mur=([1, TT], F32), rstdr=([1, TT], F32), r1=([1, TT], F32), r2=([1, TT], F32),
    murt=([1, TT], F32), rstdt=([1, TT], F32),
    tmp5=([128, TT], F32), gx=([128, TT], F32), gx2=([128, TT], F32),
    ones_col=([128, 1], BF16), ones_rowf=([1, 128], F32),
)


LAUNCH_D_DRAMS = dict(
    x_fm=([D, NT], F32, "in"),
    s_scr=([D, NT], F32, "in"),
    g_scr=([D, NT], F32, "in"),
    f1_cT=([D, 8], F32, "in"),
    ones_col=([128, 1], BF16, "in"),
    ones128f=([1, 128], F32, "in"),
    x1e=([D, NTE], F32, "out"),
    hscr=([3, D, 256], F32, "internal"),
    xn1b_e=([D, NTE], BF16, "out"),
    xhat2b_d=([D, NT], BF16, "out"),
    murowB_d=([1, NTE], F32, "out"),
    rstdrowB_d=([1, NTE], F32, "out"),
    masks_d=([8, NT], F32, "out"),
    masksb_d=([8, NT], BF16, "out"),
    alphar_d=([1, NT], F32, "out"),
)
LAUNCH_D_DRAMS.update(exchange_drams())

LAUNCH_E_DRAMS = dict(
    x1e=([D, NTE], F32, "in"),
    xhat2b_d=([D, NT], BF16, "in"),
    masksb_d=([8, NT], BF16, "in"),
    alphar_d=([1, NT], F32, "in"),
    murowB_d=([1, NTE], F32, "in"),
    rstdrowB_d=([1, NTE], F32, "in"),
    f1_w1=([8, D, 4 * D], F32, "in"),
    f1_w2=([8, 4 * D, D], F32, "in"),
    f1_b1s=([128, 256], F32, "in"),
    f1_b2=([8, D], F32, "in"),
    f1_lngs=([128, 64], F32, "in"),
    f1_lnbs=([128, 64], F32, "in"),
    ind8=([8, 8 * 128], BF16, "in"),
    ones128f=([1, 128], F32, "in"),
    g2_scr=([D, NT], F32, "out"),
)

LAUNCH_F_DRAMS = dict(
    x1e=([D, NTE], F32, "in"),
    xn1b_e=([D, NTE], BF16, "in"),
    g2_scr=([D, NT], F32, "in"),
    murowB_d=([1, NTE], F32, "in"),
    rstdrowB_d=([1, NTE], F32, "in"),
    maskt_d=([8, 128, 4 * 640], F32, "in"),
    l1_Wq=([D, D], F32, "in"), l1_Wk=([D, D], F32, "in"),
    l1_Wv=([D, D], F32, "in"), l1_Wo=([D, D], F32, "in"),
    l1_W1=([D, 4 * D], F32, "in"), l1_W2=([4 * D, D], F32, "in"),
    l1_bqc=([128, 8], F32, "in"), l1_bkc=([128, 8], F32, "in"),
    l1_bvr=([1, D], F32, "in"), l1_boc=([128, 8], F32, "in"),
    l1_g1c=([128, 8], F32, "in"), l1_b1c=([128, 8], F32, "in"),
    l1_g2c=([128, 8], F32, "in"), l1_b2c=([128, 8], F32, "in"),
    l1_c1c=([128, 32], F32, "in"), l1_c2c=([128, 8], F32, "in"),
    idt=([128, 128], BF16, "in"),
    ones128=([1, 128], BF16, "in"),
    ones_col=([128, 1], BF16, "in"),
    ones128f=([1, 128], F32, "in"),
    a_scr=([D, NT], F32, "internal"),
    out_fm=([D, NT], F32, "out"),
)

LAUNCH_B2_DRAMS = dict(LAUNCH_B_DRAMS)


def _run_launch(nc, drams, per_core, extra, cores):
    in_maps = []
    for c in cores:
        d = {}
        for k, v in drams.items():
            if v[2] != "in":
                continue
            if k in extra[c]:
                d[k] = extra[c][k]
            else:
                d[k] = np.ascontiguousarray(per_core[c][k])
        in_maps.append(d)
    return run_bass_kernel_spmd(nc, in_maps, core_ids=list(cores)).results


FUSED = True


def fused_phases():
    ph = [(emit_xch_init, None), (emit_p0, P0_BUFS), (emit_p1, P1_BUFS),
          (emit_p2, P2_BUFS), (emit_p2rot, P2ROT_BUFS), (emit_p3, P3_BUFS),
          (None, P4_SHARED, launch_c_phases()[1:]),
          (emit_p5, P5_BUFS), (emit_p6, P6_BUFS), (emit_p7, P7_BUFS),
          (emit_p8a, P8A_BUFS), (emit_p8b, P8B_BUFS)]
    return ph


def fused_drams():
    merged = {}
    order = [LAUNCH_A_DRAMS, LAUNCH_EXPM_DRAMS, LAUNCH_B_DRAMS, launch_c_drams(),
             LAUNCH_D_DRAMS, LAUNCH_E_DRAMS, LAUNCH_F_DRAMS]
    produced = set()
    for dd in order:
        for k, (shape, dt, kind) in dd.items():
            if kind == "out":
                produced.add(k)
                merged[k] = (shape, dt, "internal")
            elif kind == "in":
                if k in produced:
                    pass
                elif k not in merged:
                    merged[k] = (shape, dt, "in")
            else:
                if k not in merged or merged[k][2] == "in" and kind == "shared":
                    merged[k] = (shape, dt, kind)
    merged["out_fm"] = ([D, NT], F32, "out")
    return merged


def kernel(**inputs):
    per_core = stage_inputs(inputs)
    cores = list(range(NCORES))
    if FUSED:
        drams = fused_drams()
        nc = build_program(fused_phases(), drams)
        extra = [dict() for _ in cores]
        res = _run_launch(nc, drams, per_core, extra, cores)
        out = np.empty((4, 8192, D), np.float32)
        for c in cores:
            b, h = c // 2, c % 2
            out[b, h * NT:(h + 1) * NT, :] = res[c]["out_fm"].T
        return out
    return _kernel_multi(inputs, per_core, cores)


def _kernel_multi(inputs, per_core, cores):
    extra = [dict() for _ in cores]

    ncA = build_program([(emit_p0, P0_BUFS)], LAUNCH_A_DRAMS)
    resA = _run_launch(ncA, LAUNCH_A_DRAMS, per_core, extra, cores)
    for c in cores:
        extra[c].update(resA[c])

    ncX = build_program([(emit_p1, P1_BUFS)], LAUNCH_EXPM_DRAMS)
    resX = _run_launch(ncX, LAUNCH_EXPM_DRAMS, per_core, extra, cores)
    for c in cores:
        extra[c].update(resX[c])

    ncB = build_program([(emit_p2, P2_BUFS), (emit_p2rot, P2ROT_BUFS), (emit_p3, P3_BUFS)], LAUNCH_B_DRAMS)
    resB = _run_launch(ncB, LAUNCH_B_DRAMS, per_core, extra, cores)
    for c in cores:
        extra[c].update(resB[c])

    ncC = build_program(launch_c_phases(), launch_c_drams(), shared_bufs=P4_SHARED)
    resC = _run_launch(ncC, launch_c_drams(), per_core, extra, cores)
    for c in cores:
        extra[c].update(resC[c])

    ncD = build_program([(emit_xch_init, None), (emit_p5, P5_BUFS), (emit_p6, P6_BUFS)], LAUNCH_D_DRAMS)
    resD = _run_launch(ncD, LAUNCH_D_DRAMS, per_core, extra, cores)
    for c in cores:
        extra[c].update(resD[c])

    ncE = build_program([(emit_p7, P7_BUFS)], LAUNCH_E_DRAMS)
    resE = _run_launch(ncE, LAUNCH_E_DRAMS, per_core, extra, cores)
    for c in cores:
        extra[c].update(resE[c])

    ncF = build_program([(emit_p8a, P8A_BUFS), (emit_p8b, P8B_BUFS)], LAUNCH_F_DRAMS)
    resF = _run_launch(ncF, LAUNCH_F_DRAMS, per_core, extra, cores)

    out = np.empty((4, 8192, D), np.float32)
    for c in cores:
        b, h = c // 2, c % 2
        out[b, h * NT:(h + 1) * NT, :] = resF[c]["out_fm"].T
    return out


# version 13
# speedup vs baseline: 1.4024x; 1.4024x over previous
import numpy as np
import ml_dtypes
import concourse.bass as bass
import concourse.mybir as mybir
from concourse.bass_utils import run_bass_kernel_spmd
from contextlib import ExitStack

F32 = mybir.dt.float32
BF16 = mybir.dt.bfloat16
I32 = mybir.dt.int32
AF = mybir.ActivationFunctionType
ALU = mybir.AluOpType
AX = mybir.AxisListType.X
BF = ml_dtypes.bfloat16

NCORES = 8
D = 1024
C = 8
NT = 4096
TT = 512
NTILES = NT // TT
HEADS = 16
DK = 64
EPS = 1e-5
EPS_ATTN = 1e-6
NEG = -1.0e30
MAGIC = 123456789

ENGINES = ("tensor", "vector", "scalar", "gpsimd", "sync")


class Ctx:

    def __init__(self, nc, block, sems, bufs, dram):
        self.nc = nc
        self.block = block
        self.sems = sems
        self.counts = {k: 0 for k in sems}
        self.bufs = bufs
        self.dram = dram
        self.cur = None
        self.eng = None
        self.streams = {}
        self.nq = sum(1 for k in sems if k.startswith("q"))

    def _stream(self, name):
        if name not in self.streams:
            cls = getattr(self, "_sqc", {}).get(name, "hw")
            if cls == "sw":
                slots = [f"q{i}" for i in range(16, 24)]
            else:
                slots = [f"q{i}" for i in range(16)]
            n = sum(1 for k, v in self.streams.items()
                    if getattr(self, "_scls", {}).get(k) == cls)
            if not hasattr(self, "_scls"):
                self._scls = {}
            self._scls[name] = cls
            self.streams[name] = slots[n % len(slots)]
        return self.streams[name]

    def run(self, phase_fn, *args, **kwargs):
        base = dict(self.counts)
        final = None
        for en in ENGINES:
            self.counts = dict(base)
            self.cur = en
            def body(e):
                self.eng = e
                phase_fn(self, *args, **kwargs)
            getattr(self.block, en)(body)
            if final is None:
                final = dict(self.counts)
            else:
                assert final == self.counts, f"phase {phase_fn.__name__} nondeterministic across engine passes"
        self.counts = final

    def op(self, engine, fn, inc=None, n=1):
        if self.cur == engine:
            ins = fn(self.eng)
            if inc is not None:
                ins.then_inc(self.sems[inc], n)
        if inc is not None:
            self.counts[inc] += n
            return self.counts[inc]
        return None

    def dma(self, queue, out, in_, stream="misc", **kw):
        qclass = "sw" if queue == "gpsimd" else "hw"
        if not hasattr(self, "_sqc"):
            self._sqc = {}
        prev = self._sqc.setdefault(stream, qclass)
        assert prev == qclass, f"stream {stream} mixes HWDGE and SWDGE queues"
        sem = self._stream(stream)
        if self.cur == queue:
            self.eng.dma_start(out=out, in_=in_, **kw).then_inc(self.sems[sem], 16)
        self.counts[sem] += 16
        return self.counts[sem]

    def wait_dma(self, engine, stream):
        sem = self._stream(stream)
        if self.cur == engine:
            self.eng.wait_ge(self.sems[sem], self.counts[sem])

    def wait(self, engine, sem, val):
        assert val <= self.counts[sem], (
            f"wait_ge({sem},{val}) ahead of count {self.counts[sem]} on {engine}")
        if self.cur == engine:
            self.eng.wait_ge(self.sems[sem], val)

    def now(self, sem):
        return self.counts[sem]

    def raw(self, engine, fn):
        if self.cur == engine:
            fn(self.eng)


def _fm(x):
    return np.ascontiguousarray(np.asarray(x).T)


def _cv(v):
    v = np.asarray(v).reshape(-1)
    return np.ascontiguousarray(v.reshape(-1, 128).T)


def stage_inputs(inputs):
    x = np.asarray(inputs["L0_attn_x_in"])
    per_core = []
    for c in range(NCORES):
        b, h = c // 2, c % 2
        d = {}
        d["x_fm"] = _fm(x[b, h * NT:(h + 1) * NT, :])
        d["side"] = np.array([[h]], dtype=np.int32)
        per_core.append(d)

    shared = {}
    for l in range(4):
        for nm in ("Wq", "Wk", "Wv", "Wo"):
            shared[f"l0_{nm}{l}"] = np.ascontiguousarray(inputs[f"L0_attn_{nm}"][l])
        shared[f"l0_W1{l}"] = np.ascontiguousarray(inputs["L0_attn_W1"][l])
        shared[f"l0_W2{l}"] = np.ascontiguousarray(inputs["L0_attn_W2"][l])
    names = dict(
        l0_bq="L0_attn_bq", l0_bk="L0_attn_bk", l0_bv="L0_attn_bv", l0_bo="L0_attn_bo",
        l0_b1="L0_attn_b1", l0_b2="L0_attn_b2", l0_ln1s="L0_attn_ln1_s", l0_ln1b="L0_attn_ln1_b",
        l0_ln2s="L0_attn_ln2_s", l0_ln2b="L0_attn_ln2_b",
    )
    for k, v in names.items():
        shared[k] = np.ascontiguousarray(inputs[v])
    shared["l0_lnfs"] = inputs["L0_attn_lnf_s"].reshape(D, 1)
    shared["l0_lnfb"] = inputs["L0_attn_lnf_b"].reshape(D, 1)

    shared["f0_lng"] = inputs["L0_ffn_ln_g"].reshape(D, 1)
    shared["f0_lnb"] = inputs["L0_ffn_ln_b"].reshape(D, 1)
    shared["f0_aw1"] = np.ascontiguousarray(inputs["L0_ffn_aw1"])
    shared["f0_ab1"] = inputs["L0_ffn_ab1"].reshape(4 * D, 1)
    shared["f0_aw2"] = np.ascontiguousarray(inputs["L0_ffn_aw2"])
    shared["f0_ab2"] = inputs["L0_ffn_ab2"].reshape(D, 1)
    shared["f0_gw"] = np.ascontiguousarray(inputs["L0_ffn_gate_w"])
    shared["f0_gb"] = inputs["L0_ffn_gate_b"].reshape(6, 1)
    shared["f0_ew1"] = np.ascontiguousarray(inputs["L0_ffn_ew1"])
    shared["f0_eb1"] = np.ascontiguousarray(inputs["L0_ffn_eb1"])
    shared["f0_ew2"] = np.ascontiguousarray(inputs["L0_ffn_ew2"])
    shared["f0_eb2"] = np.ascontiguousarray(inputs["L0_ffn_eb2"])
    shared["f0_gens"] = np.ascontiguousarray(inputs["L0_ffn_gens"])
    shared["f0_gcw"] = np.ascontiguousarray(inputs["L0_ffn_gc_w"])
    shared["f0_gcb"] = inputs["L0_ffn_gc_b"].reshape(1, 1)

    for nm in ("Wq", "Wk", "Wv", "Wo", "W1", "W2"):
        shared[f"l1_{nm}"] = np.ascontiguousarray(inputs[f"L1_attn_{nm}"])
    for nm, src in (("bq", "L1_attn_bq"), ("bk", "L1_attn_bk"), ("bv", "L1_attn_bv"),
                    ("bo", "L1_attn_bo"), ("g1", "L1_attn_g1"), ("b1", "L1_attn_b1"),
                    ("c2", "L1_attn_c2"), ("g2", "L1_attn_g2"), ("b2", "L1_attn_b2")):
        shared[f"l1_{nm}"] = inputs[src].reshape(-1, 1)
    shared["l1_c1"] = inputs["L1_attn_c1"].reshape(4 * D, 1)

    shared["f1_cT"] = _fm(inputs["L1_ffn_centroids"])
    shared["f1_lng"] = np.ascontiguousarray(inputs["L1_ffn_ln_g"])
    shared["f1_lnb"] = np.ascontiguousarray(inputs["L1_ffn_ln_b"])
    shared["f1_w1"] = np.ascontiguousarray(np.transpose(inputs["L1_ffn_w1"], (0, 2, 1)))
    shared["f1_b1"] = np.ascontiguousarray(inputs["L1_ffn_b1"])
    shared["f1_w2"] = np.ascontiguousarray(np.transpose(inputs["L1_ffn_w2"], (0, 2, 1)))
    shared["f1_b2"] = np.ascontiguousarray(inputs["L1_ffn_b2"])

    shared["f0_lngc"] = _cv(inputs["L0_ffn_ln_g"])
    shared["f0_lnbc"] = _cv(inputs["L0_ffn_ln_b"])
    shared["f0_ab1c"] = _cv(inputs["L0_ffn_ab1"])
    shared["f0_ab2c"] = _cv(inputs["L0_ffn_ab2"])
    shared["f0_eb1c"] = np.stack([_cv(inputs["L0_ffn_eb1"][e]) for e in range(6)])
    eb2p = np.zeros((8, 1024), np.float32); eb2p[:6] = inputs["L0_ffn_eb2"]
    shared["f0_eb2p"] = eb2p
    shared["f0_gcwa"] = _cv(inputs["L0_ffn_gc_w"][:1024, 0])
    shared["f0_gcwb"] = _cv(inputs["L0_ffn_gc_w"][1024:, 0])
    for nm in ("bq", "bk", "bv", "bo", "b2"):
        shared[f"l0_{nm}c"] = np.stack([_cv(inputs[f"L0_attn_{nm}"][l]) for l in range(4)])
    shared["l0_b1c"] = np.stack([_cv(inputs["L0_attn_b1"][l]) for l in range(4)])
    for nm in ("ln1_s", "ln1_b", "ln2_s", "ln2_b"):
        shared[f"l0_{nm.replace('_','')}c"] = np.stack([_cv(inputs[f"L0_attn_{nm}"][l]) for l in range(4)])
    shared["l0_lnfsc"] = _cv(inputs["L0_attn_lnf_s"])
    shared["l0_lnfbc"] = _cv(inputs["L0_attn_lnf_b"])
    for nm in ("bq", "bk", "bv", "bo", "g1", "b1", "c2", "g2", "b2"):
        shared[f"l1_{nm}c"] = _cv(inputs[f"L1_attn_{nm}"])
    shared["l1_c1c"] = _cv(inputs["L1_attn_c1"])
    shared["l1_bvr"] = np.ascontiguousarray(inputs["L1_attn_bv"].reshape(1, -1))
    shared["f1_b1c"] = np.stack([_cv(inputs["L1_ffn_b1"][e]) for e in range(8)])
    shared["f1_lngcc"] = np.stack([_cv(inputs["L1_ffn_ln_g"][e]) for e in range(8)])
    shared["f1_lnbcc"] = np.stack([_cv(inputs["L1_ffn_ln_b"][e]) for e in range(8)])
    shared["f1_lngs"] = np.ascontiguousarray(shared["f1_lngcc"].transpose(1, 0, 2).reshape(128, 64))
    shared["f1_lnbs"] = np.ascontiguousarray(shared["f1_lnbcc"].transpose(1, 0, 2).reshape(128, 64))
    shared["f1_b1s"] = np.ascontiguousarray(np.stack([_cv(inputs["L1_ffn_b1"][e]) for e in range(8)]).transpose(1, 0, 2).reshape(128, 256))
    shared["ones128"] = np.ones((1, 128), dtype=BF)
    shared["ones_col"] = np.ones((128, 1), dtype=BF)
    shared["ones128f"] = np.ones((1, 128), dtype=np.float32)
    shared["idt"] = np.eye(128, dtype=BF)
    shared["idtf"] = np.eye(128, dtype=np.float32)
    shared["ind8"] = _make_ind8()
    i16c = np.zeros((128, C * 16), dtype=BF)
    i16T = np.zeros((16, C * 128), dtype=BF)
    for c in range(C):
        for p in range(128):
            h = 2 * c + p // 64
            i16c[p, c * 16 + h] = 1.0
            i16T[h, c * 128 + p] = 1.0
    shared["i16c"] = i16c
    shared["i16Tc"] = i16T
    consts_per_core = [_core_consts(c) for c in range(NCORES)]
    for c in range(NCORES):
        per_core[c].update(shared)
        per_core[c].update(consts_per_core[c])
    return per_core


def _make_ind8():
    out = np.zeros((8, 8 * 128), dtype=BF)
    for e in range(8):
        out[e, e * 128:(e + 1) * 128] = 1.0
    return out


def _posemb_slice(h):
    import math
    half = D // 2
    freq = np.exp(np.arange(half, dtype=np.float32) * (-math.log(10000.0) / (half - 1)))
    pos = (np.arange(h * NT, (h + 1) * NT, dtype=np.float32) + 1.0)[:, None] * freq[None, :]
    pe = np.concatenate([np.sin(pos), np.cos(pos)], axis=-1).astype(np.float32)
    return _fm(pe)


_BAND = None


def _win_masks(side):
    global _BAND
    if _BAND is None:
        i = np.arange(128)[:, None]
        j = np.arange(640)[None, :]
        _BAND = np.where((j - i >= 0) & (j - i <= 512), 0.0, -1e9).astype(np.float32)
    m = np.zeros((NTILES, 128, 4 * 640), np.float32)
    for t in range(NTILES):
        for qs in range(4):
            F0 = t * 512 + qs * 128
            pos = F0 + np.arange(640)
            edge = np.zeros(640, np.float32)
            if side == 0:
                edge[pos < 256] = -1e9
            else:
                edge[pos >= NTE - 256] = -1e9
            m[t, :, qs * 640:(qs + 1) * 640] = _BAND + edge[None, :]
    return m


def _core_consts(c):
    h = c % 2
    return {"posemb": _posemb_slice(h), "maskt_d": _win_masks(h)}


def _ceil(a, b):
    return (a + b - 1) // b


def fm_dram_tile(d, t0, L, c0=0, nch=C):
    return d[c0 * 128:(c0 + nch) * 128, t0:t0 + L].rearrange("(c p) t -> p c t", c=nch)


def tile_view(buf, L, nch=C):
    return buf[:, 0:nch * L].rearrange("p (c t) -> p c t", c=nch)


def ch(buf, c, L=TT):
    return buf[:, c * L:(c + 1) * L]


XROUNDS = 12


def exchange_drams():
    d = {}
    for r in range(XROUNDS):
        d[f"xch{r}"] = ([2 * 128 * 4096], F32, "shared")
    d["xfl"] = ([2 * XROUNDS], I32, "shared")
    d["side"] = ([1, 1], I32, "in")
    return d


def emit_xch_init(ctx):
    B = ctx.bufs
    ld = ctx.dma("sync", B["sside"][:], ctx.dram["side"][:], stream="xch")
    for r in range(XROUNDS):
        ctx.op("gpsimd", lambda e, r=r: e.memset(B["magics"][0:1, r:r + 1], MAGIC + r),
               inc="g" if r == XROUNDS - 1 else None)
    mg = ctx.now("g")
    ctx.wait("sync", "g", mg)
    ctx.wait_dma("sync", "xch")

    def setup(e):
        R = ctx.xregs
        e.reg_mov(R["pw"], 0)
        ctx.side_sv = e.partition_id() % 2
    ctx.raw("sync", setup)


def emit_exchange(ctx, rnd, src_ap, dst_ap, nelems, combine=None):
    B = ctx.bufs
    F = nelems // 128
    assert nelems % 128 == 0 and nelems <= 128 * 4096
    xch = ctx.dram[f"xch{rnd}"].tensor
    xfl = ctx.dram["xfl"].tensor
    sname = ctx._stream("xch")
    sem = ctx.sems[sname]
    psem = ctx.psems["poll"]
    after_payload = ctx.counts[sname] + 16

    def ex(e):
        R = ctx.xregs
        for br in e.Switch(ctx.side_sv, 2):
            e.dma_start(out=bass.AP(xch, br * nelems, [[F, 128], [1, F]]),
                        in_=src_ap).then_inc(sem, 16)
            e.wait_ge(sem, after_payload)
            e.dma_start(out=bass.AP(xfl, br * XROUNDS + rnd, [[1, 1], [1, 1]]),
                        in_=B["magics"][0:1, rnd:rnd + 1]).then_inc(sem, 16)
            e.reg_mov(R["cnd"], 1)
            with e.While(R["cnd"]):
                e.reg_add(R["pw"], R["pw"], 16)
                e.dma_start(out=B["flagsb"][0:1, 0:1],
                            in_=bass.AP(xfl, (1 - br) * XROUNDS + rnd, [[1, 1], [1, 1]])
                            ).then_inc(psem, 16)
                e.wait_ge(psem, R["pw"])
                e.reg_load(R["fl"], B["flagsb"][0:1, 0:1])
                e.reg_sub(R["cnd"], R["fl"], MAGIC + rnd)
            e.dma_start(out=dst_ap,
                        in_=bass.AP(xch, (1 - br) * nelems, [[F, 128], [1, F]])
                        ).then_inc(sem, 16)
    ctx.raw("sync", ex)
    ctx.counts[sname] += 48
    return ctx.counts[sname]


XCH_BUFS = dict(
    sside=([1, 1], I32), magics=([1, XROUNDS], I32), flagsb=([1, 1], I32),
)


KONST = {EPS: 0, EPS_ATTN: 1, 1.0: 2, -1.0e9: 3}


def emit_consts(ctx):
    B = ctx.bufs
    for val, col in KONST.items():
        ctx.op("gpsimd", lambda e, v=val, j=col: e.memset(B["konst"][:, j:j + 1], v),
               inc="g" if col == len(KONST) - 1 else None)
    kdone = ctx.now("g")
    for en in ("tensor", "vector", "scalar", "sync"):
        ctx.wait(en, "g", kdone)


def kc(ctx, val):
    return ctx.bufs["konst"][0:1, KONST[val]:KONST[val] + 1]


def emit_p0(ctx):
    B, Dm = ctx.bufs, ctx.dram
    ctx.op("gpsimd", lambda e: e.memset(B["onesr1"][0:1, :], 1.0), inc="g")
    ones_mk = ctx.now("g")
    ctx.wait("vector", "g", ones_mk)
    ctx.dma("sync", tile_view(B["gwf"], 6), Dm["f0_gw"][:, :].rearrange("(c p) e -> p c e", c=C), stream="cst")
    ctx.dma("sync", B["gb6"][0:6, :], Dm["f0_gb"][:], stream="cst")
    ctx.dma("sync", B["ones_col"][:], Dm["ones_col"][:], stream="cst")
    ctx.dma("sync", B["ones_rowf"][:], Dm["ones128f"][:], stream="cst")
    ctx.dma("sync", B["lngc"][:], Dm["f0_lngc"][:], stream="cst")
    ctx.dma("sync", B["lnbc"][:], Dm["f0_lnbc"][:], stream="cst")

    def ln_stats(xbf, sqbf, murow_ap, rstdrow_ap, wait_pre):
        for sem, val in wait_pre:
            ctx.wait("tensor", sem, val)
        ctx.wait_dma("tensor", "cst")
        for c in range(C):
            ctx.op("tensor", lambda e, c=c: e.matmul(
                B["pb0"][0:1, :], B["ones_col"][:], ch(xbf, c), start=(c == 0), stop=(c == C - 1)))
            ctx.op("tensor", lambda e, c=c: e.matmul(
                B["pb1"][0:1, :], B["ones_col"][:], ch(sqbf, c), start=(c == 0), stop=(c == C - 1)),
                inc="t" if c == C - 1 else None)
        sums_done = ctx.now("t")
        ctx.wait("vector", "t", sums_done)
        ctx.op("vector", lambda e: e.tensor_scalar_mul(murow_ap, B["pb0"][0:1, :], 1.0 / D))
        ctx.op("vector", lambda e: e.tensor_scalar_mul(B["r1"][0:1, :], B["pb1"][0:1, :], 1.0 / D))
        ctx.op("vector", lambda e: e.tensor_mul(B["r2"][0:1, :], murow_ap, murow_ap))
        ctx.op("vector", lambda e: e.tensor_sub(B["r1"][0:1, :], B["r1"][0:1, :], B["r2"][0:1, :]), inc="v")
        var_done = ctx.now("v")
        ctx.wait("scalar", "v", var_done)
        ctx.op("scalar", lambda e: e.activation(B["r2"][0:1, :], B["r1"][0:1, :], AF.Ln, bias=kc(ctx, EPS)))
        ctx.op("scalar", lambda e: e.activation(rstdrow_ap, B["r2"][0:1, :], AF.Exp, scale=-0.5), inc="s")
        return ctx.now("s"), sums_done

    for t in range(NTILES):
        t0 = t * TT
        ctx.dma("sync", tile_view(B["xt"], TT), fm_dram_tile(Dm["x_fm"], t0, TT), stream="xt")
        ctx.wait_dma("scalar", "xt")
        ctx.wait("scalar", "v", ctx.now("v"))
        ctx.wait("scalar", "t", ctx.now("t"))
        ctx.op("scalar", lambda e: e.copy(B["xb"][:], B["xt"][:]))
        ctx.op("scalar", lambda e: e.activation(B["sq"][:], B["xt"][:], AF.Square), inc="s")
        sq_done = ctx.now("s")
        rstd_done, _ = ln_stats(B["xb"], B["sq"], B["murow"][0:1, t0:t0 + TT],
                                B["rstdrow"][0:1, t0:t0 + TT], [("s", sq_done)])
        ctx.wait("tensor", "s", rstd_done)
        ctx.wait("tensor", "v", ctx.now("v"))
        ctx.op("tensor", lambda e: e.matmul(B["pb2"][:, :], B["ones_rowf"][:], B["murow"][0:1, t0:t0 + TT], start=True, stop=True))
        ctx.op("tensor", lambda e: e.matmul(B["pb3"][:, :], B["ones_rowf"][:], B["rstdrow"][0:1, t0:t0 + TT], start=True, stop=True), inc="t")
        bc_done = ctx.now("t")
        ctx.wait("vector", "t", bc_done)
        for c in range(C):
            ctx.op("vector", lambda e, c=c: e.tensor_sub(ch(B["t1"], c), ch(B["xt"], c), B["pb2"][:, :]))
            ctx.op("vector", lambda e, c=c: e.tensor_mul(ch(B["t1"], c), ch(B["t1"], c), B["pb3"][:, :]),
                   inc="v" if c == C - 1 else None)
        xn_done = ctx.now("v")
        ctx.wait("scalar", "v", xn_done)
        ctx.op("scalar", lambda e: e.copy(B["xb"][:], B["t1"][:]))
        ctx.op("scalar", lambda e: e.activation(B["sq"][:], B["t1"][:], AF.Square), inc="s")
        sq2_done = ctx.now("s")
        rstd2_done, _ = ln_stats(B["xb"], B["sq"], B["mu2r"][0:1, :], B["rstd2r"][0:1, :],
                                 [("s", sq2_done), ("v", xn_done)])
        ctx.wait("tensor", "s", rstd2_done)
        ctx.wait("tensor", "v", xn_done)
        ctx.op("tensor", lambda e: e.matmul(B["pb2"][:, :], B["ones_rowf"][:], B["mu2r"][0:1, :], start=True, stop=True))
        ctx.op("tensor", lambda e: e.matmul(B["pb3"][:, :], B["ones_rowf"][:], B["rstd2r"][0:1, :], start=True, stop=True), inc="t")
        bc2 = ctx.now("t")
        ctx.wait("vector", "t", bc2)
        for c in range(C):
            ctx.op("vector", lambda e, c=c: e.tensor_sub(ch(B["t2"], c), ch(B["t1"], c), B["pb2"][:, :]))
            ctx.op("vector", lambda e, c=c: e.tensor_mul(ch(B["t2"], c), ch(B["t2"], c), B["pb3"][:, :]))
            ctx.op("vector", lambda e, c=c: e.tensor_scalar(
                ch(B["t2"], c), ch(B["t2"], c), B["lngc"][:, c:c + 1], B["lnbc"][:, c:c + 1],
                ALU.mult, ALU.add), inc="v" if c == C - 1 else None)
        xn2_done = ctx.now("v")
        ctx.wait("scalar", "v", xn2_done)
        ctx.wait_dma("scalar", "sxnb")
        ctx.op("scalar", lambda e: e.copy(B["xnb"][:], B["t2"][:]), inc="s")
        xnb_done = ctx.now("s")
        ctx.wait("sync", "s", xnb_done)
        ctx.dma("sync", fm_dram_tile(Dm["xn2b_d"], t0, TT), tile_view(B["xnb"], TT), stream="sxnb")
        ctx.wait("tensor", "v", xn2_done)
        for c in range(C):
            ctx.op("tensor", lambda e, c=c: e.matmul(
                B["pb4"][0:6, :], B["gwf"][:, c * 6:(c + 1) * 6], ch(B["t2"], c),
                start=(c == 0), stop=(c == C - 1)), inc="t" if c == C - 1 else None)
        lg_done = ctx.now("t")
        ctx.wait("vector", "t", lg_done)
        ctx.wait_dma("vector", "cst")
        L32, M32, K32, I1, I2, R32 = B["L32"], B["M32"], B["K32"], B["I1"], B["I2"], B["R32"]
        ctx.op("vector", lambda e: e.memset(L32[0:8, :], NEG))
        ctx.op("vector", lambda e: e.tensor_scalar_add(L32[0:6, :], B["pb4"][0:6, :], B["gb6"][0:6, :]))
        ctx.op("vector", lambda e: e.tensor_copy(M32[:, :], L32[:, :]))
        for r in (4, 2, 1):
            ctx.op("vector", lambda e, r=r: e.stream_shuffle(R32[:, :], M32[:, :], mask=[(i + r) % 32 for i in range(32)]))
            ctx.op("vector", lambda e: e.tensor_max(M32[:, :], M32[:, :], R32[:, :]))
        ctx.op("vector", lambda e: e.stream_shuffle(R32[:, :], M32[:, :], mask=[0] * 32))
        ctx.op("vector", lambda e: e.tensor_tensor(I1[0:8, :], L32[0:8, :], R32[0:8, :], ALU.is_equal))
        ctx.op("vector", lambda e: e.scalar_tensor_tensor(K32[0:8, :], I1[0:8, :], NEG, L32[0:8, :], ALU.mult, ALU.add))
        for r in (4, 2, 1):
            ctx.op("vector", lambda e, r=r: e.stream_shuffle(R32[:, :], K32[:, :], mask=[(i + r) % 32 for i in range(32)]))
            ctx.op("vector", lambda e: e.tensor_max(K32[:, :], K32[:, :], R32[:, :]))
        ctx.op("vector", lambda e: e.tensor_sub(B["r1"][0:1, :], K32[0:1, :], M32[0:1, :]), inc="v")
        d01 = ctx.now("v")
        ctx.wait("scalar", "v", d01)
        ctx.op("scalar", lambda e: e.activation(B["r4"][0:1, :], B["r1"][0:1, :], AF.Sigmoid, scale=-1.0))
        ctx.op("scalar", lambda e: e.activation(B["r3"][0:1, :], B["r1"][0:1, :], AF.Sigmoid), inc="s")
        eb = ctx.now("s")
        ctx.wait("vector", "s", eb)
        ctx.op("vector", lambda e: e.scalar_tensor_tensor(R32[0:8, :], I1[0:8, :], NEG, L32[0:8, :], ALU.mult, ALU.add))
        ctx.op("vector", lambda e: e.stream_shuffle(B["S32"][:, :], K32[:, :], mask=[0] * 32))
        ctx.op("vector", lambda e: e.tensor_tensor(I2[0:8, :], R32[0:8, :], B["S32"][0:8, :], ALU.is_equal))
        ctx.op("vector", lambda e: e.tensor_copy(B["S32"][0:1, :], B["r4"][0:1, :]))
        ctx.op("vector", lambda e: e.stream_shuffle(B["S32"][:, :], B["S32"][:, :], mask=[0] * 32))
        ctx.op("vector", lambda e: e.tensor_mul(I1[0:8, :], I1[0:8, :], B["S32"][0:8, :]))
        ctx.op("vector", lambda e: e.tensor_copy(B["S32"][0:1, :], B["r3"][0:1, :]))
        ctx.op("vector", lambda e: e.stream_shuffle(B["S32"][:, :], B["S32"][:, :], mask=[0] * 32))
        ctx.op("vector", lambda e: e.tensor_mul(I2[0:8, :], I2[0:8, :], B["S32"][0:8, :]))
        ctx.op("vector", lambda e: e.tensor_add(B["wfull"][0:8, t0:t0 + TT], I1[0:8, :], I2[0:8, :]), inc="v")
        ctx.wait("sync", "v", ctx.now("v"))

    ctx.dma("sync", Dm["murow_d"][:], B["murow"][0:1, :], stream="sout")
    ctx.dma("sync", Dm["rstdrow_d"][:], B["rstdrow"][0:1, :], stream="sout")
    ctx.dma("sync", Dm["wfull_d"][:], B["wfull"][0:8, :], stream="sout")
    ctx.wait_dma("sync", "sout")


P0_BUFS = dict(
    xt=([128, C * TT], F32), t1=([128, C * TT], F32), t2=([128, C * TT], F32),
    xb=([128, C * TT], BF16), sq=([128, C * TT], BF16), xnb=([128, C * TT], BF16),
    gwf=([128, C * 6], F32), gb6=([8, 1], F32),
    lngc=([128, 8], F32), lnbc=([128, 8], F32),
    ones_col=([128, 1], BF16), ones_rowf=([1, 128], F32),
    murow=([1, NT], F32), rstdrow=([1, NT], F32), wfull=([8, NT], F32),
    mu2r=([1, TT], F32), rstd2r=([1, TT], F32),
    r1=([1, TT], F32), r2=([1, TT], F32), r3=([1, TT], F32), r4=([1, TT], F32),
    onesr1=([1, TT], F32),
    L32=([32, TT], F32), M32=([32, TT], F32), K32=([32, TT], F32),
    I1=([8, TT], F32), I2=([8, TT], F32), R32=([32, TT], F32), S32=([32, TT], F32),
)

PSUM_BANKS = ["pb0", "pb1", "pb2", "pb3", "pb4", "pb5", "pb6", "pb7"]


def emit_barrier(ctx):
    for en in ENGINES:
        for s in ("t", "v", "s", "g"):
            ctx.wait(en, s, ctx.now(s))
        for name in list(ctx.streams):
            ctx.wait_dma(en, name)


def build_program(phase_specs, drams, shared_bufs=None):
    nc = bass.Bass("TRN2", target_bir_lowering=False, detect_race_conditions=False)
    dram = {}
    for name, (shape, dt, kind) in drams.items():
        if kind == "in":
            dram[name] = nc.dram_tensor(name, shape, dt, kind="ExternalInput").ap()
        elif kind == "out":
            dram[name] = nc.dram_tensor(name, shape, dt, kind="ExternalOutput").ap()
        elif kind == "shared":
            dram[name] = nc.dram_tensor(name, shape, dt, addr_space="Shared").ap()
        else:
            dram[name] = nc.dram_tensor(name, shape, dt).ap()
    es = ExitStack()
    B = {}
    for name, (shape, dt) in (shared_bufs or {}).items():
        B[name] = es.enter_context(nc.sbuf_tensor("sb_" + name, shape, dt)).ap()
    B["konst"] = es.enter_context(nc.sbuf_tensor("sb_konst", [128, 8], F32)).ap()
    for name, (shape, dt) in XCH_BUFS.items():
        B[name] = es.enter_context(nc.sbuf_tensor("sb_" + name, shape, dt)).ap()
    for i in range(7):
        B[f"pb{i}"] = es.enter_context(nc.psum_tensor(f"pb{i}", [128, 512], F32)).ap()
    B["pt"] = es.enter_context(nc.psum_tensor("pt", [128, 1024], BF16)).ap()
    sems = {}
    for s in ["t", "v", "s", "g"] + [f"q{i}" for i in range(24)]:
        sems[s] = es.enter_context(nc.semaphore(s))
    psems = {"poll": es.enter_context(nc.semaphore("poll"))}
    xregs = {}
    for rn in ("side", "pw", "sflag", "pflag", "tmp", "tmp2", "cnd", "fl"):
        xregs[rn] = es.enter_context(nc.sync.register("xr_" + rn))
    block = es.enter_context(nc.Block())
    ctx = Ctx(nc, block, sems, B, dram)
    ctx.psems = psems
    ctx.xregs = xregs
    ctx.run(emit_consts)
    counter = [0]

    def run_specs(specs, base_bufs):
        for spec in specs:
            i = counter[0]
            counter[0] += 1
            if spec[0] is None:
                gshared, subspecs = spec[1], spec[2]
                with ExitStack() as gs:
                    gextra = {}
                    for name, (shape, dt) in (gshared or {}).items():
                        gextra[name] = gs.enter_context(
                            nc.sbuf_tensor(f"sg{i}_{name}", shape, dt)).ap()
                    run_specs(subspecs, dict(base_bufs, **gextra))
                continue
            fn, bufs = spec[0], spec[1]
            kwargs = spec[2] if len(spec) > 2 else {}
            with ExitStack() as ps:
                extra = {}
                for name, (shape, dt) in (bufs or {}).items():
                    if name in base_bufs:
                        continue
                    extra[name] = ps.enter_context(
                        nc.sbuf_tensor(f"sb{i}_{name}", shape, dt)).ap()
                ctx.bufs = dict(base_bufs, **extra)
                ctx.run(fn, **kwargs)
                ctx.run(emit_barrier)
    run_specs(phase_specs, B)
    ctx.bufs = B
    es.close()
    return nc


LAUNCH_A_DRAMS = dict(
    x_fm=([D, NT], F32, "in"),
    f0_gw=([D, 6], F32, "in"),
    f0_gb=([6, 1], F32, "in"),
    f0_lngc=([128, 8], F32, "in"),
    f0_lnbc=([128, 8], F32, "in"),
    ones_col=([128, 1], BF16, "in"),
    ones128f=([1, 128], F32, "in"),
    xn2b_d=([D, NT], BF16, "out"),
    murow_d=([1, NT], F32, "out"),
    rstdrow_d=([1, NT], F32, "out"),
    wfull_d=([8, NT], F32, "out"),
)


def kernel(**inputs):
    per_core = stage_inputs(inputs)
    raise NotImplementedError("phases under construction")


def sq(buf, c, j0=0, w=1024):
    return buf[:, c * 1024 + j0: c * 1024 + j0 + w]


def emit_mm_sq(ctx, lhsTb, rhsb, evac, wait_t=None, tag=""):
    B = ctx.bufs
    prev_evac = [None, None]
    for i in range(C):
        for j2 in range(2):
            g = i * 2 + j2
            bank = B[f"pb{g % 2}"]
            if prev_evac[g % 2] is not None:
                ctx.wait("tensor", "v", prev_evac[g % 2])
            if wait_t is not None and g == 0:
                for sem, val in wait_t:
                    ctx.wait("tensor", sem, val)
            for k in range(C):
                ctx.op("tensor", lambda e, i=i, j2=j2, k=k, bank=bank: e.matmul(
                    bank[:, 0:512], sq(lhsTb, k, i * 128, 128), sq(rhsb, k, j2 * 512, 512),
                    start=(k == 0), stop=(k == C - 1)), inc="t" if k == C - 1 else None)
            tmark = ctx.now("t")
            ctx.wait("vector", "t", tmark)
            evac(ctx, bank[:, 0:512], i, j2, last=(g == 15))
            prev_evac[g % 2] = ctx.now("v")
    return ctx.now("t"), ctx.now("v")


def emit_transpose_sq(ctx, srcb, dstb, f32=False, wait_t=None):
    B = ctx.bufs
    idt = B["idt_f"] if f32 else B["idt_b"]
    per = 4 if f32 else 8
    bank = B["pb2"] if f32 else B["pt"]
    prev_copy = None
    for j in range(C):
        for grp in range(C // per):
            if prev_copy is not None:
                ctx.wait("tensor", "v", prev_copy)
            if wait_t is not None and j == 0 and grp == 0:
                for sem, val in wait_t:
                    ctx.wait("tensor", sem, val)
            for ii in range(per):
                i = grp * per + ii
                ctx.op("tensor", lambda e, i=i, j=j, ii=ii: e.transpose(
                    bank[:, ii * 128:(ii + 1) * 128], sq(srcb, i, j * 128, 128), idt[:]),
                    inc="t" if ii == per - 1 else None)
            tm = ctx.now("t")
            ctx.wait("vector", "t", tm)
            ctx.op("vector", lambda e, j=j, grp=grp: e.tensor_copy(
                sq(dstb, j, grp * per * 128, per * 128), bank[:, 0:per * 128]), inc="v")
            prev_copy = ctx.now("v")
    return ctx.now("v")


def emit_p1(ctx):
    B, Dm = ctx.bufs, ctx.dram
    ctx.dma("sync", B["idt_b"][:], Dm["idt"][:], stream="cst")
    ctx.dma("sync", B["idt_f"][:], Dm["idtf"][:], stream="cst")
    S = 6
    NTAY = 4
    inv = 1.0 / (1 << S)
    for g in range(4):
        ctx.wait("sync", "v", ctx.now("v"))
        ctx.wait("sync", "s", ctx.now("s"))
        ctx.dma("sync", B["gensf"][:].rearrange("p (c n) -> p c n", c=C),
                Dm["f0_gens"][g].rearrange("(c p) n -> p c n", c=C), stream="gens")
        ctx.wait_dma("vector", "gens")
        ctx.wait_dma("vector", "cst")
        ctx.op("vector", lambda e: e.tensor_scalar_mul(B["Df"][:], B["gensf"][:], inv / NTAY), inc="v")
        dready = ctx.now("v")
        for k in range(NTAY - 1, 0, -1):
            ctx.wait("scalar", "v", dready)
            ctx.op("scalar", lambda e: e.copy(B["Db"][:], B["Df"][:]))
            ctx.op("scalar", lambda e, k=k: e.activation(B["Ab"][:], B["gensf"][:], AF.Copy, scale=-inv / k), inc="s")
            casts = ctx.now("s")

            def evac(ctx, ps, i, j2, last, k=k):
                ctx.op("vector", lambda e: e.scalar_tensor_tensor(
                    sq(B["Df"], i, j2 * 512, 512), sq(B["gensf"], i, j2 * 512, 512),
                    inv / k, ps, ALU.mult, ALU.add), inc="v")
            emit_mm_sq(ctx, B["Ab"], B["Db"], evac, wait_t=[("s", casts)])
            dready = ctx.now("v")
        for sqi in range(S):
            ctx.wait("scalar", "v", dready)
            ctx.op("scalar", lambda e: e.copy(B["Db"][:], B["Df"][:]), inc="s")
            casts = ctx.now("s")
            tdone = emit_transpose_sq(ctx, B["Db"], B["DTb"], wait_t=[("s", casts)])

            def evac(ctx, ps, i, j2, last):
                ctx.op("vector", lambda e: e.scalar_tensor_tensor(
                    sq(B["Df"], i, j2 * 512, 512), sq(B["Df"], i, j2 * 512, 512),
                    2.0, ps, ALU.mult, ALU.add), inc="v")
            emit_mm_sq(ctx, B["DTb"], B["Db"], evac, wait_t=[("v", tdone)])
            dready = ctx.now("v")
        if g == 0:
            ctx.wait("vector", "v", dready)
            ctx.op("vector", lambda e: e.tensor_copy(B["Rf"][:], B["Df"][:]), inc="v")
        else:
            ctx.wait("scalar", "v", dready)
            ctx.op("scalar", lambda e: e.copy(B["Db"][:], B["Df"][:]))
            ctx.op("scalar", lambda e: e.copy(B["RTb"][:], B["RTf"][:]), inc="s")
            casts = ctx.now("s")

            def evac(ctx, ps, i, j2, last):
                ctx.op("vector", lambda e: e.scalar_tensor_tensor(
                    sq(B["Rf"], i, j2 * 512, 512), sq(B["Rf"], i, j2 * 512, 512),
                    1.0, ps, ALU.mult, ALU.add), inc="v")
            emit_mm_sq(ctx, B["RTb"], B["Db"], evac, wait_t=[("s", casts)])
            ctx.wait("vector", "v", ctx.now("v"))
            ctx.op("vector", lambda e: e.tensor_add(B["Rf"][:], B["Rf"][:], B["Df"][:]), inc="v")
        rdone = ctx.now("v")
        if g < 3:
            emit_transpose_sq(ctx, B["Rf"], B["RTf"], f32=True, wait_t=[("v", rdone)])
    ctx.wait("sync", "v", ctx.now("v"))
    ctx.dma("sync", Dm["rotD_d"][:].rearrange("(c p) n -> p c n", c=C),
            B["Rf"][:].rearrange("p (c n) -> p c n", c=C), stream="sout")
    ctx.wait_dma("sync", "sout")


P1_BUFS = dict(
    gensf=([128, C * 1024], F32), Df=([128, C * 1024], F32),
    Rf=([128, C * 1024], F32), RTf=([128, C * 1024], F32),
    Ab=([128, C * 1024], BF16), Db=([128, C * 1024], BF16),
    DTb=([128, C * 1024], BF16), RTb=([128, C * 1024], BF16),
    idt_b=([128, 128], BF16), idt_f=([128, 128], F32),
)

LAUNCH_EXPM_DRAMS = dict(
    f0_gens=([4, D, D], F32, "in"),
    idt=([128, 128], BF16, "in"),
    idtf=([128, 128], F32, "in"),
    rotD_d=([D, D], F32, "out"),
)


class WStream:

    def __init__(self, ctx, bufnames, queue="gpsimd"):
        self.ctx = ctx
        self.bufs = list(bufnames)
        self.queue = queue
        self.marks = {b: 0 for b in self.bufs}
        self.i = 0

    def load(self, out_view_fn, in_ap):
        ctx = self.ctx
        b = self.bufs[self.i % len(self.bufs)]
        self.i += 1
        ctx.wait(self.queue, "t", self.marks[b])
        ctx.dma(self.queue, out_view_fn(ctx.bufs[b]), in_ap, stream="ws_" + b)
        return b

    def use(self, b):
        self.ctx.wait_dma("tensor", "ws_" + b)
        return self.ctx.bufs[b]

    def done(self, b):
        self.marks[b] = self.ctx.now("t")


def emit_proj(ctx, nj, nk, lhsT_fn, rhs_fn, evac, banks=("pb0", "pb1"), wait_first=None, nfree=TT):
    prev = {b: None for b in banks}
    last = None
    for j in range(nj):
        bank = ctx.bufs[banks[j % len(banks)]]
        if prev[banks[j % len(banks)]] is not None:
            sem, cnt = prev[banks[j % len(banks)]]
            ctx.wait("tensor", sem, cnt)
        if wait_first is not None and j == 0:
            for sem, val in wait_first:
                if sem.startswith("q:"):
                    ctx.wait_dma("tensor", sem[2:])
                else:
                    ctx.wait("tensor", sem, val)
        for k in range(nk):
            ctx.op("tensor", lambda e, k=k, j=j, bank=bank: e.matmul(
                bank[:, 0:nfree], lhsT_fn(k, j), rhs_fn(k),
                start=(k == 0), stop=(k == nk - 1)), inc="t" if k == nk - 1 else None)
        tm = ctx.now("t")
        last = evac(ctx, bank[:, 0:nfree], j, tm)
        prev[banks[j % len(banks)]] = last
    return last


def _ew_evac(ctx, engine, fn, tmark):
    ctx.wait(engine, "t", tmark)
    sem = {"vector": "v", "scalar": "s", "gpsimd": "g"}[engine]
    ctx.op(engine, fn, inc=sem)
    return (sem, ctx.now(sem))


def emit_p2(ctx):
    B, Dm = ctx.bufs, ctx.dram
    ctx.dma("sync", B["wfull"][0:8, :], Dm["wfull_d"][:], stream="cst")
    ctx.dma("sync", B["ind8"][:], Dm["ind8"][:], stream="cst")
    ctx.wait_dma("scalar", "cst")
    ctx.op("scalar", lambda e: e.copy(B["wfullb"][0:8, :], B["wfull"][0:8, :]), inc="s")
    wfb = ctx.now("s")
    for e_i in range(6):
        ctx.wait("gpsimd", "t", ctx.now("t"))
        ctx.dma("gpsimd", B["ew1b"][:].rearrange("p (c n) -> p c n", c=C),
                Dm["f0_ew1"][e_i].rearrange("(c p) n -> p c n", c=C), stream="wup")
        for half in range(2):
            ctx.dma("gpsimd", B["ew2b"][:, half * 8 * 1024:(half + 1) * 8 * 1024].rearrange("p (c n) -> p c n", c=8),
                    Dm["f0_ew2"][e_i, half * 1024:(half + 1) * 1024].rearrange("(c p) n -> p c n", c=8), stream="wup")
        ctx.dma("sync", B["eb1t"][:], Dm["f0_eb1c"][e_i], stream="wups")
        for t in range(NTILES):
            t0 = t * TT
            ctx.wait("sync", "t", ctx.now("t"))
            ctx.dma("sync", tile_view(B["xnb"], TT), fm_dram_tile(Dm["xn2b_d"], t0, TT), stream="xnb")
            ctx.wait_dma("tensor", "xnb")
            if t == 0:
                ctx.wait_dma("tensor", "wup")
                ctx.wait_dma("tensor", "wups")
                ctx.wait("tensor", "s", wfb)
            ctx.wait("tensor", "v", ctx.now("v"))
            ctx.op("tensor", lambda e: e.matmul(
                B["pb4"][:, 0:TT], B["ind8"][:, e_i * 128:(e_i + 1) * 128],
                B["wfullb"][0:8, t0:t0 + TT], start=True, stop=True), inc="t")
            wbm = ctx.now("t")
            ctx.wait("vector", "t", wbm)
            ctx.op("vector", lambda e: e.tensor_copy(B["wbs"][:, 0:TT], B["pb4"][:, 0:TT]), inc="v")
            wbs_done = ctx.now("v")
            def lh(k, j):
                return B["ew1b"][:, k * 2048 + j * 128: k * 2048 + (j + 1) * 128]
            def rh(k):
                return ch(B["xnb"], k)
            def evh(ctx, ps, j, tm):
                ctx.wait_dma("scalar", "wups")
                return _ew_evac(ctx, "scalar", lambda e, ps=ps, j=j: e.activation(
                    B["he"][:, j * TT:(j + 1) * TT], ps, AF.Gelu, bias=B["eb1t"][:, j:j + 1]), tm)
            hsem, hcnt = emit_proj(ctx, 16, C, lh, rh, evh)
            if e_i > 0:
                ctx.dma("sync", tile_view(B["ntile"], TT), fm_dram_tile(Dm["nacc_d"], t0, TT), stream="nacc")
                ctx.wait_dma("vector", "nacc")

            def lo(k, j):
                return B["ew2b"][:, k * 1024 + j * 128: k * 1024 + (j + 1) * 128]
            def ro(k):
                return B["he"][:, k * TT:(k + 1) * TT]
            def evo(ctx, ps, j, tm):
                ctx.wait("vector", "t", tm)
                if e_i == 0:
                    ctx.op("vector", lambda e, ps=ps, j=j: e.tensor_mul(
                        ch(B["ntile"], j), ps, B["wbs"][:, 0:TT]), inc="v")
                else:
                    ctx.op("vector", lambda e, ps=ps: e.tensor_mul(B["tmp5"][:, 0:TT], ps, B["wbs"][:, 0:TT]))
                    ctx.op("vector", lambda e, j=j: e.tensor_add(
                        ch(B["ntile"], j), ch(B["ntile"], j), B["tmp5"][:, 0:TT]), inc="v")
                return ("v", ctx.now("v"))
            emit_proj(ctx, C, 16, lo, ro, evo, banks=("pb2", "pb3"),
                      wait_first=[(hsem, hcnt), ("v", wbs_done)])
            ctx.wait("sync", "v", ctx.now("v"))
            ctx.dma("sync", fm_dram_tile(Dm["nacc_d"], t0, TT), tile_view(B["ntile"], TT), stream="nacc")


def emit_p2rot(ctx):
    B, Dm = ctx.bufs, ctx.dram
    ctx.dma("sync", B["wfull"][0:8, :], Dm["wfull_d"][:], stream="cst2")
    ctx.wait_dma("scalar", "cst2")
    ctx.op("scalar", lambda e: e.copy(B["wfullb"][0:8, :], B["wfull"][0:8, :]), inc="s")
    ctx.dma("gpsimd", B["eb2b"][:], Dm["f0_eb2p"][:], stream="cst2g")
    ctx.dma("gpsimd", B["rotDb"][:].rearrange("p (c n) -> p c n", c=C),
            Dm["rotD_d"][:].rearrange("(c p) n -> p c n", c=C), stream="cst2g")
    for t in range(NTILES):
        t0 = t * TT
        ctx.wait("sync", "s", ctx.now("s"))
        ctx.dma("sync", tile_view(B["ntile"], TT), fm_dram_tile(Dm["nacc_d"], t0, TT), stream="nacc")
        ctx.wait_dma("tensor", "nacc")
        if t == 0:
            ctx.wait_dma("tensor", "cst2")
            ctx.wait_dma("tensor", "cst2g")
            ctx.wait("tensor", "s", ctx.now("s"))

        def l1(k, j):
            return B["eb2b"][0:8, j * 128:(j + 1) * 128]
        def r1(k):
            return B["wfullb"][0:8, t0:t0 + TT]
        def ev1(ctx, ps, j, tm):
            return _ew_evac(ctx, "vector", lambda e, ps=ps, j=j: e.tensor_add(
                ch(B["ntile"], j), ch(B["ntile"], j), ps), tm)
        emit_proj(ctx, C, 1, l1, r1, ev1)
        nadd = ctx.now("v")
        ctx.wait("scalar", "v", nadd)
        ctx.op("scalar", lambda e: e.copy(B["nb"][:], B["ntile"][:]), inc="s")
        nbc = ctx.now("s")

        def l2(k, j):
            return B["rotDb"][:, k * 1024 + j * 128: k * 1024 + (j + 1) * 128]
        def r2(k):
            return ch(B["nb"], k)
        def ev2(ctx, ps, j, tm):
            return _ew_evac(ctx, "vector", lambda e, ps=ps, j=j: e.tensor_add(
                ch(B["ntile"], j), ch(B["ntile"], j), ps), tm)
        emit_proj(ctx, C, C, l2, r2, ev2, banks=("pb2", "pb3"), wait_first=[("s", nbc)])
        ctx.wait("sync", "v", ctx.now("v"))
        ctx.dma("sync", fm_dram_tile(Dm["nacc_d"], t0, TT), tile_view(B["ntile"], TT), stream="nacc")
    ctx.wait_dma("sync", "nacc")


P2_BUFS = dict(
    wfull=([8, NT], F32), wfullb=([8, NT], BF16), ind8=([8, 8 * 128], BF16),
    ew1b=([128, C * 2048], BF16), ew2b=([128, 16 * 1024], BF16),
    eb1t=([128, 16], F32), eb2b=([8, 1024], BF16),
    xnb=([128, C * TT], BF16), he=([128, 16 * TT], BF16),
    ntile=([128, C * TT], F32), nb=([128, C * TT], BF16),
    tmp5=([128, TT], F32), wbs=([128, TT], F32),
    rotDb=([128, C * 1024], BF16),
)


P2ROT_BUFS = dict(
    wfull=([8, NT], F32), wfullb=([8, NT], BF16), eb2b=([8, 1024], BF16),
    rotDb=([128, C * 1024], BF16), ntile=([128, C * TT], F32), nb=([128, C * TT], BF16),
)


def emit_p3(ctx):
    B, Dm = ctx.bufs, ctx.dram
    ctx.dma("sync", B["ones_rowf"][:], Dm["ones128f"][:], stream="cst3")
    ctx.dma("sync", B["gcwa"][:], Dm["f0_gcwa"][:], stream="cst3")
    ctx.dma("sync", B["gcwb"][:], Dm["f0_gcwb"][:], stream="cst3")
    ctx.dma("sync", B["gcb"][:], Dm["f0_gcb"][:], stream="cst3")
    ctx.dma("sync", B["ab1c"][:], Dm["f0_ab1c"][:], stream="cst3")
    ctx.dma("sync", B["ab2c"][:], Dm["f0_ab2c"][:], stream="cst3")
    for t in range(NTILES):
        t0 = t * TT
        ctx.wait("sync", "s", ctx.now("s"))
        ctx.wait("sync", "v", ctx.now("v"))
        ctx.wait("sync", "t", ctx.now("t"))
        ctx.dma("sync", tile_view(B["xt"], TT), fm_dram_tile(Dm["x_fm"], t0, TT), stream="xt")
        ctx.dma("sync", tile_view(B["xnb"], TT), fm_dram_tile(Dm["xn2b_d"], t0, TT), stream="xt")
        ctx.dma("sync", tile_view(B["ntile"], TT), fm_dram_tile(Dm["nacc_d"], t0, TT), stream="xt")
        ctx.dma("sync", B["murt"][:], Dm["murow_d"][0:1, t0:t0 + TT], stream="xt")
        ctx.dma("sync", B["rstdt"][:], Dm["rstdrow_d"][0:1, t0:t0 + TT], stream="xt")
        for j2 in range(4):
            ctx.wait("gpsimd", "t", ctx.now("t"))
            ctx.dma("gpsimd", B["aw1s"][:].rearrange("p (c n) -> p c n", c=C),
                    Dm["f0_aw1"][:, j2 * 1024:(j2 + 1) * 1024].rearrange("(c p) n -> p c n", c=C),
                    stream="aw")
            ctx.dma("gpsimd", B["aw2s"][:].rearrange("p (c n) -> p c n", c=C),
                    Dm["f0_aw2"][j2 * 1024:(j2 + 1) * 1024, :].rearrange("(c p) n -> p c n", c=C),
                    stream="aw")

            def lh(k, j):
                return B["aw1s"][:, k * 1024 + j * 128:k * 1024 + (j + 1) * 128]
            def rh(k):
                return ch(B["xnb"], k)
            def evh(ctx, ps, j, tm, j2=j2):
                return _ew_evac(ctx, "scalar", lambda e, ps=ps, j=j: e.activation(
                    B["h1"][:, j * TT:(j + 1) * TT], ps, AF.Gelu,
                    bias=B["ab1c"][:, j2 * 8 + j:j2 * 8 + j + 1]), tm)
            wf = [("q:aw", 0), ("q:xt", 0)] if True else None
            hsem, hcnt = emit_proj(ctx, 8, C, lh, rh, evh, wait_first=wf)

            def lo(k, j):
                return B["aw2s"][:, k * 1024 + j * 128:k * 1024 + (j + 1) * 128]
            def ro(k):
                return B["h1"][:, k * TT:(k + 1) * TT]
            def evo(ctx, ps, j, tm, j2=j2):
                ctx.wait("vector", "t", tm)
                if j2 == 0:
                    ctx.op("vector", lambda e, ps=ps, j=j: e.tensor_scalar_add(
                        ch(B["abel"], j), ps, B["ab2c"][:, j:j + 1]), inc="v")
                else:
                    ctx.op("vector", lambda e, ps=ps, j=j: e.tensor_add(
                        ch(B["abel"], j), ch(B["abel"], j), ps), inc="v")
                return ("v", ctx.now("v"))
            emit_proj(ctx, C, 8, lo, ro, evo, banks=("pb2", "pb3"), wait_first=[(hsem, hcnt)])
        abel_done = ctx.now("v")
        ctx.wait("tensor", "v", abel_done)
        ctx.wait_dma("tensor", "cst3")
        ctx.wait("tensor", "s", ctx.now("s"))
        for c in range(C):
            ctx.op("tensor", lambda e, c=c: e.matmul(
                B["pb4"][0:1, 0:TT], B["gcwa"][:, c:c + 1], ch(B["abel"], c),
                start=(c == 0), stop=False))
        for c in range(C):
            ctx.op("tensor", lambda e, c=c: e.matmul(
                B["pb4"][0:1, 0:TT], B["gcwb"][:, c:c + 1], ch(B["ntile"], c),
                start=False, stop=(c == C - 1)), inc="t" if c == C - 1 else None)
        gm = ctx.now("t")
        ctx.wait("scalar", "t", gm)
        ctx.op("scalar", lambda e: e.activation(B["r3"][0:1, :], B["pb4"][0:1, 0:TT],
                                                AF.Sigmoid, bias=B["gcb"][0:1, 0:1]), inc="s")
        gr = ctx.now("s")
        ctx.wait("tensor", "s", gr)
        ctx.wait("tensor", "v", ctx.now("v"))
        ctx.op("tensor", lambda e: e.matmul(B["pb5"][:, 0:TT], B["ones_rowf"][:], B["r3"][0:1, :], start=True, stop=True))
        ctx.op("tensor", lambda e: e.matmul(B["pb4"][:, 0:TT], B["ones_rowf"][:], B["murt"][0:1, :], start=True, stop=True))
        ctx.op("tensor", lambda e: e.matmul(B["pb6"][:, 0:TT], B["ones_rowf"][:], B["rstdt"][0:1, :], start=True, stop=True), inc="t")
        bc = ctx.now("t")
        ctx.wait("vector", "t", bc)
        for c in range(C):
            ctx.op("vector", lambda e, c=c: e.tensor_sub(ch(B["t2"], c), ch(B["xt"], c), B["pb4"][:, 0:TT]))
            ctx.op("vector", lambda e, c=c: e.tensor_mul(ch(B["t2"], c), ch(B["t2"], c), B["pb6"][:, 0:TT]))
            ctx.op("vector", lambda e, c=c: e.tensor_sub(B["tmp5"][:, 0:TT], ch(B["ntile"], c), ch(B["abel"], c)))
            ctx.op("vector", lambda e, c=c: e.tensor_mul(B["tmp5"][:, 0:TT], B["tmp5"][:, 0:TT], B["pb5"][:, 0:TT]))
            ctx.op("vector", lambda e, c=c: e.tensor_add(B["tmp5"][:, 0:TT], B["tmp5"][:, 0:TT], ch(B["abel"], c)))
            ctx.op("vector", lambda e, c=c: e.tensor_add(ch(B["t2"], c), ch(B["t2"], c), B["tmp5"][:, 0:TT]),
                   inc="v" if c == C - 1 else None)
        mix_done = ctx.now("v")
        ctx.wait("scalar", "v", mix_done)
        ctx.wait_dma("scalar", "gst")
        ctx.op("scalar", lambda e: e.activation(B["gt"][:], B["t2"][:], AF.Silu), inc="s")
        g_done = ctx.now("s")
        ctx.wait("sync", "s", g_done)
        ctx.dma("sync", fm_dram_tile(Dm["g_scr"], t0, TT), tile_view(B["gt"], TT), stream="gst")
    ctx.wait_dma("sync", "gst")


P3_BUFS = dict(
    xt=([128, C * TT], F32), t2=([128, C * TT], F32), xnb=([128, C * TT], BF16),
    ntile=([128, C * TT], F32), abel=([128, C * TT], F32), gt=([128, C * TT], F32),
    h1=([128, 8 * TT], BF16), aw1s=([128, C * 1024], BF16), aw2s=([128, C * 1024], BF16),
    murt=([1, TT], F32), rstdt=([1, TT], F32), r3=([1, TT], F32),
    gcwa=([128, 8], F32), gcwb=([128, 8], F32), gcb=([1, 1], F32),
    ab1c=([128, 32], F32), ab2c=([128, 8], F32), tmp5=([128, TT], F32),
    ones_rowf=([1, 128], F32),
)

LAUNCH_B_DRAMS = dict(
    x_fm=([D, NT], F32, "in"),
    xn2b_d=([D, NT], BF16, "in"),
    wfull_d=([8, NT], F32, "in"),
    rotD_d=([D, D], F32, "in"),
    murow_d=([1, NT], F32, "in"),
    rstdrow_d=([1, NT], F32, "in"),
    f0_ew1=([6, D, 2048], F32, "in"),
    f0_ew2=([6, 2048, D], F32, "in"),
    f0_eb1c=([6, 128, 16], F32, "in"),
    f0_eb2p=([8, 1024], F32, "in"),
    f0_aw1=([D, 4 * D], F32, "in"),
    f0_aw2=([4 * D, D], F32, "in"),
    f0_ab1c=([128, 32], F32, "in"),
    f0_ab2c=([128, 8], F32, "in"),
    f0_gcwa=([128, 8], F32, "in"),
    f0_gcwb=([128, 8], F32, "in"),
    f0_gcb=([1, 1], F32, "in"),
    ind8=([8, 8 * 128], BF16, "in"),
    ones128f=([1, 128], F32, "in"),
    nacc_d=([D, NT], F32, "internal"),
    g_scr=([D, NT], F32, "out"),
)


def _ln_stats_rows(ctx, xbf, sqbf, murow_ap, rstdrow_ap, wait_pre, banks=("pb0", "pb1")):
    B = ctx.bufs
    for sem, val in wait_pre:
        if sem.startswith("q:"):
            ctx.wait_dma("tensor", sem[2:])
        else:
            ctx.wait("tensor", sem, val)
    b0, b1 = ctx.bufs[banks[0]], ctx.bufs[banks[1]]
    for c in range(C):
        ctx.op("tensor", lambda e, c=c: e.matmul(
            b0[0:1, :], B["ones_col"][:], ch(xbf, c), start=(c == 0), stop=(c == C - 1)))
        ctx.op("tensor", lambda e, c=c: e.matmul(
            b1[0:1, :], B["ones_col"][:], ch(sqbf, c), start=(c == 0), stop=(c == C - 1)),
            inc="t" if c == C - 1 else None)
    sums_done = ctx.now("t")
    ctx.wait("vector", "t", sums_done)
    ctx.op("vector", lambda e: e.tensor_scalar_mul(murow_ap, b0[0:1, :], 1.0 / D))
    ctx.op("vector", lambda e: e.tensor_scalar_mul(B["r1"][0:1, :], b1[0:1, :], 1.0 / D))
    ctx.op("vector", lambda e: e.tensor_mul(B["r2"][0:1, :], murow_ap, murow_ap))
    ctx.op("vector", lambda e: e.tensor_sub(B["r1"][0:1, :], B["r1"][0:1, :], B["r2"][0:1, :]), inc="v")
    var_done = ctx.now("v")
    ctx.wait("scalar", "v", var_done)
    ctx.op("scalar", lambda e: e.activation(B["r2"][0:1, :], B["r1"][0:1, :], AF.Ln, bias=kc(ctx, EPS)))
    ctx.op("scalar", lambda e: e.activation(rstdrow_ap, B["r2"][0:1, :], AF.Exp, scale=-0.5), inc="s")
    return ctx.now("s"), sums_done


def _bcast2(ctx, murow_ap, rstdrow_ap, rstd_done, free_mark, banks=("pb2", "pb3")):
    B = ctx.bufs
    ctx.wait("tensor", "s", rstd_done)
    ctx.wait("tensor", "v", free_mark)
    ctx.op("tensor", lambda e: e.matmul(ctx.bufs[banks[0]][:, :], B["ones_rowf"][:], murow_ap, start=True, stop=True))
    ctx.op("tensor", lambda e: e.matmul(ctx.bufs[banks[1]][:, :], B["ones_rowf"][:], rstdrow_ap, start=True, stop=True), inc="t")
    bc = ctx.now("t")
    ctx.wait("vector", "t", bc)
    return bc


def emit_p4_init(ctx):
    B, Dm = ctx.bufs, ctx.dram
    ctx.dma("sync", B["ones_rowf"][:], Dm["ones128f"][:], stream="cst4")
    for t in range(NTILES):
        t0 = t * TT
        ctx.wait("sync", "v", ctx.now("v"))
        ctx.dma("sync", tile_view(B["xt"], TT), fm_dram_tile(Dm["x_fm"], t0, TT), stream="xt")
        ctx.dma("sync", tile_view(B["pe"], TT), fm_dram_tile(Dm["posemb"], t0, TT), stream="xt")
        ctx.dma("sync", B["murt"][:], Dm["murow_d"][0:1, t0:t0 + TT], stream="xt")
        ctx.dma("sync", B["rstdt"][:], Dm["rstdrow_d"][0:1, t0:t0 + TT], stream="xt")
        ctx.wait_dma("tensor", "xt")
        ctx.wait_dma("tensor", "cst4")
        ctx.wait("tensor", "v", ctx.now("v"))
        ctx.op("tensor", lambda e: e.matmul(B["pb2"][:, :], B["ones_rowf"][:], B["murt"][0:1, :], start=True, stop=True))
        ctx.op("tensor", lambda e: e.matmul(B["pb3"][:, :], B["ones_rowf"][:], B["rstdt"][0:1, :], start=True, stop=True), inc="t")
        bc = ctx.now("t")
        ctx.wait("vector", "t", bc)
        for c in range(C):
            yv = B["y"][:, c * NT + t0: c * NT + t0 + TT]
            ctx.op("vector", lambda e, c=c, yv=yv: e.tensor_sub(B["tmp5"][:, 0:TT], ch(B["xt"], c), B["pb2"][:, :]))
            ctx.op("vector", lambda e, c=c, yv=yv: e.tensor_mul(B["tmp5"][:, 0:TT], B["tmp5"][:, 0:TT], B["pb3"][:, :]))
            ctx.op("vector", lambda e, c=c, yv=yv: e.scalar_tensor_tensor(
                yv, B["tmp5"][:, 0:TT], 32.0, ch(B["pe"], c), ALU.mult, ALU.add),
                inc="v" if c == C - 1 else None)


P4I_BUFS = dict(
    xt=([128, C * TT], F32), pe=([128, C * TT], F32), tmp5=([128, TT], F32),
    murt=([1, TT], F32), rstdt=([1, TT], F32),
    ones_rowf=([1, 128], F32), r1=([1, TT], F32), r2=([1, TT], F32),
    ones_col=([128, 1], BF16),
)


def emit_p4_attA(ctx, l):
    B, Dm = ctx.bufs, ctx.dram
    ctx.dma("sync", B["ones_col"][:], Dm["ones_col"][:], stream="cstA")
    ctx.dma("sync", B["ones_rowf"][:], Dm["ones128f"][:], stream="cstA")
    ctx.dma("sync", B["ln1s"][:], Dm["l0_ln1sc"][l], stream="cstA")
    ctx.dma("sync", B["ln1b"][:], Dm["l0_ln1bc"][l], stream="cstA")
    ctx.dma("gpsimd", B["wkb"][:].rearrange("p (c n) -> p c n", c=C),
            Dm[f"l0_Wk{l}"][:].rearrange("(c p) n -> p c n", c=C), stream="wkv")
    ctx.dma("gpsimd", B["wvb"][:].rearrange("p (c n) -> p c n", c=C),
            Dm[f"l0_Wv{l}"][:].rearrange("(c p) n -> p c n", c=C), stream="wkv")
    ctx.dma("gpsimd", B["bkr"][:], Dm["l0_bk"][l:l + 1, :], stream="wkv")
    ctx.dma("gpsimd", B["bvr"][:], Dm["l0_bv"][l:l + 1, :], stream="wkv")
    ctx.dma("gpsimd", B["onesr"][:], Dm["ones128"][:], stream="wkv")
    nmm_kv = 0
    for t in range(NTILES):
        t0 = t * TT
        ctx.wait("scalar", "t", ctx.now("t"))
        ctx.wait("scalar", "s", ctx.now("s"))
        yv = lambda c: B["y"][:, c * NT + t0: c * NT + t0 + TT]
        for c in range(C):
            ctx.op("scalar", lambda e, c=c: e.copy(ch(B["xe"], c), yv(c)))
            ctx.op("scalar", lambda e, c=c: e.activation(ch(B["sq"], c), yv(c), AF.Square),
                   inc="s" if c == C - 1 else None)
        casts = ctx.now("s")
        rstd_done, _ = _ln_stats_rows(ctx, B["xe"], B["sq"], B["mur"][0:1, :], B["rstdr"][0:1, :],
                                      [("s", casts), ("q:cstA", 0)])
        bc = _bcast2(ctx, B["mur"][0:1, :], B["rstdr"][0:1, :], rstd_done, ctx.now("v"))
        ctx.wait_dma("vector", "xes")
        for c in range(C):
            ctx.op("vector", lambda e, c=c: e.tensor_sub(B["tmp5"][:, 0:TT], yv(c), B["pb2"][:, :]))
            ctx.op("vector", lambda e, c=c: e.tensor_mul(B["tmp5"][:, 0:TT], B["tmp5"][:, 0:TT], B["pb3"][:, :]))
            ctx.op("vector", lambda e, c=c: e.tensor_scalar(
                ch(B["xe"], c), B["tmp5"][:, 0:TT], B["ln1s"][:, c:c + 1], B["ln1b"][:, c:c + 1],
                ALU.mult, ALU.add), inc="v" if c == C - 1 else None)
        xe_done = ctx.now("v")
        ctx.wait("sync", "v", xe_done)
        ctx.dma("sync", fm_dram_tile(Dm["xe_scr"], t0, TT), tile_view(B["xe"], TT), stream="xes")
        ctx.wait("tensor", "v", xe_done)
        if t == 0:
            ctx.wait_dma("tensor", "wkv")
        prev_ev = {"pb0": None, "pb1": None}
        for s4 in range(4):
            for half in range(2):
                for W, dst, isk in ((B["wkb"], B["kp"], True), (B["wvb"], B["vt"], False)):
                    bname = "pb0" if isk else "pb1"
                    bank = B[bname]
                    br = B["bkr"] if isk else B["bvr"]
                    if prev_ev[bname] is not None:
                        ctx.wait("tensor", "v", prev_ev[bname])
                    ctx.op("tensor", lambda e, bank=bank, br=br, half=half: e.matmul(
                        bank[:, 0:512], B["onesr"][:], br[0:1, half * 512:(half + 1) * 512],
                        start=True, stop=False))
                    for k in range(C):
                        ctx.op("tensor", lambda e, k=k, s4=s4, half=half, W=W, bank=bank: e.matmul(
                            bank[:, 0:512], ch(B["xe"], k)[:, s4 * 128:(s4 + 1) * 128],
                            W[:, k * 1024 + half * 512: k * 1024 + (half + 1) * 512],
                            start=False, stop=(k == C - 1)), inc="t" if k == C - 1 else None)
                    tm = ctx.now("t")
                    dsl = dst[:, half * 512:(half + 1) * 512]
                    if isk:
                        ctx.wait("vector", "t", tm)
                        ctx.op("vector", lambda e, bank=bank: e.tensor_single_scalar(
                            B["tmp5"][:, 0:512], bank[:, 0:512], 0.0, ALU.min))
                        ctx.op("vector", lambda e, bank=bank: e.tensor_relu(
                            B["tmp6"][:, 0:512], bank[:, 0:512]), inc="v")
                        vm = ctx.now("v")
                        ctx.wait("scalar", "v", vm)
                        ctx.op("scalar", lambda e: e.activation(B["tmp5"][:, 0:512], B["tmp5"][:, 0:512], AF.Exp), inc="s")
                        sm = ctx.now("s")
                        ctx.wait("vector", "s", sm)
                        ctx.op("vector", lambda e, dsl=dsl: e.tensor_add(dsl, B["tmp5"][:, 0:512], B["tmp6"][:, 0:512]), inc="v")
                    else:
                        ctx.wait("vector", "t", tm)
                        ctx.op("vector", lambda e, dsl=dsl, bank=bank: e.tensor_copy(dsl, bank[:, 0:512]), inc="v")
                    prev_ev[bname] = ctx.now("v")
            kvready = ctx.now("v")
            ctx.wait("tensor", "v", kvready)
            for h in range(16):
                pbase, pcol = (h % 2) * 64, (h // 2) * 64
                ctx.op("tensor", lambda e, h=h, pbase=pbase, pcol=pcol: e.matmul(
                    B["pb6"][pbase:pbase + 64, pcol:pcol + 64],
                    B["kp"][:, h * 64:(h + 1) * 64],
                    B["vt"][:, h * 64:(h + 1) * 64],
                    start=True, stop=True), inc="t" if h == 15 else None)
            for h in range(16):
                pbase = 0 if h % 2 == 0 else 64
                pcol = h // 2
                ctx.op("tensor", lambda e, h=h, pbase=pbase, pcol=pcol: e.matmul(
                    B["pb4"][pbase:pbase + 64, pcol:pcol + 1],
                    B["kp"][:, h * 64:(h + 1) * 64],
                    B["ones_col"][:, 0:1],
                    start=True, stop=True), inc="t" if h == 15 else None)
            km = ctx.now("t")
            ctx.wait("vector", "t", km)
            first = (t == 0 and s4 == 0)
            nmm_kv += 1
            if first:
                ctx.op("vector", lambda e: e.tensor_copy(B["kvsb"][:, 0:512], B["pb6"][:, :]))
                ctx.op("vector", lambda e: e.tensor_copy(B["ksum"][:, 0:8], B["pb4"][:, 0:8]), inc="v")
            else:
                ctx.op("vector", lambda e: e.tensor_add(B["kvsb"][:, 0:512], B["kvsb"][:, 0:512], B["pb6"][:, :]))
                ctx.op("vector", lambda e: e.tensor_add(B["ksum"][:, 0:8], B["ksum"][:, 0:8], B["pb4"][:, 0:8]), inc="v")
            kvacc = ctx.now("v")
    ctx.op("vector", lambda e: e.tensor_copy(B["kvpay"][:, 0:512], B["kvsb"][:, 0:512]))
    ctx.op("vector", lambda e: e.tensor_copy(B["kvpay"][:, 512:520], B["ksum"][:, 0:8]), inc="v")
    pay = ctx.now("v")
    ctx.wait("sync", "v", pay)
    emit_exchange(ctx, l, B["kvpay"][:, 0:520], B["kvgot"][:, 0:520], 128 * 520)
    ctx.wait_dma("vector", "xch")
    ctx.op("vector", lambda e: e.tensor_add(B["kvpay"][:, 0:520], B["kvpay"][:, 0:520], B["kvgot"][:, 0:520]), inc="v")
    ksm = ctx.now("v")
    ctx.wait("scalar", "v", ksm)
    ctx.op("scalar", lambda e: e.copy(B["kvb"][:, :], B["kvpay"][:, 0:512]), inc="s")
    ctx.wait("vector", "v", ksm)
    ctx.op("vector", lambda e: e.tensor_copy(B["ksumf"][:, 0:8], B["kvpay"][:, 512:520]), inc="v")
    ctx.wait("sync", "s", ctx.now("s"))
    ctx.wait("sync", "v", ctx.now("v"))


P4A_BUFS = dict(
    sq=([128, C * TT], BF16), xe=([128, C * TT], BF16),
    kp=([128, 1024], BF16), vt=([128, 1024], BF16),
    wkb=([128, C * 1024], BF16), wvb=([128, C * 1024], BF16),
    bkr=([1, 1024], BF16), bvr=([1, 1024], BF16), onesr=([1, 128], BF16),
    ones_col=([128, 1], BF16), ones_rowf=([1, 128], F32),
    ln1s=([128, 8], F32), ln1b=([128, 8], F32),
    mur=([1, TT], F32), rstdr=([1, TT], F32), r1=([1, TT], F32), r2=([1, TT], F32),
    tmp5=([128, TT], F32), tmp6=([128, TT], F32),
    kvpay=([128, 520], F32), kvgot=([128, 520], F32), ksum=([128, 8], F32),
    kvsb=([128, 512], F32),
)


def emit_p4_attB(ctx, l):
    B, Dm = ctx.bufs, ctx.dram
    ctx.dma("sync", B["i16"][:], Dm["i16c"][:], stream="cstB")
    ctx.dma("sync", B["i16T"][:], Dm["i16Tc"][:], stream="cstB")
    ctx.dma("sync", B["boc"][:], Dm["l0_boc"][l], stream="cstB")
    ctx.dma("sync", B["bqc"][:], Dm["l0_bqc"][l], stream="cstB")
    ctx.dma("gpsimd", B["wqb"][:].rearrange("p (c n) -> p c n", c=C),
            Dm[f"l0_Wq{l}"][:].rearrange("(c p) n -> p c n", c=C), stream="wqo")
    ctx.dma("gpsimd", B["wob"][:].rearrange("p (c n) -> p c n", c=C),
            Dm[f"l0_Wo{l}"][:].rearrange("(c p) n -> p c n", c=C), stream="wqo")
    ctx.wait_dma("vector", "cstB")
    for c in range(C):
        ctx.op("vector", lambda e, c=c: e.tensor_scalar_mul(
            B["i16k"][:, c * 16:(c + 1) * 16], B["i16"][:, c * 16:(c + 1) * 16],
            B["ksumf"][:, c:c + 1]), inc="v" if c == C - 1 else None)
    i16k_done = ctx.now("v")
    for t in range(NTILES):
        t0 = t * TT
        ctx.wait("sync", "t", ctx.now("t"))
        ctx.dma("sync", tile_view(B["xe"], TT), fm_dram_tile(Dm["xe_scr"], t0, TT), stream="xeld")
        ctx.wait_dma("tensor", "xeld")
        if t == 0:
            ctx.wait_dma("tensor", "wqo")
            ctx.wait_dma("tensor", "cstB")
            ctx.wait("tensor", "v", i16k_done)
            ctx.wait("tensor", "s", ctx.now("s"))

        def lq(k, j):
            return B["wqb"][:, k * 1024 + j * 128: k * 1024 + (j + 1) * 128]
        def rq(k):
            return ch(B["xe"], k)
        def evq(ctx, ps, j, tm):
            ctx.wait("vector", "t", tm)
            ctx.op("vector", lambda e, ps=ps, j=j: e.tensor_scalar(
                B["tmp5"][:, 0:TT], ps, B["bqc"][:, j:j + 1], 0.0, ALU.add, ALU.min))
            ctx.op("vector", lambda e, ps=ps, j=j: e.tensor_scalar(
                B["tmp6"][:, 0:TT], ps, B["bqc"][:, j:j + 1], 0.0, ALU.add, ALU.max), inc="v")
            vm = ctx.now("v")
            ctx.wait("scalar", "v", vm)
            ctx.op("scalar", lambda e: e.activation(B["tmp5"][:, 0:TT], B["tmp5"][:, 0:TT], AF.Exp), inc="s")
            sm = ctx.now("s")
            ctx.wait("vector", "s", sm)
            ctx.op("vector", lambda e, j=j: e.tensor_add(
                B["qp"][:, j * TT:(j + 1) * TT], B["tmp5"][:, 0:TT], B["tmp6"][:, 0:TT]), inc="v")
            return ("v", ctx.now("v"))
        qsem, qcnt = emit_proj(ctx, C, C, lq, rq, evq, banks=("pb0", "pb1"))
        ctx.wait("tensor", qsem, qcnt)
        ctx.wait("tensor", "s", ctx.now("s"))
        for c in range(C):
            ctx.op("tensor", lambda e, c=c: e.matmul(
                B["pb2"][0:16, 0:TT], B["i16k"][:, c * 16:(c + 1) * 16],
                B["qp"][:, c * TT:(c + 1) * TT], start=(c == 0), stop=(c == C - 1)),
                inc="t" if c == C - 1 else None)
        dm_ = ctx.now("t")
        ctx.wait("vector", "t", dm_)
        ctx.op("vector", lambda e: e.tensor_scalar_add(B["zr"][0:16, :], B["pb2"][0:16, 0:TT], EPS_ATTN))
        ctx.op("vector", lambda e: e.reciprocal(B["zr"][0:16, :], B["zr"][0:16, :]), inc="v")
        zv = ctx.now("v")
        ctx.wait("scalar", "v", zv)
        ctx.op("scalar", lambda e: e.copy(B["zb"][0:16, :], B["zr"][0:16, :]), inc="s")
        zb_done = ctx.now("s")
        ctx.wait("tensor", "s", zb_done)
        for c in range(C):
            ctx.op("tensor", lambda e, c=c: e.matmul(
                B["pb3"][:, 0:TT], B["i16T"][0:16, c * 128:(c + 1) * 128],
                B["zb"][0:16, :], start=True, stop=True), inc="t")
            zm = ctx.now("t")
            ctx.wait("vector", "t", zm)
            ctx.op("vector", lambda e, c=c: e.tensor_mul(
                B["qps"][:, 0:TT], B["qp"][:, c * TT:(c + 1) * TT], B["pb3"][:, 0:TT]), inc="v")
            qs = ctx.now("v")
            ctx.wait("tensor", "v", qs)
            for hh in range(2):
                h = 2 * c + hh
                kb, kcol = (h % 2) * 64, (h // 2) * 64
                ctx.op("tensor", lambda e, hh=hh, h=h, kb=kb, kcol=kcol: e.matmul(
                    B["pb4"][hh * 64:(hh + 1) * 64, 0:TT],
                    B["kvb"][kb:kb + 64, kcol:kcol + 64],
                    B["qps"][hh * 64:(hh + 1) * 64, 0:TT], start=True, stop=True),
                    inc="t" if hh == 1 else None)
            am = ctx.now("t")
            ctx.wait("vector", "t", am)
            ctx.op("vector", lambda e, c=c: e.tensor_copy(
                B["attnb"][:, c * TT:(c + 1) * TT], B["pb4"][:, 0:TT]), inc="v")
        attn_done = ctx.now("v")

        def loo(k, j):
            return B["wob"][:, k * 1024 + j * 128: k * 1024 + (j + 1) * 128]
        def roo(k):
            return B["attnb"][:, k * TT:(k + 1) * TT]
        def evo(ctx, ps, j, tm):
            ctx.wait("vector", "t", tm)
            yv = B["y"][:, j * NT + t0: j * NT + t0 + TT]
            ctx.op("vector", lambda e, ps=ps, yv=yv, j=j: e.scalar_tensor_tensor(
                yv, ps, B["boc"][:, j:j + 1], yv, ALU.add, ALU.add), inc="v")
            return ("v", ctx.now("v"))
        emit_proj(ctx, C, C, loo, roo, evo, banks=("pb5", "pb6"), wait_first=[("v", attn_done)])
    ctx.wait("sync", "v", ctx.now("v"))
    ctx.wait("sync", "s", ctx.now("s"))


P4B_BUFS = dict(
    xe=([128, C * TT], BF16), qp=([128, C * TT], BF16), qps=([128, TT], BF16),
    attnb=([128, C * TT], BF16),
    wqb=([128, C * 1024], BF16), wob=([128, C * 1024], BF16),
    bqc=([128, 8], F32), boc=([128, 8], F32),
    i16=([128, C * 16], BF16), i16k=([128, C * 16], BF16), i16T=([16, C * 128], BF16),
    zr=([16, TT], F32), zr2=([16, TT], F32), zb=([16, TT], BF16), ones16=([16, TT], F32),
    tmp5=([128, TT], F32), tmp6=([128, TT], F32),
)


def emit_p4_ffn(ctx, l):
    B, Dm = ctx.bufs, ctx.dram
    ctx.dma("sync", B["ones_col"][:], Dm["ones_col"][:], stream="cstF")
    ctx.dma("sync", B["ones_rowf"][:], Dm["ones128f"][:], stream="cstF")
    ctx.dma("sync", B["ln2s"][:], Dm["l0_ln2sc"][l], stream="cstF")
    ctx.dma("sync", B["ln2b"][:], Dm["l0_ln2bc"][l], stream="cstF")
    ctx.dma("sync", B["b1c"][:], Dm["l0_b1c"][l], stream="cstF")
    ctx.dma("sync", B["b2c"][:], Dm["l0_b2c"][l], stream="cstF")
    for t in range(NTILES):
        t0 = t * TT
        yv = lambda c: B["y"][:, c * NT + t0: c * NT + t0 + TT]
        ctx.wait("scalar", "t", ctx.now("t"))
        ctx.wait("scalar", "s", ctx.now("s"))
        for c in range(C):
            ctx.op("scalar", lambda e, c=c: e.copy(ch(B["bfX"], c), yv(c)))
        for c in range(C):
            ctx.op("scalar", lambda e, c=c: e.activation(ch(B["h1"], c), yv(c), AF.Square),
                   inc="s" if c == C - 1 else None)
        casts = ctx.now("s")
        rstd_done, _ = _ln_stats_rows(ctx, B["bfX"], B["h1"][:, 0:C * TT], B["mur"][0:1, :], B["rstdr"][0:1, :],
                                      [("s", casts), ("q:cstF", 0)])
        bc = _bcast2(ctx, B["mur"][0:1, :], B["rstdr"][0:1, :], rstd_done, ctx.now("v"))
        for c in range(C):
            ctx.op("vector", lambda e, c=c: e.tensor_sub(B["tmp5"][:, 0:TT], yv(c), B["pb2"][:, :]))
            ctx.op("vector", lambda e, c=c: e.tensor_mul(B["tmp5"][:, 0:TT], B["tmp5"][:, 0:TT], B["pb3"][:, :]))
            ctx.op("vector", lambda e, c=c: e.tensor_scalar(
                ch(B["bfX"], c), B["tmp5"][:, 0:TT], B["ln2s"][:, c:c + 1], B["ln2b"][:, c:c + 1],
                ALU.mult, ALU.add), inc="v" if c == C - 1 else None)
        xf_done = ctx.now("v")
        if t == 0:
            ctx.fws = WStream(ctx, ("fwA", "fwB"))
            ctx.fws_pre = []
        ws = ctx.fws

        def w1ap(j2):
            return Dm[f"l0_W1{l}"][:, j2 * 512:(j2 + 1) * 512].rearrange("(c p) n -> p c n", c=C)
        def w2ap(cc):
            return Dm[f"l0_W2{l}"][:, cc * 128:(cc + 1) * 128].rearrange("(c p) n -> p c n", c=32)
        items = [("w1", j2) for j2 in range(C)] + [("w2", cc) for cc in range(C)]
        toks = list(ctx.fws_pre)
        ctx.fws_pre = []
        depth = 1
        def issue(idx):
            kind, a = items[idx]
            ap = w1ap(a) if kind == "w1" else w2ap(a)
            cdim = C if kind == "w1" else 32
            return ws.load(lambda bb, cdim=cdim: bb[:].rearrange("p (c n) -> p c n", c=cdim), ap)
        while len(toks) < min(depth, len(items)):
            toks.append(issue(len(toks)))
        for idx, (kind, a) in enumerate(items):
            if idx + depth < len(items):
                toks.append(issue(idx + depth))
            wbuf = ws.use(toks[idx])
            if kind == "w1":
                j2 = a

                def lh(k, j, wbuf=wbuf):
                    return wbuf[:, k * 512 + j * 128: k * 512 + (j + 1) * 128]
                def rh(k):
                    return ch(B["bfX"], k)
                def evh(ctx, ps, j, tm, j2=j2):
                    return _ew_evac(ctx, "scalar", lambda e, ps=ps, j=j, j2=j2: e.activation(
                        B["h1"][:, (j2 * 4 + j) * TT:(j2 * 4 + j + 1) * TT], ps, AF.Relu,
                        bias=B["b1c"][:, j2 * 4 + j:j2 * 4 + j + 1]), tm)
                emit_proj(ctx, 4, C, lh, rh, evh, banks=("pb0", "pb1"),
                          wait_first=[("v", xf_done)])
            else:
                cc = a

                def lo(k, j, wbuf=wbuf):
                    return wbuf[:, k * 128:(k + 1) * 128]
                def ro(k):
                    return B["h1"][:, k * TT:(k + 1) * TT]
                def evo(ctx, ps, j, tm, cc=cc):
                    ctx.wait("vector", "t", tm)
                    yvv = B["y"][:, cc * NT + t0: cc * NT + t0 + TT]
                    ctx.op("vector", lambda e, ps=ps, yvv=yvv, cc=cc: e.scalar_tensor_tensor(
                        yvv, ps, B["b2c"][:, cc:cc + 1], yvv, ALU.add, ALU.add), inc="v")
                    return ("v", ctx.now("v"))
                emit_proj(ctx, 1, 32, lo, ro, evo, banks=("pb4", "pb5") if cc % 2 == 0 else ("pb6", "pb5"),
                          wait_first=None)
            ws.done(toks[idx])
        if t + 1 < NTILES:
            ctx.fws_pre = [issue(0)]
    ctx.wait("sync", "v", ctx.now("v"))


P4F_BUFS = dict(
    bfX=([128, C * TT], BF16), h1=([128, 32 * TT], BF16),
    fwA=([128, 4096], BF16), fwB=([128, 4096], BF16),
    ln2s=([128, 8], F32), ln2b=([128, 8], F32), b1c=([128, 32], F32), b2c=([128, 8], F32),
    mur=([1, TT], F32), rstdr=([1, TT], F32), r1=([1, TT], F32), r2=([1, TT], F32),
    tmp5=([128, TT], F32),
    ones_col=([128, 1], BF16), ones_rowf=([1, 128], F32),
)


def emit_p4_fin(ctx):
    B, Dm = ctx.bufs, ctx.dram
    ctx.dma("sync", B["ones_col"][:], Dm["ones_col"][:], stream="cstZ")
    ctx.dma("sync", B["ones_rowf"][:], Dm["ones128f"][:], stream="cstZ")
    ctx.dma("sync", B["lnfs"][:], Dm["l0_lnfsc"][:], stream="cstZ")
    ctx.dma("sync", B["lnfb"][:], Dm["l0_lnfbc"][:], stream="cstZ")
    for t in range(NTILES):
        t0 = t * TT
        yv = lambda c: B["y"][:, c * NT + t0: c * NT + t0 + TT]
        ctx.wait("scalar", "t", ctx.now("t"))
        ctx.wait("scalar", "s", ctx.now("s"))
        for c in range(C):
            ctx.op("scalar", lambda e, c=c: e.copy(ch(B["bfX"], c), yv(c)))
        for c in range(C):
            ctx.op("scalar", lambda e, c=c: e.activation(ch(B["sq"], c), yv(c), AF.Square),
                   inc="s" if c == C - 1 else None)
        casts = ctx.now("s")
        rstd_done, _ = _ln_stats_rows(ctx, B["bfX"], B["sq"], B["mur"][0:1, :], B["rstdr"][0:1, :],
                                      [("s", casts), ("q:cstZ", 0)])
        bc = _bcast2(ctx, B["mur"][0:1, :], B["rstdr"][0:1, :], rstd_done, ctx.now("v"))
        ctx.wait_dma("vector", "sst")
        for c in range(C):
            ctx.op("vector", lambda e, c=c: e.tensor_sub(B["tmp5"][:, 0:TT], yv(c), B["pb2"][:, :]))
            ctx.op("vector", lambda e, c=c: e.tensor_mul(B["tmp5"][:, 0:TT], B["tmp5"][:, 0:TT], B["pb3"][:, :]))
            ctx.op("vector", lambda e, c=c: e.tensor_scalar(
                ch(B["st"], c), B["tmp5"][:, 0:TT], B["lnfs"][:, c:c + 1], B["lnfb"][:, c:c + 1],
                ALU.mult, ALU.add), inc="v" if c == C - 1 else None)
        sdone = ctx.now("v")
        ctx.wait("sync", "v", sdone)
        ctx.dma("sync", fm_dram_tile(Dm["s_scr"], t0, TT), tile_view(B["st"], TT), stream="sst")
    ctx.wait_dma("sync", "sst")


P4Z_BUFS = dict(
    bfX=([128, C * TT], BF16), sq=([128, C * TT], BF16), st=([128, C * TT], F32),
    lnfs=([128, 8], F32), lnfb=([128, 8], F32),
    mur=([1, TT], F32), rstdr=([1, TT], F32), r1=([1, TT], F32), r2=([1, TT], F32),
    tmp5=([128, TT], F32),
    ones_col=([128, 1], BF16), ones_rowf=([1, 128], F32),
)

P4_SHARED = dict(
    y=([128, C * NT], F32), kvb=([128, 512], BF16), ksumf=([128, 8], F32),
)


def launch_c_phases():
    ph = [(emit_xch_init, None), (emit_p4_init, P4I_BUFS)]
    for l in range(4):
        ph.append((emit_p4_attA, P4A_BUFS, dict(l=l)))
        ph.append((emit_p4_attB, P4B_BUFS, dict(l=l)))
        ph.append((emit_p4_ffn, P4F_BUFS, dict(l=l)))
    ph.append((emit_p4_fin, P4Z_BUFS))
    return ph


def launch_c_drams():
    d = dict(
        x_fm=([D, NT], F32, "in"),
        posemb=([D, NT], F32, "in"),
        murow_d=([1, NT], F32, "in"),
        rstdrow_d=([1, NT], F32, "in"),
        ones_col=([128, 1], BF16, "in"),
        ones128=([1, 128], BF16, "in"),
        ones128f=([1, 128], F32, "in"),
        i16c=([128, C * 16], BF16, "in"),
        i16Tc=([16, C * 128], BF16, "in"),
        l0_bk=([4, D], F32, "in"),
        l0_bv=([4, D], F32, "in"),
        l0_bqc=([4, 128, 8], F32, "in"),
        l0_boc=([4, 128, 8], F32, "in"),
        l0_b1c=([4, 128, 32], F32, "in"),
        l0_b2c=([4, 128, 8], F32, "in"),
        l0_ln1sc=([4, 128, 8], F32, "in"),
        l0_ln1bc=([4, 128, 8], F32, "in"),
        l0_ln2sc=([4, 128, 8], F32, "in"),
        l0_ln2bc=([4, 128, 8], F32, "in"),
        l0_lnfsc=([128, 8], F32, "in"),
        l0_lnfbc=([128, 8], F32, "in"),
        xe_scr=([D, NT], BF16, "internal"),
        s_scr=([D, NT], F32, "out"),
    )
    for l in range(4):
        for nm in ("Wq", "Wk", "Wv", "Wo"):
            d[f"l0_{nm}{l}"] = ([D, D], F32, "in")
        d[f"l0_W1{l}"] = ([D, 4 * D], F32, "in")
        d[f"l0_W2{l}"] = ([4 * D, D], F32, "in")
    d.update(exchange_drams())
    return d


NTE = NT + 512


def emit_p5(ctx):
    B, Dm = ctx.bufs, ctx.dram
    for t in range(NTILES):
        t0 = t * TT
        ctx.wait("sync", "v", ctx.now("v"))
        ctx.dma("sync", tile_view(B["xt"], TT), fm_dram_tile(Dm["x_fm"], t0, TT), stream="xt5")
        ctx.dma("sync", tile_view(B["st"], TT), fm_dram_tile(Dm["s_scr"], t0, TT), stream="xt5")
        ctx.dma("sync", tile_view(B["gt"], TT), fm_dram_tile(Dm["g_scr"], t0, TT), stream="xt5")
        ctx.wait_dma("vector", "xt5")
        for c in range(C):
            ctx.op("vector", lambda e, c=c: e.tensor_mul(ch(B["st"], c), ch(B["st"], c), ch(B["gt"], c)))
            ctx.op("vector", lambda e, c=c: e.tensor_add(ch(B["st"], c), ch(B["st"], c), ch(B["xt"], c)),
                   inc="v" if c == C - 1 else None)
        x1done = ctx.now("v")
        ctx.wait("sync", "v", x1done)
        ctx.dma("sync", fm_dram_tile(Dm["x1e"], 256 + t0, TT), tile_view(B["st"], TT), stream="x1st")
    ctx.wait_dma("sync", "x1st")
    ctx.dma("sync", B["hx"][:, 0:2048].rearrange("p (c t) -> p c t", c=C),
            fm_dram_tile(Dm["x1e"], 256, 256), stream="hld")
    ctx.dma("sync", B["hx"][:, 2048:4096].rearrange("p (c t) -> p c t", c=C),
            fm_dram_tile(Dm["x1e"], 256 + NT - 256, 256), stream="hld")
    ctx.wait_dma("sync", "hld")
    emit_exchange(ctx, 4, B["hx"][:, 0:4096], B["hg"][:, 0:4096], 128 * 4096)
    ctx.wait_dma("gpsimd", "xch")
    ctx.op("gpsimd", lambda e: e.memset(B["hz"][:], 0.0), inc="g")
    zs = ctx.now("g")
    ctx.wait("vector", "g", zs)
    ctx.wait_dma("vector", "xch")
    ctx.op("vector", lambda e: e.tensor_copy(B["hm"][:, 0:2048], B["hz"][:]), inc="v")
    hm = ctx.now("v")
    ctx.wait("sync", "v", hm)
    ctx.dma("sync", Dm["hscr"][0].rearrange("(c p) t -> p c t", c=C), B["hm"][:, 0:2048].rearrange("p (c t) -> p c t", c=C), stream="hst")
    ctx.dma("sync", Dm["hscr"][1].rearrange("(c p) t -> p c t", c=C), B["hg"][:, 0:2048].rearrange("p (c t) -> p c t", c=C), stream="hst")
    ctx.dma("sync", Dm["hscr"][2].rearrange("(c p) t -> p c t", c=C), B["hg"][:, 2048:4096].rearrange("p (c t) -> p c t", c=C), stream="hst")
    ctx.wait_dma("sync", "hst")

    S_ = D * 256
    sname = ctx._stream("x1st")
    sem = ctx.sems[sname]

    def margins(e):
        R = ctx.xregs
        hscr_t = ctx.dram["hscr"].tensor
        for br in e.Switch(ctx.side_sv, 2):
            e.dma_start(out=fm_dram_tile(ctx.dram["x1e"], 0, 256),
                        in_=bass.AP(hscr_t, 2 * br * S_, [[256, 128], [256 * 128, 8], [1, 256]])
                        ).then_inc(sem, 16)
            e.dma_start(out=fm_dram_tile(ctx.dram["x1e"], NTE - 256, 256),
                        in_=bass.AP(hscr_t, (1 - br) * S_, [[256, 128], [256 * 128, 8], [1, 256]])
                        ).then_inc(sem, 16)
    ctx.raw("sync", margins)
    ctx.counts[sname] += 32
    ctx.wait_dma("sync", "x1st")


P5_BUFS = dict(
    xt=([128, C * TT], F32), st=([128, C * TT], F32), gt=([128, C * TT], F32),
    hx=([128, 4096], F32), hg=([128, 4096], F32), hz=([128, 2048], F32),
    hm=([128, 2048], F32),
)


def fm_dram_tile_e(d, t0, L, nch=C):
    return d[:, t0:t0 + L].rearrange("(c p) t -> p c t", c=nch)


def emit_p6(ctx):
    B, Dm = ctx.bufs, ctx.dram
    ctx.dma("sync", tile_view(B["ctf"], 8), Dm["f1_cT"][:, :].rearrange("(c p) e -> p c e", c=C), stream="cst6")
    ctx.dma("sync", B["ones_col"][:], Dm["ones_col"][:], stream="cst6")
    ctx.dma("sync", B["ones_rowf"][:], Dm["ones128f"][:], stream="cst6")
    NTILE_E = NTE // TT
    for t in range(NTILE_E):
        t0 = t * TT
        ctx.wait("sync", "t", ctx.now("t"))
        ctx.wait("sync", "v", ctx.now("v"))
        ctx.dma("sync", tile_view(B["xt"], TT), Dm["x1e"][:, t0:t0 + TT].rearrange("(c p) t -> p c t", c=C), stream="xt6")
        ctx.wait_dma("scalar", "xt6")
        ctx.wait("scalar", "v", ctx.now("v"))
        ctx.wait("scalar", "t", ctx.now("t"))
        ctx.op("scalar", lambda e: e.copy(B["xb"][:], B["xt"][:]))
        ctx.op("scalar", lambda e: e.activation(B["sq"][:], B["xt"][:], AF.Square), inc="s")
        sq_done = ctx.now("s")
        rstd_done, _ = _ln_stats_rows(ctx, B["xb"], B["sq"], B["murowB"][0:1, t0:t0 + TT],
                                      B["rstdrowB"][0:1, t0:t0 + TT], [("s", sq_done), ("q:cst6", 0)])
        bc = _bcast2(ctx, B["murowB"][0:1, t0:t0 + TT], B["rstdrowB"][0:1, t0:t0 + TT], rstd_done, ctx.now("v"))
        for c in range(C):
            ctx.op("vector", lambda e, c=c: e.tensor_sub(ch(B["t1"], c), ch(B["xt"], c), B["pb2"][:, :]))
            ctx.op("vector", lambda e, c=c: e.tensor_mul(ch(B["t1"], c), ch(B["t1"], c), B["pb3"][:, :]),
                   inc="v" if c == C - 1 else None)
        xn_done = ctx.now("v")
        ctx.wait("scalar", "v", xn_done)
        ctx.wait_dma("scalar", "sx6")
        ctx.op("scalar", lambda e: e.copy(B["xnb"][:], B["t1"][:]), inc="s")
        xnb_done = ctx.now("s")
        ctx.wait("sync", "s", xnb_done)
        ctx.dma("sync", fm_dram_tile_e(Dm["xn1b_e"], t0, TT), tile_view(B["xnb"], TT), stream="sx6")
    for t in range(NTILES):
        t0 = t * TT
        e0 = 256 + t0
        ctx.wait("sync", "t", ctx.now("t"))
        ctx.wait("sync", "v", ctx.now("v"))
        ctx.wait("sync", "s", ctx.now("s"))
        ctx.dma("sync", tile_view(B["xt"], TT), Dm["x1e"][:, e0:e0 + TT].rearrange("(c p) t -> p c t", c=C), stream="xt6")
        ctx.wait_dma("tensor", "xt6")
        ctx.wait("tensor", "v", ctx.now("v"))
        ctx.wait("tensor", "s", ctx.now("s"))
        ctx.op("tensor", lambda e, e0=e0: e.matmul(B["pb2"][:, :], B["ones_rowf"][:], B["murowB"][0:1, e0:e0 + TT], start=True, stop=True))
        ctx.op("tensor", lambda e, e0=e0: e.matmul(B["pb3"][:, :], B["ones_rowf"][:], B["rstdrowB"][0:1, e0:e0 + TT], start=True, stop=True), inc="t")
        bc = ctx.now("t")
        ctx.wait("vector", "t", bc)
        for c in range(C):
            ctx.op("vector", lambda e, c=c: e.tensor_sub(ch(B["t1"], c), ch(B["xt"], c), B["pb2"][:, :]))
            ctx.op("vector", lambda e, c=c: e.tensor_mul(ch(B["t1"], c), ch(B["t1"], c), B["pb3"][:, :]),
                   inc="v" if c == C - 1 else None)
        xn_done = ctx.now("v")
        ctx.wait("scalar", "v", xn_done)
        ctx.op("scalar", lambda e: e.copy(B["xb"][:], B["t1"][:]))
        ctx.op("scalar", lambda e: e.activation(B["sq"][:], B["t1"][:], AF.Square), inc="s")
        sq2 = ctx.now("s")
        rstd2, _ = _ln_stats_rows(ctx, B["xb"], B["sq"], B["mu2r"][0:1, :], B["rstd2r"][0:1, :],
                                  [("s", sq2), ("v", xn_done)])
        bc2 = _bcast2(ctx, B["mu2r"][0:1, :], B["rstd2r"][0:1, :], rstd2, xn_done)
        for c in range(C):
            ctx.op("vector", lambda e, c=c: e.tensor_sub(ch(B["t2"], c), ch(B["t1"], c), B["pb2"][:, :]))
            ctx.op("vector", lambda e, c=c: e.tensor_mul(ch(B["t2"], c), ch(B["t2"], c), B["pb3"][:, :]),
                   inc="v" if c == C - 1 else None)
        x2done = ctx.now("v")
        ctx.wait("scalar", "v", x2done)
        ctx.wait_dma("scalar", "sx7")
        ctx.op("scalar", lambda e: e.copy(B["xh2b"][:], B["t2"][:]), inc="s")
        xh2c = ctx.now("s")
        ctx.wait("sync", "s", xh2c)
        ctx.dma("sync", fm_dram_tile(Dm["xhat2b_d"], t0, TT), tile_view(B["xh2b"], TT), stream="sx7")
        ctx.wait("tensor", "v", xn_done)
        for c in range(C):
            ctx.op("tensor", lambda e, c=c: e.matmul(
                B["pb4"][0:8, :], B["ctf"][:, c * 8:(c + 1) * 8], ch(B["t1"], c),
                start=(c == 0), stop=(c == C - 1)), inc="t" if c == C - 1 else None)
        am = ctx.now("t")
        ctx.wait("vector", "t", am)
        L32, M32, R32 = B["L32"], B["M32"], B["R32"]
        ctx.op("vector", lambda e: e.tensor_copy(L32[0:8, :], B["pb4"][0:8, :]))
        ctx.op("vector", lambda e: e.tensor_copy(M32[0:8, :], L32[0:8, :]))
        for r in (4, 2, 1):
            ctx.op("vector", lambda e, r=r: e.stream_shuffle(R32[:, :], M32[:, :], mask=[(i + r) % 32 for i in range(32)]))
            ctx.op("vector", lambda e: e.tensor_max(M32[:, :], M32[:, :], R32[:, :]))
        ctx.op("vector", lambda e: e.stream_shuffle(R32[:, :], M32[:, :], mask=[0] * 32))
        ctx.op("vector", lambda e, t0=t0: e.tensor_tensor(
            B["masks"][0:8, t0:t0 + TT], L32[0:8, :], R32[0:8, :], ALU.is_equal), inc="v")
        mk = ctx.now("v")
        ctx.wait("scalar", "v", mk)
        ctx.op("scalar", lambda e, t0=t0: e.activation(
            B["alphar"][0:1, t0:t0 + TT], M32[0:1, :], AF.Sigmoid), inc="s")
    fin_s = ctx.now("s")
    ctx.wait("scalar", "s", fin_s)
    ctx.op("scalar", lambda e: e.copy(B["masksb"][0:8, :], B["masks"][0:8, :]), inc="s")
    ctx.wait("sync", "s", ctx.now("s"))
    ctx.wait("sync", "v", ctx.now("v"))
    ctx.dma("sync", Dm["murowB_d"][:], B["murowB"][0:1, :], stream="so6")
    ctx.dma("sync", Dm["rstdrowB_d"][:], B["rstdrowB"][0:1, :], stream="so6")
    ctx.dma("sync", Dm["masks_d"][:], B["masks"][0:8, :], stream="so6")
    ctx.dma("sync", Dm["masksb_d"][:], B["masksb"][0:8, :], stream="so6")
    ctx.dma("sync", Dm["alphar_d"][:], B["alphar"][0:1, :], stream="so6")
    ctx.wait_dma("sync", "so6")


P6_BUFS = dict(
    xt=([128, C * TT], F32), t1=([128, C * TT], F32), t2=([128, C * TT], F32),
    xb=([128, C * TT], BF16), sq=([128, C * TT], BF16),
    xnb=([128, C * TT], BF16), xh2b=([128, C * TT], BF16),
    ctf=([128, C * 8], F32),
    murowB=([1, NTE], F32), rstdrowB=([1, NTE], F32),
    mu2r=([1, TT], F32), rstd2r=([1, TT], F32),
    masks=([8, NT], F32), masksb=([8, NT], BF16), alphar=([1, NT], F32),
    r1=([1, TT], F32), r2=([1, TT], F32),
    L32=([32, TT], F32), M32=([32, TT], F32), R32=([32, TT], F32),
    ones_col=([128, 1], BF16), ones_rowf=([1, 128], F32),
)


TT7 = 1024
NTILES7 = NT // TT7


def emit_p7(ctx):
    B, Dm = ctx.bufs, ctx.dram
    ctx.dma("sync", B["ind8"][:], Dm["ind8"][:], stream="cst7")
    ctx.dma("sync", B["masksb"][0:8, :], Dm["masksb_d"][:], stream="cst7")
    ctx.dma("sync", B["ones_rowf"][:], Dm["ones128f"][:], stream="cst7")
    ctx.dma("gpsimd", B["b2b"][:], Dm["f1_b2"][:], stream="cst7g")
    ctx.dma("sync", B["lngs"][:], Dm["f1_lngs"][:], stream="cst7")
    ctx.dma("sync", B["lnbs"][:], Dm["f1_lnbs"][:], stream="cst7")
    ctx.dma("sync", B["b1s"][:], Dm["f1_b1s"][:], stream="cst7")
    for t in range(NTILES7):
        t0 = t * TT7
        ctx.wait("sync", "t", ctx.now("t"))
        ctx.wait("sync", "v", ctx.now("v"))
        ctx.wait("sync", "s", ctx.now("s"))
        ctx.dma("sync", tile_view(B["xh2"], TT7), fm_dram_tile(Dm["xhat2b_d"], t0, TT7), stream="x7")
        ctx.dma("sync", B["alr"][0:1, 0:TT7], Dm["alphar_d"][0:1, t0:t0 + TT7], stream="x7")
        for e_i in range(8):
            if e_i > 0:
                ctx.wait("vector", "t", ctx.now("t"))
            else:
                ctx.wait("vector", "s", ctx.now("s"))
            ctx.wait_dma("vector", "x7")
            ctx.wait_dma("vector", "cst7")
            for c in range(C):
                ctx.op("vector", lambda e, c=c, e_i=e_i: e.tensor_scalar(
                    ch(B["xhe"], c, TT7), ch(B["xh2"], c, TT7),
                    B["lngs"][:, e_i * 8 + c:e_i * 8 + c + 1], B["lnbs"][:, e_i * 8 + c:e_i * 8 + c + 1],
                    ALU.mult, ALU.add), inc="v" if c == C - 1 else None)
            xe_done = ctx.now("v")
            ctx.wait("tensor", "v", xe_done)
            for hf in range(2):
                ctx.op("tensor", lambda e, hf=hf, e_i=e_i, t0=t0: e.matmul(
                    B["pb5" if hf == 0 else "pb6"][:, 0:512],
                    B["ind8"][:, e_i * 128:(e_i + 1) * 128],
                    B["masksb"][0:8, t0 + hf * 512: t0 + (hf + 1) * 512],
                    start=True, stop=True), inc="t" if hf == 1 else None)
            mb = ctx.now("t")
            ctx.wait("vector", "t", mb)
            ctx.op("vector", lambda e: e.tensor_copy(B["mbs"][:, 0:512], B["pb5"][:, 0:512]))
            ctx.op("vector", lambda e: e.tensor_copy(B["mbs"][:, 512:1024], B["pb6"][:, 0:512]), inc="v")
            mbv = ctx.now("v")
            if e_i == 0 and t == 0:
                ctx.ws7 = WStream(ctx, ("wsA", "wsB"))
                ctx.ws7_pre = None
            ws = ctx.ws7

            def w1ap(j2, e_i=e_i):
                return Dm["f1_w1"][e_i][:, j2 * 512:(j2 + 1) * 512].rearrange("(c p) n -> p c n", c=C)
            def w2ap(cc, e_i=e_i):
                return Dm["f1_w2"][e_i][:, cc * 128:(cc + 1) * 128].rearrange("(c p) n -> p c n", c=32)
            if ctx.ws7_pre is not None and e_i > 0:
                toks = [ctx.ws7_pre]
                ctx.ws7_pre = None
            else:
                toks = [ws.load(lambda bb: bb[:].rearrange("p (c n) -> p c n", c=C), w1ap(0))]
            for j2 in range(8):
                if j2 + 1 < 8:
                    toks.append(ws.load(lambda bb: bb[:].rearrange("p (c n) -> p c n", c=C), w1ap(j2 + 1)))
                else:
                    toks.append(ws.load(lambda bb: bb[:].rearrange("p (c n) -> p c n", c=32), w2ap(0)))
                wbuf = ws.use(toks[j2])

                def lh(k, j, wbuf=wbuf):
                    return wbuf[:, k * 512 + j * 128: k * 512 + (j + 1) * 128]
                for half in range(2):
                    def rh(k, half=half):
                        return ch(B["xhe"], k, TT7)[:, half * 512:(half + 1) * 512]
                    def evh(ctx, ps, j, tm, j2=j2, half=half, e_i=e_i):
                        return _ew_evac(ctx, "scalar", lambda e, ps=ps, j=j: e.activation(
                            B["h1"][:, (j2 * 4 + j) * TT7 + half * 512:(j2 * 4 + j) * TT7 + (half + 1) * 512],
                            ps, AF.Relu, bias=B["b1s"][:, e_i * 32 + j2 * 4 + j:e_i * 32 + j2 * 4 + j + 1]), tm)
                    emit_proj(ctx, 4, C, lh, rh, evh, banks=("pb0", "pb1"))
                ws.done(toks[j2])
            for cc in range(C):
                if cc + 1 < C:
                    toks.append(ws.load(lambda bb: bb[:].rearrange("p (c n) -> p c n", c=32), w2ap(cc + 1)))
                elif e_i + 1 < 8:
                    toks.append(ws.load(lambda bb: bb[:].rearrange("p (c n) -> p c n", c=C),
                                        Dm["f1_w1"][e_i + 1][:, 0:512].rearrange("(c p) n -> p c n", c=C)))
                    ctx.ws7_pre = toks[-1]
                wbuf = ws.use(toks[8 + cc])
                for half in range(2):
                    def lo(k, j, wbuf=wbuf):
                        return wbuf[:, k * 128:(k + 1) * 128]
                    def ro(k, half=half):
                        return B["h1"][:, k * TT7 + half * 512:k * TT7 + (half + 1) * 512]
                    def evo(ctx, ps, j, tm, cc=cc, half=half, e_i=e_i):
                        ctx.wait("vector", "t", tm)
                        msl = B["mbs"][:, half * 512:(half + 1) * 512]
                        dst = ch(B["ft"], cc, TT7)[:, half * 512:(half + 1) * 512]
                        if e_i == 0:
                            ctx.op("vector", lambda e, ps=ps, dst=dst, msl=msl: e.tensor_mul(
                                dst, ps, msl), inc="v")
                        else:
                            ctx.op("vector", lambda e, ps=ps, msl=msl: e.tensor_mul(
                                B["tmp5"][:, 0:512], ps, msl))
                            ctx.op("vector", lambda e, dst=dst: e.tensor_add(
                                dst, dst, B["tmp5"][:, 0:512]), inc="v")
                        return ("v", ctx.now("v"))
                    emit_proj(ctx, 1, 32, lo, ro, evo, banks=("pb2", "pb3") if half == 0 else ("pb4", "pb3"),
                              wait_first=[("v", mbv)])
                ws.done(toks[8 + cc])
        ctx.dma("sync", tile_view(B["x1t"], TT7), Dm["x1e"][:, 256 + t0:256 + t0 + TT7].rearrange("(c p) t -> p c t", c=C), stream="x7b")
        ctx.dma("sync", B["murt"][0:1, 0:TT7], Dm["murowB_d"][0:1, 256 + t0:256 + t0 + TT7], stream="x7b")
        ctx.dma("sync", B["rstdt"][0:1, 0:TT7], Dm["rstdrowB_d"][0:1, 256 + t0:256 + t0 + TT7], stream="x7b")
        ctx.wait("tensor", "v", ctx.now("v"))
        ctx.wait("tensor", "s", ctx.now("s"))
        ctx.wait_dma("tensor", "cst7g")
        for cc in range(C):
            for half in range(2):
                ctx.wait("tensor", "v", ctx.now("v"))
                ctx.op("tensor", lambda e, cc=cc, half=half, t0=t0: e.matmul(
                    B["pb2"][:, 0:512], B["b2b"][0:8, cc * 128:(cc + 1) * 128],
                    B["masksb"][0:8, t0 + half * 512:t0 + (half + 1) * 512],
                    start=True, stop=True), inc="t")
                b2m = ctx.now("t")
                ctx.wait("vector", "t", b2m)
                dst = ch(B["ft"], cc, TT7)[:, half * 512:(half + 1) * 512]
                ctx.op("vector", lambda e, dst=dst: e.tensor_add(dst, dst, B["pb2"][:, 0:512]), inc="v")
        fb2 = ctx.now("v")
        ctx.wait_dma("tensor", "x7b")
        ctx.wait("tensor", "v", fb2)
        ctx.wait("tensor", "s", ctx.now("s"))
        for half in range(2):
            ctx.op("tensor", lambda e, half=half: e.matmul(
                B["pb5" if half == 0 else "pb6"][:, 0:512], B["ones_rowf"][:],
                B["alr"][0:1, half * 512:(half + 1) * 512], start=True, stop=True))
        ctx.op("tensor", lambda e: e.matmul(B["pb3"][:, 0:512], B["ones_rowf"][:], B["murt"][0:1, 0:512], start=True, stop=True))
        ctx.op("tensor", lambda e: e.matmul(B["pb4"][:, 0:512], B["ones_rowf"][:], B["rstdt"][0:1, 0:512], start=True, stop=True))
        ctx.op("tensor", lambda e: e.matmul(B["pb0"][:, 0:512], B["ones_rowf"][:], B["murt"][0:1, 512:1024], start=True, stop=True))
        ctx.op("tensor", lambda e: e.matmul(B["pb1"][:, 0:512], B["ones_rowf"][:], B["rstdt"][0:1, 512:1024], start=True, stop=True), inc="t")
        bcx = ctx.now("t")
        ctx.wait("vector", "t", bcx)
        for c in range(C):
            for half in range(2):
                mu_b = B["pb3"] if half == 0 else B["pb0"]
                rs_b = B["pb4"] if half == 0 else B["pb1"]
                al_b = B["pb5"] if half == 0 else B["pb6"]
                x1s = ch(B["x1t"], c, TT7)[:, half * 512:(half + 1) * 512]
                fs = ch(B["ft"], c, TT7)[:, half * 512:(half + 1) * 512]
                ctx.op("vector", lambda e, x1s=x1s, mu_b=mu_b: e.tensor_sub(B["tmp5"][:, 0:512], x1s, mu_b[:, 0:512]))
                ctx.op("vector", lambda e, rs_b=rs_b: e.tensor_mul(B["tmp5"][:, 0:512], B["tmp5"][:, 0:512], rs_b[:, 0:512]))
                ctx.op("vector", lambda e, fs=fs, al_b=al_b: e.tensor_mul(fs, fs, al_b[:, 0:512]))
                ctx.op("vector", lambda e, fs=fs: e.tensor_add(fs, fs, B["tmp5"][:, 0:512]),
                       inc="v" if (c == C - 1 and half == 1) else None)
        pre = ctx.now("v")
        ctx.wait("scalar", "v", pre)
        ctx.wait_dma("scalar", "g2st")
        ctx.op("scalar", lambda e: e.activation(B["ft"][:, 0:C * TT7], B["ft"][:, 0:C * TT7], AF.Silu), inc="s")
        g2d = ctx.now("s")
        ctx.wait("sync", "s", g2d)
        ctx.dma("sync", fm_dram_tile(Dm["g2_scr"], t0, TT7), tile_view(B["ft"], TT7), stream="g2st")
    ctx.wait_dma("sync", "g2st")


def ch(buf, c, L=TT):
    return buf[:, c * L:(c + 1) * L]


P7_BUFS = dict(
    xh2=([128, C * TT7], BF16), xhe=([128, C * TT7], BF16),
    h1=([128, 32 * TT7], BF16), wsA=([128, 4096], BF16), wsB=([128, 4096], BF16),
    ft=([128, C * TT7], F32), x1t=([128, C * TT7], F32),
    masksb=([8, NT], BF16), ind8=([8, 8 * 128], BF16), b2b=([8, 1024], BF16),
    lngs=([128, 64], F32), lnbs=([128, 64], F32), b1s=([128, 8 * 32], F32),
    alr=([1, TT7], F32), murt=([1, TT7], F32), rstdt=([1, TT7], F32),
    tmp5=([128, 512], F32), mbs=([128, 1024], BF16), ones_rowf=([1, 128], F32),
)


DBG_P8 = False


def emit_p8a(ctx):
    B, Dm = ctx.bufs, ctx.dram
    for nm, dn in (("wqb", "l1_Wq"), ("wkb", "l1_Wk"), ("wvb", "l1_Wv"), ("wob", "l1_Wo")):
        ctx.dma("gpsimd", B[nm][:].rearrange("p (c n) -> p c n", c=C),
                Dm[dn][:].rearrange("(c p) n -> p c n", c=C), stream="w8")
    ctx.dma("gpsimd", B["bvr"][:], Dm["l1_bvr"][:], stream="w8")
    ctx.dma("gpsimd", B["onesr"][:], Dm["ones128"][:], stream="w8")
    ctx.dma("sync", B["bqc"][:], Dm["l1_bqc"][:], stream="w8s")
    ctx.dma("sync", B["bkc"][:], Dm["l1_bkc"][:], stream="w8s")
    ctx.dma("sync", B["boc"][:], Dm["l1_boc"][:], stream="w8s")
    ctx.dma("sync", B["idt_b"][:], Dm["idt"][:], stream="w8s")
    for t in range(NTILES):
        t0 = t * TT
        ctx.wait("sync", "t", ctx.now("t"))
        ctx.wait("sync", "v", ctx.now("v"))
        ctx.dma("sync", tile_view(B["xkv"], 1024), fm_dram_tile_e(Dm["xn1b_e"], t0, 1024), stream="x8")
        ctx.dma("sync", B["mask"][:], Dm["maskt_d"][t], stream="x8")
        ctx.wait_dma("tensor", "x8")
        if t == 0:
            ctx.wait_dma("tensor", "w8")

        def lk(k, j):
            cc, half = j // 2, j % 2
            return B["wkb"][:, k * 1024 + cc * 128: k * 1024 + (cc + 1) * 128]
        def rk(k, _t=t):
            return None
        prevv = None
        for j in range(16):
            cc, half = j // 2, j % 2
            bank = B["pb0"] if j % 2 == 0 else B["pb1"]
            if j >= 2:
                ctx.wait("tensor", "v", ctx.now("v"))
            for k in range(C):
                ctx.op("tensor", lambda e, k=k, cc=cc, half=half, bank=bank: e.matmul(
                    bank[:, 0:512], B["wkb"][:, k * 1024 + cc * 128: k * 1024 + (cc + 1) * 128],
                    ch(B["xkv"], k, 1024)[:, half * 512:(half + 1) * 512],
                    start=(k == 0), stop=(k == C - 1)), inc="t" if k == C - 1 else None)
            tm = ctx.now("t")
            ctx.wait("vector", "t", tm)
            ctx.op("vector", lambda e, cc=cc, half=half, bank=bank: e.tensor_scalar_add(
                B["kf"][:, cc * 1024 + half * 512: cc * 1024 + (half + 1) * 512],
                bank[:, 0:512], B["bkc"][:, cc:cc + 1]), inc="v")
        for j in range(16):
            sb, half = j // 2, j % 2
            bank = B["pb0"] if j % 2 == 0 else B["pb1"]
            ctx.wait("tensor", "v", ctx.now("v"))
            ctx.op("tensor", lambda e, half=half, bank=bank: e.matmul(
                bank[:, 0:512], B["onesr"][:], B["bvr"][0:1, half * 512:(half + 1) * 512],
                start=True, stop=False))
            for k in range(C):
                ctx.op("tensor", lambda e, k=k, sb=sb, half=half, bank=bank: e.matmul(
                    bank[:, 0:512], ch(B["xkv"], k, 1024)[:, sb * 128:(sb + 1) * 128],
                    B["wvb"][:, k * 1024 + half * 512: k * 1024 + (half + 1) * 512],
                    start=False, stop=(k == C - 1)), inc="t" if k == C - 1 else None)
            tm = ctx.now("t")
            ctx.wait("vector", "t", tm)
            ctx.op("vector", lambda e, sb=sb, half=half, bank=bank: e.tensor_copy(
                B["vt"][:, sb * 1024 + half * 512: sb * 1024 + (half + 1) * 512], bank[:, 0:512]), inc="v")
        for cc in range(C):
            bank = B["pb0"] if cc % 2 == 0 else B["pb1"]
            ctx.wait("tensor", "v", ctx.now("v"))
            for k in range(C):
                ctx.op("tensor", lambda e, k=k, cc=cc, bank=bank: e.matmul(
                    bank[:, 0:512], B["wqb"][:, k * 1024 + cc * 128: k * 1024 + (cc + 1) * 128],
                    ch(B["xkv"], k, 1024)[:, 256:768],
                    start=(k == 0), stop=(k == C - 1)), inc="t" if k == C - 1 else None)
            tm = ctx.now("t")
            ctx.wait("vector", "t", tm)
            ctx.op("vector", lambda e, cc=cc, bank=bank: e.tensor_scalar(
                B["qf"][:, cc * 512:(cc + 1) * 512], bank[:, 0:512],
                B["bqc"][:, cc:cc + 1], 0.125, ALU.add, ALU.mult), inc="v")
        qv = ctx.now("v")
        for qs in range(4):
            for cc in range(C):
                for hh in range(2):
                    h = 2 * cc + hh
                    hb = hh * 64
                    ctx.wait("tensor", "v", ctx.now("v"))
                    ctx.wait("tensor", "s", ctx.now("s"))
                    ctx.op("tensor", lambda e, hb=hb, cc=cc, qs=qs: e.matmul(
                        B["pb0"][:, 0:512],
                        B["qf"][hb:hb + 64, cc * 512 + qs * 128: cc * 512 + (qs + 1) * 128],
                        B["kf"][hb:hb + 64, cc * 1024 + qs * 128: cc * 1024 + qs * 128 + 512],
                        start=True, stop=True))
                    ctx.op("tensor", lambda e, hb=hb, cc=cc, qs=qs: e.matmul(
                        B["pb1"][:, 0:128],
                        B["qf"][hb:hb + 64, cc * 512 + qs * 128: cc * 512 + (qs + 1) * 128],
                        B["kf"][hb:hb + 64, cc * 1024 + qs * 128 + 512: cc * 1024 + qs * 128 + 640],
                        start=True, stop=True), inc="t")
                    sm_ = ctx.now("t")
                    ctx.wait("vector", "t", sm_)
                    ctx.op("vector", lambda e, qs=qs: e.tensor_add(
                        B["sc"][:, 0:512], B["pb0"][:, 0:512], B["mask"][:, qs * 640: qs * 640 + 512]))
                    ctx.op("vector", lambda e, qs=qs: e.tensor_add(
                        B["sc"][:, 512:640], B["pb1"][:, 0:128], B["mask"][:, qs * 640 + 512: (qs + 1) * 640]))
                    ctx.op("vector", lambda e: e.reduce_max(
                        B["rmax"][:, 0:1], B["sc"][:, 0:640], axis=AX, negate=True), inc="v")
                    vm = ctx.now("v")
                    ctx.wait("scalar", "v", vm)
                    ctx.op("scalar", lambda e: e.activation(
                        B["pnf"][:, 0:640], B["sc"][:, 0:640], AF.Exp, bias=B["rmax"][:, 0:1],
                        accum_out=B["rsum"][:, 0:1]), inc="s")
                    em = ctx.now("s")
                    ctx.wait("vector", "s", em)
                    ctx.op("vector", lambda e: e.reciprocal(B["rz"][:, 0:1], B["rsum"][:, 0:1]), inc="v")
                    rzv = ctx.now("v")
                    ctx.wait("scalar", "v", rzv)
                    ctx.op("scalar", lambda e: e.copy(B["rz2"][:, 0:1], B["rz"][:, 0:1]), inc="s")
                    rz2m = ctx.now("s")
                    ctx.wait("vector", "s", rz2m)
                    ctx.op("vector", lambda e: e.tensor_scalar_mul(
                        B["pn"][:, 0:640], B["pnf"][:, 0:640], B["rz2"][:, 0:1]), inc="v")
                    pg = ctx.now("v")
                    ctx.wait("tensor", "v", pg)
                    for j in range(5):
                        ctx.op("tensor", lambda e, j=j: e.transpose(
                            B["pt"][:, j * 128:(j + 1) * 128], B["pn"][:, j * 128:(j + 1) * 128],
                            B["idt_b"][:]), inc="t" if j == 4 else None)
                    ttm = ctx.now("t")
                    ctx.wait("vector", "t", ttm)
                    ctx.op("vector", lambda e: e.tensor_copy(B["pTs"][:, 0:640], B["pt"][:, 0:640]), inc="v")
                    ptv = ctx.now("v")
                    ctx.wait("tensor", "v", ptv)
                    for j in range(5):
                        ctx.op("tensor", lambda e, j=j, qs=qs, h=h, hb=hb: e.matmul(
                            B["pb2"][hb:hb + 64, 0:128],
                            B["vt"][:, (qs + j) * 1024 + h * 64: (qs + j) * 1024 + (h + 1) * 64],
                            B["pTs"][:, j * 128:(j + 1) * 128],
                            start=(j == 0), stop=(j == 4)), inc="t" if j == 4 else None)
                    avm = ctx.now("t")
                    if hh == 1:
                        ctx.wait("vector", "t", avm)
                        ctx.op("vector", lambda e, cc=cc, qs=qs: e.tensor_copy(
                            B["afm"][:, cc * 512 + qs * 128: cc * 512 + (qs + 1) * 128],
                            B["pb2"][:, 0:128]), inc="v")
                    if DBG_P8 and t == NTILES - 1 and qs == 3 and cc == 7 and hh == 1:
                        ctx.wait("sync", "v", ctx.now("v"))
                        ctx.wait("sync", "s", ctx.now("s"))
                        ctx.dma("sync", Dm["sc_dbg"][:], B["sc"][:, 0:640], stream="ast")
                        ctx.dma("sync", Dm["pn_dbg"][:], B["pn"][:, 0:640], stream="ast")
                        ctx.dma("sync", Dm["pts_dbg"][:], B["pTs"][:, 0:640], stream="ast")
                        ctx.dma("sync", Dm["rs_dbg"][:], B["rsum"][:, 0:1], stream="ast")
                        ctx.dma("sync", Dm["rz_dbg"][:], B["rz"][:, 0:1], stream="ast")
        attn_done = ctx.now("v")

        def loo(k, j):
            return B["wob"][:, k * 1024 + j * 128: k * 1024 + (j + 1) * 128]
        def roo(k):
            return B["afm"][:, k * 512:(k + 1) * 512]
        def evo(ctx, ps, j, tm):
            ctx.wait("vector", "t", tm)
            ctx.op("vector", lambda e, ps=ps, j=j: e.tensor_scalar_add(
                ch(B["a_t"], j), ps, B["boc"][:, j:j + 1]), inc="v")
            return ("v", ctx.now("v"))
        emit_proj(ctx, C, C, loo, roo, evo, banks=("pb3", "pb4"), wait_first=[("v", attn_done)])
        ad = ctx.now("v")
        ctx.wait("sync", "v", ad)
        ctx.dma("sync", fm_dram_tile(Dm["a_scr"], t0, TT), tile_view(B["a_t"], TT), stream="ast")
    if DBG_P8:
        ctx.wait("sync", "v", ctx.now("v"))
        ctx.dma("sync", Dm["kf_dbg"][:], B["kf"][:], stream="ast")
        ctx.dma("sync", Dm["vt_dbg"][:], B["vt"][:], stream="ast")
        ctx.dma("sync", Dm["qf_dbg"][:], B["qf"][:], stream="ast")
        ctx.dma("sync", Dm["afm_dbg"][:], B["afm"][:], stream="ast")
    ctx.wait_dma("sync", "ast")


P8A_BUFS = dict(
    xkv=([128, C * 1024], BF16), kf=([128, C * 1024], BF16), vt=([128, 8 * 1024], BF16),
    qf=([128, C * 512], BF16), afm=([128, C * 512], BF16), a_t=([128, C * TT], F32),
    wqb=([128, C * 1024], BF16), wkb=([128, C * 1024], BF16),
    wvb=([128, C * 1024], BF16), wob=([128, C * 1024], BF16),
    bqc=([128, 8], F32), bkc=([128, 8], F32), boc=([128, 8], F32),
    bvr=([1, 1024], BF16), onesr=([1, 128], BF16), idt_b=([128, 128], BF16),
    mask=([128, 4 * 640], F32), sc=([128, 640], F32), pn=([128, 640], BF16), pnf=([128, 640], F32),
    pTs=([128, 640], BF16), rmax=([128, 1], F32), rsum=([128, 1], F32), rz=([128, 1], F32),
    rz2=([128, 1], F32),
)


def emit_p8b(ctx):
    B, Dm = ctx.bufs, ctx.dram
    ctx.dma("sync", B["ones_col"][:], Dm["ones_col"][:], stream="c8b")
    ctx.dma("sync", B["ones_rowf"][:], Dm["ones128f"][:], stream="c8b")
    ctx.dma("sync", B["g1c"][:], Dm["l1_g1c"][:], stream="c8b")
    ctx.dma("sync", B["b1cc"][:], Dm["l1_b1c"][:], stream="c8b")
    ctx.dma("sync", B["g2c"][:], Dm["l1_g2c"][:], stream="c8b")
    ctx.dma("sync", B["b2cc"][:], Dm["l1_b2c"][:], stream="c8b")
    ctx.dma("sync", B["c1c"][:], Dm["l1_c1c"][:], stream="c8b")
    ctx.dma("sync", B["c2c"][:], Dm["l1_c2c"][:], stream="c8b")
    for t in range(NTILES):
        t0 = t * TT
        ctx.wait("sync", "t", ctx.now("t"))
        ctx.wait("sync", "v", ctx.now("v"))
        ctx.wait("sync", "s", ctx.now("s"))
        ctx.dma("sync", tile_view(B["a_t"], TT), fm_dram_tile(Dm["a_scr"], t0, TT), stream="x8b")
        ctx.dma("sync", tile_view(B["x1t"], TT), Dm["x1e"][:, 256 + t0:256 + t0 + TT].rearrange("(c p) t -> p c t", c=C), stream="x8b")
        ctx.dma("sync", B["murt"][:], Dm["murowB_d"][0:1, 256 + t0:256 + t0 + TT], stream="x8b")
        ctx.dma("sync", B["rstdt"][:], Dm["rstdrowB_d"][0:1, 256 + t0:256 + t0 + TT], stream="x8b")
        ctx.dma("sync", tile_view(B["g2t"], TT), fm_dram_tile(Dm["g2_scr"], t0, TT), stream="x8b")
        ctx.wait_dma("tensor", "x8b")
        ctx.wait_dma("tensor", "c8b")
        ctx.wait("tensor", "v", ctx.now("v"))
        ctx.op("tensor", lambda e: e.matmul(B["pb2"][:, :], B["ones_rowf"][:], B["murt"][0:1, :], start=True, stop=True))
        ctx.op("tensor", lambda e: e.matmul(B["pb3"][:, :], B["ones_rowf"][:], B["rstdt"][0:1, :], start=True, stop=True), inc="t")
        bc = ctx.now("t")
        ctx.wait("vector", "t", bc)
        ctx.wait_dma("vector", "x8b")
        for c in range(C):
            ctx.op("vector", lambda e, c=c: e.tensor_sub(B["tmp5"][:, 0:TT], ch(B["x1t"], c), B["pb2"][:, :]))
            ctx.op("vector", lambda e, c=c: e.tensor_mul(B["tmp5"][:, 0:TT], B["tmp5"][:, 0:TT], B["pb3"][:, :]))
            ctx.op("vector", lambda e, c=c: e.tensor_add(ch(B["ht"], c), B["tmp5"][:, 0:TT], ch(B["a_t"], c)),
                   inc="v" if c == C - 1 else None)
        pre_done = ctx.now("v")
        ctx.wait("scalar", "v", pre_done)
        ctx.op("scalar", lambda e: e.copy(B["hb"][:], B["ht"][:]))
        ctx.op("scalar", lambda e: e.activation(B["sq"][:], B["ht"][:], AF.Square), inc="s")
        casts = ctx.now("s")
        rstd_done, _ = _ln_stats_rows(ctx, B["hb"], B["sq"], B["mur"][0:1, :], B["rstdr"][0:1, :],
                                      [("s", casts)])
        bc2 = _bcast2(ctx, B["mur"][0:1, :], B["rstdr"][0:1, :], rstd_done, pre_done)
        for c in range(C):
            ctx.op("vector", lambda e, c=c: e.tensor_sub(B["tmp5"][:, 0:TT], ch(B["ht"], c), B["pb2"][:, :]))
            ctx.op("vector", lambda e, c=c: e.tensor_mul(B["tmp5"][:, 0:TT], B["tmp5"][:, 0:TT], B["pb3"][:, :]))
            ctx.op("vector", lambda e, c=c: e.tensor_scalar(
                ch(B["ht"], c), B["tmp5"][:, 0:TT], B["g1c"][:, c:c + 1], B["b1cc"][:, c:c + 1],
                ALU.mult, ALU.add), inc="v" if c == C - 1 else None)
        h_done = ctx.now("v")
        ctx.wait("scalar", "v", h_done)
        ctx.op("scalar", lambda e: e.copy(B["hb"][:], B["ht"][:]), inc="s")
        hb_done = ctx.now("s")
        for j2 in range(C):
            ctx.wait("gpsimd", "t", ctx.now("t"))
            ctx.dma("gpsimd", B["w1s"][:].rearrange("p (c n) -> p c n", c=C),
                    Dm["l1_W1"][:, j2 * 512:(j2 + 1) * 512].rearrange("(c p) n -> p c n", c=C),
                    stream="w1b")

            def lh(k, j, j2=j2):
                return B["w1s"][:, k * 512 + j * 128: k * 512 + (j + 1) * 128]
            def rh(k):
                return ch(B["hb"], k)
            def evh(ctx, ps, j, tm, j2=j2):
                ctx.wait("vector", "t", tm)
                jj = j2 * 4 + j
                ctx.op("vector", lambda e, ps=ps, jj=jj: e.tensor_scalar_add(
                    B["gx"][:, 0:TT], ps, B["c1c"][:, jj:jj + 1]), inc="v")
                v1 = ctx.now("v")
                ctx.wait("scalar", "v", v1)
                ctx.op("scalar", lambda e: e.activation(B["gx2"][:, 0:TT], B["gx"][:, 0:TT], AF.Square), inc="s")
                s1_ = ctx.now("s")
                ctx.wait("vector", "s", s1_)
                ctx.op("vector", lambda e: e.tensor_scalar(
                    B["gx2"][:, 0:TT], B["gx2"][:, 0:TT], 0.044715 * 0.7978845608028654, 0.7978845608028654,
                    ALU.mult, ALU.add))
                ctx.op("vector", lambda e: e.tensor_mul(B["gx2"][:, 0:TT], B["gx2"][:, 0:TT], B["gx"][:, 0:TT]), inc="v")
                v2 = ctx.now("v")
                ctx.wait("scalar", "v", v2)
                ctx.op("scalar", lambda e: e.activation(B["gx2"][:, 0:TT], B["gx2"][:, 0:TT], AF.Tanh), inc="s")
                s2_ = ctx.now("s")
                ctx.wait("vector", "s", s2_)
                ctx.op("vector", lambda e: e.tensor_scalar(
                    B["gx2"][:, 0:TT], B["gx2"][:, 0:TT], 1.0, 0.5, ALU.add, ALU.mult))
                ctx.op("vector", lambda e, jj=jj: e.tensor_mul(
                    B["h1"][:, jj * TT:(jj + 1) * TT], B["gx2"][:, 0:TT], B["gx"][:, 0:TT]), inc="v")
                return ("v", ctx.now("v"))
            emit_proj(ctx, 4, C, lh, rh, evh, banks=("pb0", "pb1"),
                      wait_first=[("q:w1b", 0), ("s", hb_done)])
        for cc in range(C):
            ctx.wait("gpsimd", "t", ctx.now("t"))
            ctx.dma("gpsimd", B["w2s"][:].rearrange("p (c n) -> p c n", c=32),
                    Dm["l1_W2"][:, cc * 128:(cc + 1) * 128].rearrange("(c p) n -> p c n", c=32),
                    stream="w2b")

            def lo(k, j, cc=cc):
                return B["w2s"][:, k * 128:(k + 1) * 128]
            def ro(k):
                return B["h1"][:, k * TT:(k + 1) * TT]
            def evo(ctx, ps, j, tm, cc=cc):
                ctx.wait("vector", "t", tm)
                ctx.op("vector", lambda e, ps=ps, cc=cc: e.tensor_scalar_add(
                    ch(B["ft"], cc), ps, B["c2c"][:, cc:cc + 1]))
                ctx.op("vector", lambda e, cc=cc: e.tensor_add(
                    ch(B["ft"], cc), ch(B["ft"], cc), ch(B["ht"], cc)), inc="v")
                return ("v", ctx.now("v"))
            emit_proj(ctx, 1, 32, lo, ro, evo, banks=("pb4", "pb5") if cc % 2 == 0 else ("pb6", "pb5"),
                      wait_first=[("q:w2b", 0)])
        f_done = ctx.now("v")
        ctx.wait("scalar", "v", f_done)
        ctx.op("scalar", lambda e: e.copy(B["hb"][:], B["ft"][:]))
        ctx.op("scalar", lambda e: e.activation(B["sq"][:], B["ft"][:], AF.Square), inc="s")
        casts2 = ctx.now("s")
        rstd2_done, _ = _ln_stats_rows(ctx, B["hb"], B["sq"], B["mur"][0:1, :], B["rstdr"][0:1, :],
                                       [("s", casts2)])
        bc3 = _bcast2(ctx, B["mur"][0:1, :], B["rstdr"][0:1, :], rstd2_done, f_done)
        ctx.wait_dma("vector", "ost")
        for c in range(C):
            ctx.op("vector", lambda e, c=c: e.tensor_sub(B["tmp5"][:, 0:TT], ch(B["ft"], c), B["pb2"][:, :]))
            ctx.op("vector", lambda e, c=c: e.tensor_mul(B["tmp5"][:, 0:TT], B["tmp5"][:, 0:TT], B["pb3"][:, :]))
            ctx.op("vector", lambda e, c=c: e.tensor_scalar(
                B["tmp5"][:, 0:TT], B["tmp5"][:, 0:TT], B["g2c"][:, c:c + 1], B["b2cc"][:, c:c + 1],
                ALU.mult, ALU.add))
            ctx.op("vector", lambda e, c=c: e.tensor_mul(B["tmp5"][:, 0:TT], B["tmp5"][:, 0:TT], ch(B["g2t"], c)))
            ctx.op("vector", lambda e, c=c: e.tensor_add(ch(B["ot"], c), B["tmp5"][:, 0:TT], ch(B["x1t"], c)),
                   inc="v" if c == C - 1 else None)
        odone = ctx.now("v")
        ctx.wait("sync", "v", odone)
        ctx.dma("sync", fm_dram_tile(Dm["out_fm"], t0, TT), tile_view(B["ot"], TT), stream="ost")
    ctx.wait_dma("sync", "ost")


P8B_BUFS = dict(
    a_t=([128, C * TT], F32), x1t=([128, C * TT], F32), ht=([128, C * TT], F32),
    ft=([128, C * TT], F32), g2t=([128, C * TT], F32), ot=([128, C * TT], F32),
    hb=([128, C * TT], BF16), sq=([128, C * TT], BF16), h1=([128, 32 * TT], BF16),
    w1s=([128, C * 512], BF16), w2s=([128, 32 * 128], BF16),
    g1c=([128, 8], F32), b1cc=([128, 8], F32), g2c=([128, 8], F32), b2cc=([128, 8], F32),
    c1c=([128, 32], F32), c2c=([128, 8], F32),
    mur=([1, TT], F32), rstdr=([1, TT], F32), r1=([1, TT], F32), r2=([1, TT], F32),
    murt=([1, TT], F32), rstdt=([1, TT], F32),
    tmp5=([128, TT], F32), gx=([128, TT], F32), gx2=([128, TT], F32),
    ones_col=([128, 1], BF16), ones_rowf=([1, 128], F32),
)


LAUNCH_D_DRAMS = dict(
    x_fm=([D, NT], F32, "in"),
    s_scr=([D, NT], F32, "in"),
    g_scr=([D, NT], F32, "in"),
    f1_cT=([D, 8], F32, "in"),
    ones_col=([128, 1], BF16, "in"),
    ones128f=([1, 128], F32, "in"),
    x1e=([D, NTE], F32, "out"),
    hscr=([3, D, 256], F32, "internal"),
    xn1b_e=([D, NTE], BF16, "out"),
    xhat2b_d=([D, NT], BF16, "out"),
    murowB_d=([1, NTE], F32, "out"),
    rstdrowB_d=([1, NTE], F32, "out"),
    masks_d=([8, NT], F32, "out"),
    masksb_d=([8, NT], BF16, "out"),
    alphar_d=([1, NT], F32, "out"),
)
LAUNCH_D_DRAMS.update(exchange_drams())

LAUNCH_E_DRAMS = dict(
    x1e=([D, NTE], F32, "in"),
    xhat2b_d=([D, NT], BF16, "in"),
    masksb_d=([8, NT], BF16, "in"),
    alphar_d=([1, NT], F32, "in"),
    murowB_d=([1, NTE], F32, "in"),
    rstdrowB_d=([1, NTE], F32, "in"),
    f1_w1=([8, D, 4 * D], F32, "in"),
    f1_w2=([8, 4 * D, D], F32, "in"),
    f1_b1s=([128, 256], F32, "in"),
    f1_b2=([8, D], F32, "in"),
    f1_lngs=([128, 64], F32, "in"),
    f1_lnbs=([128, 64], F32, "in"),
    ind8=([8, 8 * 128], BF16, "in"),
    ones128f=([1, 128], F32, "in"),
    g2_scr=([D, NT], F32, "out"),
)

LAUNCH_F_DRAMS = dict(
    x1e=([D, NTE], F32, "in"),
    xn1b_e=([D, NTE], BF16, "in"),
    g2_scr=([D, NT], F32, "in"),
    murowB_d=([1, NTE], F32, "in"),
    rstdrowB_d=([1, NTE], F32, "in"),
    maskt_d=([8, 128, 4 * 640], F32, "in"),
    l1_Wq=([D, D], F32, "in"), l1_Wk=([D, D], F32, "in"),
    l1_Wv=([D, D], F32, "in"), l1_Wo=([D, D], F32, "in"),
    l1_W1=([D, 4 * D], F32, "in"), l1_W2=([4 * D, D], F32, "in"),
    l1_bqc=([128, 8], F32, "in"), l1_bkc=([128, 8], F32, "in"),
    l1_bvr=([1, D], F32, "in"), l1_boc=([128, 8], F32, "in"),
    l1_g1c=([128, 8], F32, "in"), l1_b1c=([128, 8], F32, "in"),
    l1_g2c=([128, 8], F32, "in"), l1_b2c=([128, 8], F32, "in"),
    l1_c1c=([128, 32], F32, "in"), l1_c2c=([128, 8], F32, "in"),
    idt=([128, 128], BF16, "in"),
    ones128=([1, 128], BF16, "in"),
    ones_col=([128, 1], BF16, "in"),
    ones128f=([1, 128], F32, "in"),
    a_scr=([D, NT], F32, "internal"),
    out_fm=([D, NT], F32, "out"),
)

LAUNCH_B2_DRAMS = dict(LAUNCH_B_DRAMS)


def _run_launch(nc, drams, per_core, extra, cores):
    in_maps = []
    for c in cores:
        d = {}
        for k, v in drams.items():
            if v[2] != "in":
                continue
            if k in extra[c]:
                d[k] = extra[c][k]
            else:
                d[k] = np.ascontiguousarray(per_core[c][k])
        in_maps.append(d)
    return run_bass_kernel_spmd(nc, in_maps, core_ids=list(cores)).results


FUSED = True


def fused_phases():
    ph = [(emit_xch_init, None), (emit_p0, P0_BUFS), (emit_p1, P1_BUFS),
          (emit_p2, P2_BUFS), (emit_p2rot, P2ROT_BUFS), (emit_p3, P3_BUFS),
          (None, P4_SHARED, launch_c_phases()[1:]),
          (emit_p5, P5_BUFS), (emit_p6, P6_BUFS), (emit_p7, P7_BUFS),
          (emit_p8a, P8A_BUFS), (emit_p8b, P8B_BUFS)]
    return ph


def fused_drams():
    merged = {}
    order = [LAUNCH_A_DRAMS, LAUNCH_EXPM_DRAMS, LAUNCH_B_DRAMS, launch_c_drams(),
             LAUNCH_D_DRAMS, LAUNCH_E_DRAMS, LAUNCH_F_DRAMS]
    produced = set()
    for dd in order:
        for k, (shape, dt, kind) in dd.items():
            if kind == "out":
                produced.add(k)
                merged[k] = (shape, dt, "internal")
            elif kind == "in":
                if k in produced:
                    pass
                elif k not in merged:
                    merged[k] = (shape, dt, "in")
            else:
                if k not in merged or merged[k][2] == "in" and kind == "shared":
                    merged[k] = (shape, dt, kind)
    merged["out_fm"] = ([D, NT], F32, "out")
    return merged


def kernel(**inputs):
    per_core = stage_inputs(inputs)
    cores = list(range(NCORES))
    if FUSED:
        drams = fused_drams()
        nc = build_program(fused_phases(), drams)
        extra = [dict() for _ in cores]
        res = _run_launch(nc, drams, per_core, extra, cores)
        out = np.empty((4, 8192, D), np.float32)
        for c in cores:
            b, h = c // 2, c % 2
            out[b, h * NT:(h + 1) * NT, :] = res[c]["out_fm"].T
        return out
    return _kernel_multi(inputs, per_core, cores)


def _kernel_multi(inputs, per_core, cores):
    extra = [dict() for _ in cores]

    ncA = build_program([(emit_p0, P0_BUFS)], LAUNCH_A_DRAMS)
    resA = _run_launch(ncA, LAUNCH_A_DRAMS, per_core, extra, cores)
    for c in cores:
        extra[c].update(resA[c])

    ncX = build_program([(emit_p1, P1_BUFS)], LAUNCH_EXPM_DRAMS)
    resX = _run_launch(ncX, LAUNCH_EXPM_DRAMS, per_core, extra, cores)
    for c in cores:
        extra[c].update(resX[c])

    ncB = build_program([(emit_p2, P2_BUFS), (emit_p2rot, P2ROT_BUFS), (emit_p3, P3_BUFS)], LAUNCH_B_DRAMS)
    resB = _run_launch(ncB, LAUNCH_B_DRAMS, per_core, extra, cores)
    for c in cores:
        extra[c].update(resB[c])

    ncC = build_program(launch_c_phases(), launch_c_drams(), shared_bufs=P4_SHARED)
    resC = _run_launch(ncC, launch_c_drams(), per_core, extra, cores)
    for c in cores:
        extra[c].update(resC[c])

    ncD = build_program([(emit_xch_init, None), (emit_p5, P5_BUFS), (emit_p6, P6_BUFS)], LAUNCH_D_DRAMS)
    resD = _run_launch(ncD, LAUNCH_D_DRAMS, per_core, extra, cores)
    for c in cores:
        extra[c].update(resD[c])

    ncE = build_program([(emit_p7, P7_BUFS)], LAUNCH_E_DRAMS)
    resE = _run_launch(ncE, LAUNCH_E_DRAMS, per_core, extra, cores)
    for c in cores:
        extra[c].update(resE[c])

    ncF = build_program([(emit_p8a, P8A_BUFS), (emit_p8b, P8B_BUFS)], LAUNCH_F_DRAMS)
    resF = _run_launch(ncF, LAUNCH_F_DRAMS, per_core, extra, cores)

    out = np.empty((4, 8192, D), np.float32)
    for c in cores:
        b, h = c // 2, c % 2
        out[b, h * NT:(h + 1) * NT, :] = resF[c]["out_fm"].T
    return out
